# version 35
# baseline (speedup 1.0000x reference)
import math
import numpy as np
import ml_dtypes
import concourse.bass as bass
import concourse.mybir as mybir
from concourse.bass_utils import run_bass_kernel_spmd

F32 = mybir.dt.float32
BF16 = mybir.dt.bfloat16
AF = mybir.ActivationFunctionType
OP = mybir.AluOpType
AX = mybir.AxisListType

T = 2048
D = 1024
NCH = 8
NQ = 4
QW = 512
NKC = 16
HD = 64
RMS_EPS = 1.1920929e-07
LN_EPS = 1e-5
DFF = 4096
FCW = 128
NFC = DFF // FCW
PI_F = float(np.float32(math.pi))
TPI_HI = float(np.float32(2.0 * math.pi))
TPI_LO = float(np.float32(2.0 * math.pi - float(np.float32(2.0 * math.pi))))
ENGS = ("tensor", "vector", "scalar", "gpsimd", "sync")
ESEM = {"tensor": "t", "vector": "v", "scalar": "s", "gpsimd": "g", "sync": "q"}


class KB:
    def __init__(self, stop_after="l1f"):
        self.stop_after = stop_after
        nc = bass.Bass("TRN2", target_bir_lowering=False)
        self.nc = nc
        self.th = {e: [] for e in ENGS}
        self.sems = {}
        self.cnt = {}
        self.hw = {e: {} for e in ENGS}
        self.LDS = [f"ld{i}" for i in range(8)]
        self.STS = [f"st{i}" for i in range(4)]
        for sname in ("t", "v", "s", "g", "q", *self.LDS, *self.STS):
            self.sems[sname] = nc.semaphore(sname).__enter__()
            self.cnt[sname] = 0
        self.ld_rr = 0
        self.st_rr = 0
        self.dr = {}
        self.AL = {}
        self.lw = {}
        self.rdrs = {}
        self.pend_ep = None

    def do(self, eng, build, r=(), w=(), dma=None, xw=(), force=False):
        waits = {}
        for k in r:
            for sn, v0 in self.lw.get(k, {}).items():
                waits[sn] = max(waits.get(sn, 0), v0)
        for k in w:
            for sn, v0 in self.lw.get(k, {}).items():
                waits[sn] = max(waits.get(sn, 0), v0)
            for sn, v0 in self.rdrs.get(k, {}).items():
                waits[sn] = max(waits.get(sn, 0), v0)
        for k in w:
            if k not in self.lw and k not in self.rdrs and getattr(self, "floor", None):
                for sn, v0 in self.floor.items():
                    waits[sn] = max(waits.get(sn, 0), v0)
        for sn, v0 in xw:
            if v0:
                waits[sn] = max(waits.get(sn, 0), v0)
        real = []
        for sn, v0 in waits.items():
            if not force:
                if eng == "tensor" and sn == "t":
                    continue
                if self.hw[eng].get(sn, 0) >= v0 and sn in ("q",):
                    continue
            self.hw[eng][sn] = max(self.hw[eng].get(sn, 0), v0)
            real.append((sn, v0))
        if dma is None:
            sn_inc, n = ESEM[eng], 1
        else:
            if dma == "ld":
                sn_inc = self.LDS[self.ld_rr % len(self.LDS)]
                self.ld_rr += 1
            else:
                sn_inc = self.STS[self.st_rr % len(self.STS)]
                self.st_rr += 1
            n = 16
            prev = self.cnt[sn_inc]
            if prev:
                self.hw[eng][sn_inc] = max(self.hw[eng].get(sn_inc, 0), prev)
                real.append((sn_inc, prev))
        self.cnt[sn_inc] += n
        mycnt = self.cnt[sn_inc]
        sems = self.sems

        def f(e, build=build, real=tuple(real), sn_inc=sn_inc, n=n):
            for (sn, v0) in real:
                e.wait_ge(sems[sn], v0)
            build(e).then_inc(sems[sn_inc], n)
        self.th[eng].append(f)
        for k in w:
            d = self.lw.setdefault(k, {})
            d[sn_inc] = max(d.get(sn_inc, 0), mycnt)
        for k in r:
            d = self.rdrs.setdefault(k, {})
            d[sn_inc] = max(d.get(sn_inc, 0), mycnt)
        return mycnt

    def close_group(self, *keys):
        for k in keys:
            d = self.lw.setdefault(k, {})
            d["t"] = max(d.get("t", 0), self.cnt["t"])

    def flush_ep(self):
        if self.pend_ep is not None:
            ep, self.pend_ep = self.pend_ep, None
            ep()

    def soft_barrier(self):
        self.flush_ep()
        self.floor = {ESEM[e]: self.cnt[ESEM[e]] for e in ENGS}
        for sn in (*self.LDS, *self.STS):
            self.floor[sn] = self.cnt[sn]

    def barrier(self):
        self.flush_ep()
        self.do("sync", lambda e: e.nop(),
                xw=[(sn, self.cnt[sn]) for sn in (*self.LDS, *self.STS)])
        vals = {}
        for e in ENGS:
            vals[e] = self.do(e, lambda en: en.nop(),
                              xw=[(ESEM[e], self.cnt[ESEM[e]])], force=True)
        for e in ENGS:
            for o in ENGS:
                if o == e:
                    continue
                sn = ESEM[o]
                if self.hw[e].get(sn, 0) < vals[o]:
                    self.hw[e][sn] = vals[o]
                    self.th[e].append(
                        lambda en, sn=sn, v0=vals[o]: en.wait_ge(self.sems[sn], v0))
            for sn in (*self.LDS, *self.STS):
                self.hw[e][sn] = max(self.hw[e].get(sn, 0), self.cnt[sn])
        self.floor = {ESEM[o]: vals[o] for o in ENGS}
        for sn in (*self.LDS, *self.STS):
            self.floor[sn] = self.cnt[sn]

    def din(self, name, shape, dt_=F32):
        self.dr[name] = self.nc.dram_tensor(name, list(shape), dt_,
                                            kind="ExternalInput").ap()
        return self.dr[name]

    def sb(self, name, shape, dt_=F32):
        self.AL[name] = self.nc.alloc_sbuf_tensor(name, list(shape), dt_)
        return self.AL[name]

    def ov(self, name, shape, dt_, off):
        sz = int(np.prod(shape[1:])) * mybir.dt.size(dt_)
        assert off + sz <= self.ARENA, (name, off, sz, self.ARENA)
        if name in self.AL:
            return self.AL[name]
        self.AL[name] = self.nc.alloc_sbuf_tensor_at(
            name, list(shape), dt_, offset=self.ovl_base + off)
        return self.AL[name]

    def loadw(self, name, dst, dkey, cols=None, rowsl=None, dstcol0=0):
        ap = self.dr[name]
        rows = ap.shape[0] if rowsl is None else (rowsl.stop - rowsl.start)
        r0 = 0 if rowsl is None else rowsl.start
        cols = ap.shape[1] if cols is None else cols
        nch = (rows + 127) // 128
        v = None
        for ch in range(nch):
            v = self.do("sync", lambda e, ap=ap, ch=ch, r0=r0, cols=cols, dst=dst,
                        dstcol0=dstcol0:
                        e.dma_start(out=dst[:, dstcol0 + cols * ch:dstcol0 + cols * (ch + 1)],
                                    in_=ap[r0 + 128 * ch:r0 + 128 * (ch + 1), 0:cols]),
                        w=(dkey,), dma="ld")
        return v

    def build(self):
        nc = self.nc
        s = self
        self.declare_dram()
        self.X = [s.sb(f"X{c}", [128, T]) for c in range(NCH)]
        self.MU = s.sb("MU", [128, T])
        self.RBC = s.sb("RBC", [128, T])
        self.E32 = [s.sb(f"E32_{i}", [128, QW]) for i in range(3)]
        self.I128 = s.sb("I128", [128, 128])
        self.ALL1 = s.sb("ALL1", [128, 128])
        self.ONES2B = s.sb("ONES2B", [128, 2])
        self.ONESB2 = s.sb("ONESB2", [2, 128])
        self.ONE1 = s.sb("ONE1", [1, 128])
        self.SM = s.sb("SM", [128, 64])
        self.ROWS = s.sb("ROWS", [2, QW])
        self.SC = [s.sb(f"SC_{i}", [128, QW]) for i in range(4)]
        self.RBF = s.sb("RBF", [128, QW])
        self.TMPE = s.sb("TMPE", [128, QW])
        self.ARENA = 104 * 1024
        self.ovl_base = nc.sbuf_top - self.ARENA
        self.PS = [nc.alloc_psum_tensor(f"PS{i}", [128, QW], F32) for i in range(8)]
        self.MSK = s.ov("MSK", [128, 4 * QW], BF16, 90 * 1024)
        self.I128B = s.sb("I128B", [128, 128], BF16)

        self.emit_entry()
        for phase in ("l0a", "l0f", "l1a", "l1f"):
            if phase != "l1f":
                self.emit_ln()
            getattr(self, "emit_" + phase)()
            self.soft_barrier()
            if self.stop_after == phase:
                break
        return self.fin()

    def declare_dram(self):
        s = self
        s.din("x_in", [T, D])
        self.y_out = s.nc.dram_tensor("y_out", [T, D], F32,
                                      kind="ExternalOutput").ap()
        s.din("wdq", [512, HD]); s.din("wuqT", [512, HD])
        s.din("wdkv", [1024, HD]); s.din("wukvT", [1024, 2 * HD])
        s.din("wqrT", [HD, HD]); s.din("wkrT", [HD, HD]); s.din("woT", [HD, HD])
        s.din("qnw", [1, HD]); s.din("knw", [1, HD])
        s.din("w0T", [D, D])
        s.din("wdq1T", [D, 256]); s.din("wdq1", [256, D])
        s.din("wuq1", [D, 256]); s.din("wuq1T", [256, D])
        s.din("wdkv1T", [D, 256]); s.din("wuk1", [D, 256]); s.din("wuv1", [D, 256])
        s.din("wo1T", [D, D])
        s.din("w1p", [DFF, D]); s.din("w2", [DFF, D])
        s.din("b1", [32, 128]); s.din("th1", [32, 128])
        s.din("qwr", [32, 128]); s.din("qwi", [32, 128]); s.din("b2", [8, 128])
        s.din("ident", [128, 128]); s.din("mask", [128, 4 * QW], BF16)
        s.din("identb", [128, 128], BF16)
        s.din("all1", [128, 128]); s.din("one1", [1, 128])
        s.din("ones2b", [128, 2]); s.din("onesb2", [2, 128])

    def emit_entry(self):
        s = self
        TM = [s.ov(f"TM{i}", [128, D], F32, i * 4096) for i in range(16)]
        for i in range(16):
            s.do("sync", lambda e, i=i, TM=TM:
                 e.dma_start(out=TM[i][:], in_=s.dr["x_in"][128 * i:128 * (i + 1), :]),
                 w=(f"TM{i}",), dma="ld")
        for nm, dst in (("ident", s.I128), ("mask", s.MSK), ("all1", s.ALL1),
                        ("one1", s.ONE1), ("ones2b", s.ONES2B),
                        ("onesb2", s.ONESB2), ("identb", s.I128B)):
            s.do("sync", lambda e, nm=nm, dst=dst:
                 e.dma_start(out=dst[:, :], in_=s.dr[nm][:, :]),
                 w=(nm,), dma="ld")
        for c in range(NCH):
            for g in range(4):
                bk = f"PS{g % 2}"
                bank = s.PS[g % 2]
                for j in range(4):
                    tt = 4 * g + j
                    s.do("tensor",
                         lambda e, c=c, tt=tt, j=j, bank=bank, TM=TM:
                         e.transpose(bank[:, 128 * j:128 * (j + 1)],
                                     TM[tt][:, 128 * c:128 * (c + 1)], s.I128[:]),
                         r=(f"TM{tt}", "ident"), w=(bk,))
                s.do("vector",
                     lambda e, c=c, g=g, bank=bank:
                     e.tensor_copy(s.X[c][:, QW * g:QW * (g + 1)], bank[:]),
                     r=(bk,), w=(f"X{c}q{g}",))
        s.barrier()

    def emit_ln_win(self, qq):
        s = self
        for c in range(NCH):
            sq = s.SC[c % 2]
            s.do("scalar",
                 lambda e, c=c, qq=qq, sq=sq:
                 e.activation(sq[:], s.X[c][:, QW * qq:QW * (qq + 1)], AF.Square),
                 r=(f"X{c}q{qq}",), w=(f"SC{c % 2}",))
            s.do("tensor",
                 lambda e, c=c, qq=qq:
                 e.matmul(s.PS[4][:], s.ALL1[:], s.X[c][:, QW * qq:QW * (qq + 1)],
                          start=(c == 0), stop=(c == NCH - 1)),
                 r=(f"X{c}q{qq}", "all1"), w=("PS4",))
            s.do("tensor",
                 lambda e, sq=sq, c=c:
                 e.matmul(s.PS[5][:], s.ALL1[:], sq[:],
                          start=(c == 0), stop=(c == NCH - 1)),
                 r=(f"SC{c % 2}", "all1"), w=("PS5",))
        s.close_group("PS4", "PS5")
        mu_s = s.MU[:, QW * qq:QW * (qq + 1)]
        rb_s = s.RBC[:, QW * qq:QW * (qq + 1)]
        s.do("scalar", lambda e, mu_s=mu_s:
             e.activation(mu_s, s.PS[4][:], AF.Identity, scale=1.0 / D),
             r=("PS4",), w=(f"MU{qq}",))
        s.do("scalar", lambda e:
             e.activation(s.SC[2][:], s.PS[5][:], AF.Identity, scale=1.0 / D),
             r=("PS5",), w=("SC2",))
        s.do("scalar", lambda e, mu_s=mu_s:
             e.activation(s.SC[3][:], mu_s, AF.Square),
             r=(f"MU{qq}",), w=("SC3",))
        s.do("vector", lambda e:
             e.tensor_tensor(s.SC[2][:], s.SC[2][:], s.SC[3][:], OP.subtract),
             r=("SC2", "SC3"), w=("SC2",))
        s.do("vector", lambda e:
             e.tensor_single_scalar(s.SC[2][:], s.SC[2][:], LN_EPS, OP.add),
             r=("SC2",), w=("SC2",))
        s.do("scalar", lambda e:
             e.activation(s.SC[3][:], s.SC[2][:], AF.Ln), r=("SC2",), w=("SC3",))
        s.do("scalar", lambda e, rb_s=rb_s:
             e.activation(rb_s, s.SC[3][:], AF.Exp, scale=-0.5),
             r=("SC3",), w=(f"RB{qq}",))

    def emit_ln(self):
        for qq in range(NQ):
            self.emit_ln_win(qq)

    def xn_into(self, dst, dkey, c, qq, eng="vector", wx=()):
        s = self
        s.do(eng, lambda e, c=c, qq=qq, dst=dst:
             e.tensor_tensor(dst, s.X[c][:, QW * qq:QW * (qq + 1)],
                             s.MU[:, QW * qq:QW * (qq + 1)], OP.subtract),
             r=(f"X{c}q{qq}", f"MU{qq}"), w=(dkey, *wx))
        s.do(eng, lambda e, qq=qq, dst=dst:
             e.tensor_tensor(dst, dst, s.RBC[:, QW * qq:QW * (qq + 1)], OP.mult),
             r=(dkey, f"RB{qq}"), w=(dkey,))

    def attn_head(self, k_lhsT, kkey, q_rhs, qkey, v_lhsT, vkey, negc, nkey,
                  xout, xkey, half):
        s = self
        hs_ctx = slice(64, 128) if half else slice(0, 64)
        hs_z = slice(0, 64) if half else slice(64, 128)
        for qq in range(NQ):
            kcmax = 4 * qq + 4
            s.pvc = getattr(s, "pvc", 0) + 1
            pvb = (2, 3, 7)[s.pvc % 3]
            pvbank = s.PS[pvb]

            def scores(kc, qq=qq):
                sb = (0, 1, 4, 5)[kc % 4]
                diag = kc >= 4 * qq
                s.do("tensor", lambda e, kc=kc, sb=sb, qq=qq, diag=diag:
                     e.matmul(s.PS[sb][:], k_lhsT(kc), q_rhs(qq),
                              start=True, stop=not diag),
                     r=(kkey, qkey), w=(f"PS{sb}",))
                if diag:
                    j = kc - 4 * qq
                    s.do("tensor", lambda e, sb=sb, j=j:
                         e.matmul(s.PS[sb][:], s.I128B[:],
                                  s.MSK[:, QW * j:QW * (j + 1)],
                                  start=False, stop=True),
                         r=("identb", "mask"), w=(f"PS{sb}",))
                eb = kc % 3
                s.do("scalar", lambda e, sb=sb, eb=eb:
                     e.activation(s.E32[eb][:], s.PS[sb][:], AF.Exp, bias=negc),
                     r=(f"PS{sb}", nkey), w=(f"E{eb}",))

            def pv(kc, qq=qq, kcmax=kcmax, pvbank=pvbank, pvb=pvb):
                eb = kc % 3
                s.do("tensor", lambda e, kc=kc, eb=eb, pvbank=pvbank, kcmax=kcmax:
                     e.matmul(pvbank[:], v_lhsT(kc), s.E32[eb][:],
                              start=(kc == 0), stop=(kc == kcmax - 1)),
                     r=(vkey, f"E{eb}"), w=(f"PS{pvb}",))

            def epilogue(qq=qq, pvb=pvb, pvbank=pvbank,
                         hs_ctx=hs_ctx, hs_z=hs_z, xout=xout, xkey=xkey):
                s.do("tensor", lambda e:
                     e.matmul(s.PS[6][hs_ctx, :], s.I128[hs_z, hs_z],
                              s.RBF[hs_z, :], start=True, stop=True),
                     r=("RBFz", "ident"), w=("PS6",))
                s.do("vector", lambda e:
                     e.tensor_copy(s.RBF[hs_ctx, :], s.PS[6][hs_ctx, :]),
                     r=("PS6",), w=("RBFc",))
                s.do("vector", lambda e:
                     e.tensor_tensor(s.TMPE[hs_ctx, :], pvbank[hs_ctx, :],
                                     s.RBF[hs_ctx, :], OP.mult),
                     r=(f"PS{pvb}", "RBFc"), w=("TMPK",))
                s.do("vector", lambda e:
                     e.tensor_tensor(xout(qq), xout(qq), s.TMPE[hs_ctx, :],
                                     OP.add),
                     r=("TMPK", xkey(qq)), w=(xkey(qq),))

            for kc in range(kcmax):
                scores(kc)
                if kc >= 2:
                    pv(kc - 2)
                if kc == 3:
                    s.flush_ep()
            if kcmax >= 2:
                pv(kcmax - 2)
            pv(kcmax - 1)
            s.close_group(f"PS{pvb}")
            s.do("scalar", lambda e, pvbank=pvbank:
                 e.activation(s.TMPE[hs_z, :], pvbank[hs_z, :], AF.Ln),
                 r=(f"PS{pvb}",), w=("TMPK",))
            s.do("scalar", lambda e:
                 e.activation(s.RBF[hs_z, :], s.TMPE[hs_z, :], AF.Exp,
                              scale=-1.0),
                 r=("TMPK",), w=("RBFz",))
            s.pend_ep = epilogue

    def head_bound(self, qt, qkey, kt, kkey, col):
        s = self
        s.do("vector", lambda e, qt=qt:
             e.tensor_reduce(s.SC[0][:, 0:1], qt[:, 0:T], AX.X, OP.max,
                             apply_absolute_value=True),
             r=(qkey,), w=("SC0",))
        s.do("vector", lambda e, kt=kt:
             e.tensor_reduce(s.SC[0][:, 1:2], kt[:, 0:T], AX.X, OP.max,
                             apply_absolute_value=True),
             r=(kkey,), w=("SC0",))
        s.do("scalar", lambda e:
             e.activation(s.SC[0][:, 2:4], s.SC[0][:, 0:2], AF.Square),
             r=("SC0",), w=("SC0",))
        s.do("tensor", lambda e:
             e.matmul(s.PS[6][0:128, 0:2], s.ALL1[:], s.SC[0][:, 2:4],
                      start=True, stop=True),
             r=("SC0", "all1"), w=("PS6",))
        s.do("scalar", lambda e:
             e.activation(s.SC[0][:, 4:6], s.PS[6][0:128, 0:2], AF.Sqrt),
             r=("PS6",), w=("SC0",))
        s.do("vector", lambda e:
             e.tensor_tensor(s.SC[0][:, 6:7], s.SC[0][:, 4:5], s.SC[0][:, 5:6],
                             OP.mult),
             r=("SC0",), w=("SC0",))
        s.do("vector", lambda e, col=col:
             e.tensor_scalar(s.SM[:, col:col + 1], s.SC[0][:, 6:7],
                             43.0, -1.0, OP.min, OP.mult),
             r=("SC0",), w=(f"SM{col}",))

    def emit_l0a(self):
        s = self
        WT = {}
        off = 70 * 1024
        for name, rows, cols in (("wdq", 512, HD), ("wuqT", 512, HD),
                                 ("wdkv", 1024, HD), ("wukvT", 1024, 2 * HD)):
            nch = rows // 128
            WT[name] = s.ov("w_" + name, [128, nch * cols], F32, off)
            off += nch * cols * 4
            s.loadw(name, WT[name], "w_" + name, cols=cols)
        for name in ("wqrT", "wkrT", "woT"):
            WT[name] = s.ov("w_" + name, [128, HD], F32, off)
            off += HD * 4
            for hh in range(2):
                s.do("sync", lambda e, name=name, hh=hh, WT=WT:
                     e.dma_start(out=WT[name][64 * hh:64 * (hh + 1), :],
                                 in_=s.dr[name][:, :]),
                     w=("w_" + name,), dma="ld")
        for col, name in ((0, "qnw"), (1, "knw")):
            for hh in range(2):
                s.do("sync", lambda e, name=name, col=col, hh=hh:
                     e.dma_start(out=s.SM[64 * hh:64 * (hh + 1), col:col + 1],
                                 in_=s.dr[name][0:1, :]),
                     w=(f"SM{col}",), dma="ld")
        EQT = s.ov("EQT", [128, 128], F32, 84 * 1024)
        EKT = s.ov("EKT", [128, 128], F32, 84 * 1024 + 512)
        WOE2 = s.ov("WOE2", [128, 128], F32, 84 * 1024 + 1024)
        WQR2 = s.ov("WQR2", [128, HD], F32, 84 * 1024 + 1536)
        WKR2 = s.ov("WKR2", [128, HD], F32, 84 * 1024 + 1792)
        EVT = s.ov("EVT", [64, 64], F32, 84 * 1024 + 2048)
        STG = s.ov("STG", [64, 192], F32, 84 * 1024 + 2304)

        for ch in range(4):
            s.do("tensor", lambda e, ch=ch, WT=WT:
                 e.matmul(s.PS[6][0:64, 0:64], WT["wdq"][:, HD * ch:HD * (ch + 1)],
                          WT["wuqT"][:, HD * ch:HD * (ch + 1)],
                          start=(ch == 0), stop=(ch == 3)),
                 r=("w_wdq", "w_wuqT"), w=("PS6",))
        for ch in range(8):
            s.do("tensor", lambda e, ch=ch, WT=WT:
                 e.matmul(s.PS[6][0:64, 64:128], WT["wdkv"][:, HD * ch:HD * (ch + 1)],
                          WT["wukvT"][:, 128 * ch:128 * ch + 64],
                          start=(ch == 0), stop=(ch == 7)),
                 r=("w_wdkv", "w_wukvT"))
        for ch in range(8):
            s.do("tensor", lambda e, ch=ch, WT=WT:
                 e.matmul(s.PS[6][0:64, 128:192],
                          WT["wukvT"][:, 128 * ch + 64:128 * (ch + 1)],
                          WT["wdkv"][:, HD * ch:HD * (ch + 1)],
                          start=(ch == 0), stop=(ch == 7)),
                 r=("w_wdkv", "w_wukvT"))
        s.close_group("PS6")
        s.do("vector", lambda e, STG=STG:
             e.tensor_copy(STG[:, 0:192], s.PS[6][0:64, 0:192]),
             r=("PS6",), w=("STG",))
        s.do("vector", lambda e, STG=STG, EVT=EVT:
             e.tensor_copy(EVT[:], STG[:, 128:192]), r=("STG",), w=("EVT",))
        s.do("tensor", lambda e, EVT=EVT, WT=WT:
             e.matmul(s.PS[7][0:64, 0:64], EVT[:], WT["woT"][0:64, :],
                      start=True, stop=True),
             r=("EVT", "w_woT"), w=("PS7",))
        s.do("vector", lambda e, STG=STG:
             e.tensor_copy(STG[:, 128:192], s.PS[7][0:64, 0:64]),
             r=("PS7",), w=("STG",))
        for tn, key in ((EQT, "EQT"), (EKT, "EKT"), (WOE2, "WOE2")):
            s.do("vector", lambda e, tn=tn: e.memset(tn[:], 0.0), w=(key,))
        for tn, key, sl in ((EQT, "EQT", slice(0, 64)), (EKT, "EKT", slice(64, 128)),
                            (WOE2, "WOE2", slice(128, 192))):
            s.do("vector", lambda e, tn=tn, sl=sl, STG=STG:
                 e.tensor_copy(tn[0:64, 0:64], STG[:, sl]), r=("STG",), w=(key,))
            s.do("sync", lambda e, tn=tn, sl=sl, STG=STG:
                 e.dma_start(out=tn[64:128, 64:128], in_=STG[:, sl]),
                 r=("STG",), w=(key,), dma="ld")
        s.do("vector", lambda e:
             e.tensor_tensor(s.SM[:, 2:3], s.SM[:, 0:1], s.SM[:, 1:2], OP.mult),
             r=("SM0", "SM1"), w=("SM2",))
        s.do("vector", lambda e:
             e.tensor_single_scalar(s.SM[:, 2:3], s.SM[:, 2:3], HD ** -0.5, OP.mult),
             r=("SM2",), w=("SM2",))
        s.do("vector", lambda e, WQR2=WQR2, WT=WT:
             e.tensor_scalar_mul(WQR2[:], WT["wqrT"][:, :], s.SM[:, 0:1]),
             r=("w_wqrT", "SM0"), w=("WQR2",))
        s.do("vector", lambda e, WKR2=WKR2, WT=WT:
             e.tensor_scalar_mul(WKR2[:], WT["wkrT"][:, :], HD ** -0.5),
             r=("w_wkrT",), w=("WKR2",))

        NC_ = s.ov("NC_", [128, T], F32, 0)
        QA = s.ov("QA", [128, T], F32, 8 * 1024)
        QB = s.ov("QB", [128, T], F32, 16 * 1024)
        KA = s.ov("KA", [128, T], F32, 24 * 1024)
        KB = s.ov("KB", [128, T], F32, 32 * 1024)
        VFA = s.ov("VFA", [128, T], F32, 40 * 1024)
        VFB = s.ov("VFB", [128, T], F32, 48 * 1024)
        RSBP = [s.ov("RSB", [128, QW], F32, 56 * 1024),
                s.ov("RSB2", [128, QW], F32, 60 * 1024)]
        SQSP = [s.ov("SQS", [128, QW], F32, 58 * 1024),
                s.ov("SQS2", [128, QW], F32, 62 * 1024)]
        ROWP = [s.ROWS, s.ov("ROW2", [2, QW], F32, 64 * 1024)]

        units = ((EQT, "EQT", QA, "QA", QB, "QB", False),
                 (EKT, "EKT", KA, "KA", KB, "KB", True))

        def stA(i, NC_=NC_, SQSP=SQSP):
            (WTm, wkey, dA, kA_, dB, kB_, is_k) = units[i % 2]
            qq = i // 2
            bk = f"PS{4 + (i % 2)}"
            bank = s.PS[4 + (i % 2)]
            sq, sqk = SQSP[i % 2], f"SQSp{i % 2}"
            s.do("tensor", lambda e, WTm=WTm, bank=bank, qq=qq:
                 e.matmul(bank[:], WTm[:], NC_[:, QW * qq:QW * (qq + 1)],
                          start=True, stop=True),
                 r=(wkey, "NC"), w=(bk,))
            s.do("vector", lambda e, qq=qq, dA=dA, bank=bank:
                 e.tensor_copy(dA[0:64, QW * qq:QW * (qq + 1)], bank[0:64, :]),
                 r=(bk,), w=(kA_,))
            s.do("vector", lambda e, qq=qq, dB=dB, bank=bank:
                 e.tensor_copy(dB[64:128, QW * qq:QW * (qq + 1)],
                               bank[64:128, :]),
                 r=(bk,), w=(kB_,))
            s.do("scalar", lambda e, qq=qq, dA=dA, sq=sq:
                 e.activation(sq[0:64, :], dA[0:64, QW * qq:QW * (qq + 1)],
                              AF.Square),
                 r=(kA_,), w=(sqk,))
            s.do("scalar", lambda e, qq=qq, dB=dB, sq=sq:
                 e.activation(sq[64:128, :], dB[64:128, QW * qq:QW * (qq + 1)],
                              AF.Square),
                 r=(kB_,), w=(sqk,))

        def stB(i, SQSP=SQSP, ROWP=ROWP):
            pb = f"PS{6 + (i % 2)}"
            pbank = s.PS[6 + (i % 2)]
            sq, sqk = SQSP[i % 2], f"SQSp{i % 2}"
            rw, rwk = ROWP[i % 2], f"ROWSp{i % 2}"
            s.do("tensor", lambda e, pbank=pbank, sq=sq:
                 e.matmul(pbank[0:2, :], s.ONES2B[:], sq[:],
                          start=True, stop=True),
                 r=(sqk, "ones2b"), w=(pb,))
            s.do("vector", lambda e, pbank=pbank, rw=rw:
                 e.tensor_scalar(rw[:], pbank[0:2, :], 1.0 / HD, RMS_EPS,
                                 OP.mult, OP.add),
                 r=(pb,), w=(rwk,))
            s.do("scalar", lambda e, sq=sq, rw=rw:
                 e.activation(sq[0:2, :], rw[:], AF.Ln), r=(rwk,), w=(sqk,))
            s.do("scalar", lambda e, sq=sq, rw=rw:
                 e.activation(rw[:], sq[0:2, :], AF.Exp, scale=-0.5),
                 r=(sqk,), w=(rwk,))

        def stC(i, ROWP=ROWP, RSBP=RSBP):
            (WTm, wkey, dA, kA_, dB, kB_, is_k) = units[i % 2]
            qq = i // 2
            pb = f"PS{6 + (i % 2)}"
            pbank = s.PS[6 + (i % 2)]
            rw, rwk = ROWP[i % 2], f"ROWSp{i % 2}"
            rb, rbk = RSBP[i % 2], f"RSBp{i % 2}"
            s.do("tensor", lambda e, pbank=pbank, rw=rw:
                 e.matmul(pbank[:], s.ONESB2[:], rw[:], start=True, stop=True),
                 r=(rwk, "onesb2"), w=(pb,))
            s.do("vector", lambda e, pbank=pbank, rb=rb:
                 e.tensor_copy(rb[:], pbank[:]), r=(pb,), w=(rbk,))
            for half, (dd, dk) in enumerate(((dA, kA_), (dB, kB_))):
                hcn = slice(0, 64) if half == 0 else slice(64, 128)
                if is_k:
                    s.do("vector", lambda e, qq=qq, dd=dd, hcn=hcn, rb=rb:
                         e.scalar_tensor_tensor(
                             dd[hcn, QW * qq:QW * (qq + 1)],
                             dd[hcn, QW * qq:QW * (qq + 1)],
                             s.SM[hcn, 2:3], rb[hcn, :], OP.mult, OP.mult),
                         r=(dk, "SM2", rbk), w=(dk,))
                else:
                    s.do("vector", lambda e, qq=qq, dd=dd, hcn=hcn, rb=rb:
                         e.tensor_tensor(
                             dd[hcn, QW * qq:QW * (qq + 1)],
                             dd[hcn, QW * qq:QW * (qq + 1)],
                             rb[hcn, :], OP.mult),
                         r=(dk, rbk), w=(dk,))

        def stD(i, NC_=NC_, WQR2=WQR2, WKR2=WKR2):
            (WTm, wkey, dA, kA_, dB, kB_, is_k) = units[i % 2]
            qq = i // 2
            for half, (dd, dk) in enumerate(((dA, kA_), (dB, kB_))):
                hcn = slice(0, 64) if half == 0 else slice(64, 128)
                hr = slice(64, 128) if half == 0 else slice(0, 64)
                bk = f"PS{half}"
                bank = s.PS[half]
                if is_k:
                    s.do("tensor", lambda e, qq=qq, hcn=hcn, hr=hr, bank=bank:
                         e.matmul(bank[hr, :], WKR2[hcn, :],
                                  NC_[hcn, QW * qq:QW * (qq + 1)],
                                  start=True, stop=True),
                         r=("WKR2", "NC"), w=(bk,))
                else:
                    s.do("tensor", lambda e, qq=qq, dd=dd, hcn=hcn, hr=hr,
                         bank=bank:
                         e.matmul(bank[hr, :], WQR2[hcn, :],
                                  dd[hcn, QW * qq:QW * (qq + 1)],
                                  start=True, stop=True),
                         r=("WQR2", dk), w=(bk,))
                s.do("vector", lambda e, qq=qq, dd=dd, hr=hr, bank=bank:
                     e.tensor_copy(dd[hr, QW * qq:QW * (qq + 1)], bank[hr, :]),
                     r=(bk,), w=(dk,))

        def veff(kc, NC_=NC_, WOE2=WOE2, VFA=VFA, VFB=VFB):
            bk = f"PS{2 + (kc % 2)}"
            bank = s.PS[2 + (kc % 2)]
            s.do("tensor", lambda e, kc=kc, bank=bank:
                 e.matmul(bank[:, 0:128], NC_[:, 128 * kc:128 * (kc + 1)],
                          WOE2[:], start=True, stop=True),
                 r=("NC", "WOE2"), w=(bk,))
            s.do("vector", lambda e, kc=kc, bank=bank:
                 e.tensor_copy(VFA[:, 128 * kc:128 * kc + 64], bank[:, 0:64]),
                 r=(bk,), w=("VFA",))
            s.do("vector", lambda e, kc=kc, bank=bank:
                 e.tensor_copy(VFB[:, 128 * kc + 64:128 * (kc + 1)],
                               bank[:, 64:128]),
                 r=(bk,), w=("VFB",))

        for c in range(NCH):
            s.flush_ep()
            if c == 0:
                s.do("vector", lambda e, VFA=VFA: e.memset(VFA[:], 1.0),
                     w=("VFA",))
                s.do("vector", lambda e, VFB=VFB: e.memset(VFB[:], 1.0),
                     w=("VFB",))
            s.xn_into(NC_[:, 0:QW], "NC", c, 0, eng="gpsimd")
            s.xn_into(NC_[:, QW:2 * QW], "NC", c, 1, eng="gpsimd")
            for i in range(11):
                if i in (2, 4):
                    w = i // 2 + 1
                    s.xn_into(NC_[:, QW * w:QW * (w + 1)], "NC", c, w,
                              eng="gpsimd")
                if i < 8:
                    stA(i)
                    veff(2 * i)
                    veff(2 * i + 1)
                if 1 <= i <= 8:
                    stB(i - 1)
                if 2 <= i <= 9:
                    stC(i - 2)
                if i >= 3:
                    stD(i - 3)
            s.head_bound(QA, "QA", KA, "KA", 3 + 2 * c)
            s.head_bound(QB, "QB", KB, "KB", 3 + 2 * c + 1)
            for half, (qt, qk_, kt, kk_, vt, vk_) in enumerate(
                    ((QA, "QA", KA, "KA", VFA, "VFA"),
                     (QB, "QB", KB, "KB", VFB, "VFB"))):
                h = 2 * c + half
                negc = s.SM[:, 3 + h:4 + h]
                xs = s.X[c]
                hs0 = 64 * half
                s.attn_head(
                    k_lhsT=lambda kc, kt=kt: kt[:, 128 * kc:128 * (kc + 1)],
                    kkey=kk_,
                    q_rhs=lambda qq, qt=qt: qt[:, QW * qq:QW * (qq + 1)],
                    qkey=qk_,
                    v_lhsT=lambda kc, vt=vt: vt[:, 128 * kc:128 * (kc + 1)],
                    vkey=vk_,
                    negc=negc, nkey=f"SM{3 + h}",
                    xout=lambda qq, xs=xs, hs0=hs0:
                        xs[hs0:hs0 + 64, QW * qq:QW * (qq + 1)],
                    xkey=lambda qq, c=c: f"X{c}q{qq}",
                    half=half)

    def emit_l0f(self):
        s = self
        WQ = s.ov("WQB", [128, 8 * 1024], BF16, 0)
        AQW = s.ov("AQW", [128, 8 * QW], BF16, 16 * 1024)
        WPB = s.ov("WPB", [128, T], F32, 24 * 1024)
        WTR = [s.ov(f"WTR{i}", [128, 1024], F32, 48 * 1024 + i * 4096)
               for i in range(2)]
        AMX = s.ov("AMX", [128, 64], F32, 56 * 1024)
        SB32 = s.ov("SB32", [32, 128], F32, 57 * 1024)
        RAB = s.ov("RAB", [128, QW], F32, 58 * 1024)
        ASB = s.ov("ASB", [128, QW], F32, 60 * 1024)
        ROWR = s.ov("ROWR", [1, 4096], F32, 64 * 1024)
        WPA = s.ov("WPA", [128, T], F32, 82 * 1024)
        U_ = s.ov("U_", [128, 2 * QW], F32, 24 * 1024)
        R_ = s.ov("R_", [128, 2 * QW], F32, 28 * 1024)
        B_ = s.ov("B_", [128, 2 * QW], F32, 94 * 1024)
        XN_ = s.ov("XN_", [128, 2 * QW], F32, 98 * 1024)
        s.KEF = [s.ov(f"KEF{i}", [128, 1024], F32, 32 * 1024 + i * 4096)
                 for i in range(2)]
        s.VEF = [s.ov(f"VEF{i}", [128, 1024], F32, 40 * 1024 + i * 4096)
                 for i in range(2)]
        AT = [s.ov(f"AT{i}", [128, 256], F32, 80 * 1024 + i * 1024)
              for i in range(2)]
        s.do("vector", lambda e: e.memset(s.SM[:, 30:31], 128.5), w=("SM30",))

        KEF, VEF = s.KEF, s.VEF
        s.loadw("wuq1", WPA, "WPA", cols=256)
        s.loadw("wuk1", WPB, "WPB", cols=256)
        for m in range(2):
            for ch in range(8):
                s.do("tensor", lambda e, m=m, ch=ch, WPA=WPA, WPB=WPB:
                     e.matmul(s.PS[2][0:128, 0:256],
                              WPA[:, 256 * ch + 128 * m:256 * ch + 128 * (m + 1)],
                              WPB[:, 256 * ch:256 * (ch + 1)],
                              start=(ch == 0), stop=(ch == 7)),
                     r=("WPA", "WPB"), w=("PS2",))
            s.close_group("PS2")
            s.do("vector", lambda e, m=m, AT=AT:
                 e.tensor_copy(AT[m][:], s.PS[2][0:128, 0:256]),
                 r=("PS2",), w=(f"AT{m}",))
        s.loadw("wdq1", WPA, "WPA", cols=1024)
        for m in range(2):
            for nn in range(2):
                for ch in range(2):
                    s.do("tensor", lambda e, m=m, nn=nn, ch=ch, AT=AT, WPA=WPA:
                         e.matmul(s.PS[2 + nn][:],
                                  AT[ch][:, 128 * m:128 * (m + 1)],
                                  WPA[:, 1024 * ch + 512 * nn:1024 * ch + 512 * (nn + 1)],
                                  start=(ch == 0), stop=(ch == 1)),
                         r=(f"AT{ch}", "WPA"), w=(f"PS{2 + nn}",))
                s.close_group(f"PS{2 + nn}")
            for nn in range(2):
                s.do("vector", lambda e, m=m, nn=nn, KEF=KEF:
                     e.tensor_copy(KEF[m][:, 512 * nn:512 * (nn + 1)],
                                   s.PS[2 + nn][:]),
                     r=(f"PS{2 + nn}",), w=(f"KEF{m}",))
        s.loadw("wuv1", WPB, "WPB", cols=256)
        for nn in range(2):
            for g4 in range(2):
                for ci in range(4):
                    ch = 4 * g4 + ci
                    s.do("sync", lambda e, ch=ch, ci=ci, nn=nn, WPA=WPA:
                         e.dma_start(out=WPA[:, 512 * ci:512 * (ci + 1)],
                                     in_=s.dr["wo1T"][128 * ch:128 * (ch + 1),
                                                      512 * nn:512 * (nn + 1)]),
                         w=("WPA",), dma="ld")
                for m in range(2):
                    for ci in range(4):
                        ch = 4 * g4 + ci
                        s.do("tensor", lambda e, m=m, ci=ci, ch=ch, WPA=WPA,
                             WPB=WPB:
                             e.matmul(s.PS[2 + m][:],
                                      WPB[:, 256 * ch + 128 * m:256 * ch + 128 * (m + 1)],
                                      WPA[:, 512 * ci:512 * (ci + 1)],
                                      start=(ch == 0), stop=(ch == 7)),
                             r=("WPB", "WPA"),
                             w=(f"PS{2 + m}",))
            s.close_group("PS2", "PS3")
            for m in range(2):
                s.do("vector", lambda e, m=m, nn=nn, VEF=VEF:
                     e.tensor_copy(VEF[m][:, 512 * nn:512 * (nn + 1)],
                                   s.PS[2 + m][:]),
                     r=(f"PS{2 + m}",), w=(f"VEF{m}",))
        for ch in range(NCH):
            wt = WTR[ch % 2]
            s.loadw("w0T", wt, f"WTR{ch % 2}", rowsl=slice(128 * ch, 128 * (ch + 1)))
            s.do("vector", lambda e, ch=ch, wt=wt:
                 e.tensor_reduce(s.SC[0][:, ch:ch + 1], wt[:], AX.X, OP.add,
                                 apply_absolute_value=True),
                 r=(f"WTR{ch % 2}",), w=("SC0",))
        s.do("tensor", lambda e:
             e.matmul(s.PS[6][0:128, 0:8], s.ALL1[:], s.SC[0][:, 0:8],
                      start=True, stop=True),
             r=("SC0", "all1"), w=("PS6",))
        s.do("vector", lambda e:
             e.tensor_reduce(s.SC[1][:, 0:1], s.PS[6][0:128, 0:8], AX.X, OP.add),
             r=("PS6",), w=("SC1",))
        s.do("vector", lambda e:
             e.tensor_scalar(s.SC[1][:, 1:2], s.SC[1][:, 0:1],
                             1.0 / (1024.0 * 1024.0), 1e-8, OP.mult, OP.add),
             r=("SC1",), w=("SC1",))
        s.do("vector", lambda e:
             e.reciprocal(s.SM[:, 20:21], s.SC[1][:, 1:2]),
             r=("SC1",), w=("SM20",))
        s.do("vector", lambda e:
             e.tensor_scalar_mul(s.SM[:, 21:22], s.SC[1][:, 1:2], 1.0 / 127.0),
             r=("SC1",), w=("SM21",))
        for ch in range(NCH):
            wt = WTR[ch % 2]
            s.loadw("w0T", wt, f"WTR{ch % 2}", rowsl=slice(128 * ch, 128 * (ch + 1)))
            for hf in range(2):
                sl0 = 512 * hf
                s.do("vector", lambda e, wt=wt, sl0=sl0:
                     e.tensor_scalar(s.SC[0][:], wt[:, sl0:sl0 + 512],
                                     s.SM[:, 20:21], 0.5, OP.mult, OP.is_ge),
                     r=(f"WTR{ch % 2}", "SM20"), w=("SC0",))
                s.do("vector", lambda e, wt=wt, sl0=sl0:
                     e.tensor_scalar(s.SC[1][:], wt[:, sl0:sl0 + 512],
                                     s.SM[:, 20:21], -0.5, OP.mult, OP.is_le),
                     r=(f"WTR{ch % 2}", "SM20"), w=("SC1",))
                s.do("vector", lambda e, ch=ch, sl0=sl0, WQ=WQ:
                     e.tensor_tensor(WQ[:, 1024 * ch + sl0:1024 * ch + sl0 + 512],
                                     s.SC[0][:], s.SC[1][:], OP.subtract),
                     r=("SC0", "SC1"), w=("WQ",))
        for tt in range(16):
            for grp in range(2):
                bk = f"PS{6 + grp}"
                bank = s.PS[6 + grp]
                dst = XN_ if grp == 0 else U_
                eng = "vector" if grp == 0 else "gpsimd"
                for ci in range(4):
                    c = 4 * grp + ci
                    xk = f"XN{grp}{ci}"
                    xwx = ("WPB",) if grp == 1 and tt == 0 else ()
                    s.do(eng, lambda e, c=c, tt=tt, dst=dst, ci=ci:
                         e.tensor_tensor(dst[:, 128 * ci:128 * (ci + 1)],
                                         s.X[c][:, 128 * tt:128 * (tt + 1)],
                                         s.MU[:, 128 * tt:128 * (tt + 1)],
                                         OP.subtract),
                         r=(f"X{c}q{tt // 4}", f"MU{tt // 4}"), w=(xk,) + xwx)
                    s.do(eng, lambda e, tt=tt, dst=dst, ci=ci:
                         e.tensor_tensor(dst[:, 128 * ci:128 * (ci + 1)],
                                         dst[:, 128 * ci:128 * (ci + 1)],
                                         s.RBC[:, 128 * tt:128 * (tt + 1)],
                                         OP.mult),
                         r=(xk, f"RB{tt // 4}"), w=(xk,))
                    s.do("tensor", lambda e, ci=ci, bank=bank, dst=dst:
                         e.transpose(bank[:, 128 * ci:128 * (ci + 1)],
                                     dst[:, 128 * ci:128 * (ci + 1)], s.I128[:]),
                         r=(xk, "ident"), w=(bk,))
                s.close_group(bk)
            s.do("vector", lambda e, tt=tt, AMX=AMX:
                 e.tensor_reduce(AMX[:, tt:tt + 1], s.PS[6][:], AX.X, OP.max,
                                 apply_absolute_value=True),
                 r=("PS6",), w=("AMXa",))
            s.do("vector", lambda e, tt=tt, AMX=AMX:
                 e.tensor_reduce(AMX[:, 32 + tt:33 + tt], s.PS[7][:], AX.X, OP.max,
                                 apply_absolute_value=True),
                 r=("PS7",), w=("AMXb",))
            s.do("vector", lambda e, tt=tt, AMX=AMX:
                 e.tensor_tensor(AMX[:, tt:tt + 1], AMX[:, tt:tt + 1],
                                 AMX[:, 32 + tt:33 + tt], OP.max),
                 r=("AMXa", "AMXb"), w=("AMXa",))
        s.do("vector", lambda e, AMX=AMX:
             e.tensor_single_scalar(AMX[:, 0:16], AMX[:, 0:16], 1e-8, OP.add),
             r=("AMXa",), w=("AMXa",))
        s.do("vector", lambda e, AMX=AMX:
             e.tensor_scalar_mul(AMX[:, 16:32], AMX[:, 0:16], s.SM[:, 21:22]),
             r=("AMXa", "SM21"), w=("AMXa",))
        s.do("vector", lambda e, AMX=AMX:
             e.reciprocal(AMX[:, 0:16], AMX[:, 0:16]), r=("AMXa",), w=("AMXa",))
        s.do("vector", lambda e, AMX=AMX:
             e.tensor_single_scalar(AMX[:, 0:16], AMX[:, 0:16], 127.0, OP.mult),
             r=("AMXa",), w=("AMXa",))
        s.do("tensor", lambda e, AMX=AMX:
             e.transpose(s.PS[6][0:32, 0:128], AMX[:, 0:32], s.I128[:]),
             r=("AMXa", "ident"), w=("PS6",))
        s.do("vector", lambda e, SB32=SB32:
             e.tensor_copy(SB32[:], s.PS[6][0:32, 0:128]), r=("PS6",), w=("SB32",))
        s.do("sync", lambda e, SB32=SB32, ROWR=ROWR:
             e.dma_start(out=ROWR[0:1, :].rearrange("p (r j) -> p r j", r=32),
                         in_=SB32[:, :]),
             r=("SB32",), w=("ROWR",), dma="ld")
        RABP = [RAB, s.ov("RAB2", [128, QW], F32, 102 * 1024)]

        def rab_bcast(qq):
            rab, rk = RABP[qq % 2], f"RABp{qq % 2}"
            s.do("tensor", lambda e, qq=qq, ROWR=ROWR:
                 e.matmul(s.PS[6][:], s.ONE1[0:1, :],
                          ROWR[0:1, QW * qq:QW * (qq + 1)], start=True, stop=True),
                 r=("ROWR", "one1"), w=("PS6",))
            s.do("vector", lambda e, rab=rab:
                 e.tensor_copy(rab[:], s.PS[6][:]), r=("PS6",), w=(rk,))

        rab_bcast(0)
        for qq in range(NQ):
            rab, rk = RABP[qq % 2], f"RABp{qq % 2}"
            s.do("tensor", lambda e, qq=qq, ROWR=ROWR:
                 e.matmul(s.PS[6][:], s.ONE1[0:1, :],
                          ROWR[0:1, 2048 + QW * qq:2048 + QW * (qq + 1)],
                          start=True, stop=True),
                 r=("ROWR", "one1"), w=("PS6",))
            s.do("vector", lambda e, ASB=ASB:
                 e.tensor_copy(ASB[:], s.PS[6][:]), r=("PS6",), w=("ASB",))
            if qq + 1 < NQ:
                rab_bcast(qq + 1)
            for cp in range(0, NCH, 2):
                wx = ("XN00", "XN01", "XN02", "XN03") if qq == 0 and cp == 0 \
                    else ()
                s.xn_into(XN_[:, 0:QW], "XN", cp, qq, eng="gpsimd", wx=wx)
                s.xn_into(XN_[:, QW:2 * QW], "XN", cp + 1, qq, eng="gpsimd")
                for j in range(2):
                    s.do("gpsimd", lambda e, j=j, XN_=XN_, rab=rab:
                         e.tensor_tensor(XN_[:, QW * j:QW * (j + 1)],
                                         XN_[:, QW * j:QW * (j + 1)],
                                         rab[:], OP.mult),
                         r=("XN", rk), w=("XN",))
                uwx = ("XN10", "XN11", "XN12", "XN13", "WPB") \
                    if qq == 0 and cp == 0 else ()
                s.do("scalar", lambda e, U_=U_, XN_=XN_:
                     e.activation(U_[:], XN_[:], AF.Identity,
                                  bias=s.SM[:, 30:31], scale=1.0),
                     r=("XN", "SM30"), w=("U",) + uwx)
                rwx = ("WPB",) if qq == 0 and cp == 0 else ()
                s.do("scalar", lambda e, R_=R_, U_=U_:
                     e.activation(R_[:], U_[:], AF.Copy),
                     r=("U",), w=("R",) + rwx)
                for k in range(7, -1, -1):
                    val = float(1 << k)
                    s.do("vector", lambda e, val=val, B_=B_, R_=R_:
                         e.tensor_scalar(B_[:], R_[:], val, val, OP.is_ge, OP.mult),
                         r=("R",), w=("B",))
                    s.do("vector", lambda e, R_=R_, B_=B_:
                         e.tensor_tensor(R_[:], R_[:], B_[:], OP.subtract),
                         r=("R", "B"), w=("R",))
                s.do("vector", lambda e, cp=cp, U_=U_, R_=R_, AQW=AQW:
                     e.scalar_tensor_tensor(
                         AQW[:, QW * cp:QW * (cp + 2)],
                         U_[:], -128.0, R_[:], OP.add, OP.subtract),
                     r=("U", "R"), w=(f"AQc{cp}", f"AQc{cp + 1}"))
                for mch in range(NCH):
                    for kc in (cp, cp + 1):
                        s.do("tensor", lambda e, mch=mch, kc=kc,
                             WQ=WQ, AQW=AQW:
                             e.matmul(s.PS[mch][:],
                                      WQ[:, 1024 * kc + 128 * mch:1024 * kc + 128 * (mch + 1)],
                                      AQW[:, QW * kc:QW * (kc + 1)],
                                      start=(kc == 0), stop=(kc == NCH - 1)),
                             r=("WQ", f"AQc{kc}"), w=(f"PS{mch}",))
            for mch in range(NCH):
                s.close_group(f"PS{mch}")
            for mch in range(NCH):
                s.do("vector", lambda e, mch=mch, ASB=ASB:
                     e.tensor_tensor(s.SC[2 + mch % 2][:], s.PS[mch][:], ASB[:],
                                     OP.mult),
                     r=(f"PS{mch}", "ASB"), w=(f"SC{2 + mch % 2}",))
                s.do("gpsimd", lambda e, mch=mch, qq=qq:
                     e.tensor_tensor(s.X[mch][:, QW * qq:QW * (qq + 1)],
                                     s.X[mch][:, QW * qq:QW * (qq + 1)],
                                     s.SC[2 + mch % 2][:], OP.add),
                     r=(f"SC{2 + mch % 2}", f"X{mch}q{qq}"), w=(f"X{mch}q{qq}",))

    def emit_l1a(self):
        s = self
        CKV = [s.ov(f"CKV{i}", [128, T], F32, i * 8192) for i in range(2)]
        UU = [s.ov(f"UU{i}", [128, T], F32, 16 * 1024 + i * 8192) for i in range(2)]
        KEF, VEF = s.KEF, s.VEF
        WA = [s.ov(f"L1W{i}", [128, T], F32, 48 * 1024 + i * 8192) for i in range(4)]
        WUQT = s.ov("WUQT", [128, T], F32, 94 * 1024)

        s.loadw("wdq1T", WA[0], "WA0", cols=256)
        s.loadw("wdkv1T", WA[1], "WA1", cols=256)
        for qq in range(NQ):
            for c in range(NCH):
                s.xn_into(s.SC[0][:], "SC0", c, qq)
                for m in range(2):
                    s.do("tensor", lambda e, c=c, m=m:
                         e.matmul(s.PS[2 + m][:],
                                  WA[0][:, 256 * c + 128 * m:256 * c + 128 * (m + 1)],
                                  s.SC[0][:], start=(c == 0), stop=(c == NCH - 1)),
                         r=("WA0", "SC0"), w=(f"PS{2 + m}",))
                    s.do("tensor", lambda e, c=c, m=m:
                         e.matmul(s.PS[4 + m][:],
                                  WA[1][:, 256 * c + 128 * m:256 * c + 128 * (m + 1)],
                                  s.SC[0][:], start=(c == 0), stop=(c == NCH - 1)),
                         r=("WA1", "SC0"), w=(f"PS{4 + m}",))
            s.close_group("PS2", "PS3", "PS4", "PS5")
            for m in range(2):
                s.do("vector", lambda e, m=m, qq=qq:
                     e.tensor_copy(UU[m][:, QW * qq:QW * (qq + 1)], s.PS[2 + m][:]),
                     r=(f"PS{2 + m}",), w=(f"UU{m}",))
                s.do("vector", lambda e, m=m, qq=qq:
                     e.tensor_copy(CKV[m][:, QW * qq:QW * (qq + 1)], s.PS[4 + m][:]),
                     r=(f"PS{4 + m}",), w=(f"CKV{m}",))
        s.loadw("wuq1T", WUQT, "WUQT", cols=1024)
        for m in range(2):
            s.do("vector", lambda e, m=m, WUQT=WUQT:
                 e.tensor_single_scalar(WUQT[:, 1024 * m:1024 * (m + 1)],
                                        WUQT[:, 1024 * m:1024 * (m + 1)],
                                        0.125, OP.mult),
                 r=("WUQT",), w=("WUQT",))
        QP = s.ov("QP", [128, T], F32, 48 * 1024)
        KHD = s.ov("KHD", [128, T], F32, 56 * 1024)
        VFA = s.ov("VFA1", [128, T], F32, 64 * 1024)
        VFB = s.ov("VFB1", [128, T], F32, 72 * 1024)
        s.do("vector", lambda e, VFA=VFA: e.memset(VFA[:], 1.0), w=("WA2",))
        s.do("vector", lambda e, VFB=VFB: e.memset(VFB[:], 1.0), w=("WA3",))
        for c in range(NCH):
            s.flush_ep()
            if c == 2:
                s.emit_l1f_prologue()
            for qq in range(NQ):
                bk = f"PS{2 + (qq % 4)}"
                bank = s.PS[2 + (qq % 4)]
                for m in range(2):
                    s.do("tensor", lambda e, c=c, qq=qq, m=m, bank=bank, WUQT=WUQT,
                         UU=UU:
                         e.matmul(bank[:],
                                  WUQT[:, 1024 * m + 128 * c:1024 * m + 128 * (c + 1)],
                                  UU[m][:, QW * qq:QW * (qq + 1)],
                                  start=(m == 0), stop=(m == 1)),
                         r=("WUQT", f"UU{m}"), w=(bk, "WA0"))
                s.close_group(bk)
                s.do("vector", lambda e, qq=qq, bank=bank, QP=QP:
                     e.tensor_copy(QP[:, QW * qq:QW * (qq + 1)], bank[:]),
                     r=(bk,), w=("WA0",))
            for nn in range(NQ):
                bk = f"PS{2 + (nn % 4)}"
                bank = s.PS[2 + (nn % 4)]
                for m in range(2):
                    s.do("tensor", lambda e, c=c, nn=nn, m=m, bank=bank, KEF=KEF,
                         CKV=CKV:
                         e.matmul(bank[:], KEF[m][:, 128 * c:128 * (c + 1)],
                                  CKV[m][:, QW * nn:QW * (nn + 1)],
                                  start=(m == 0), stop=(m == 1)),
                         r=(f"KEF{m}", f"CKV{m}"), w=(bk, "WA1"))
                s.close_group(bk)
                s.do("vector", lambda e, nn=nn, bank=bank, KHD=KHD:
                     e.tensor_copy(KHD[:, QW * nn:QW * (nn + 1)], bank[:]),
                     r=(bk,), w=("WA1",))
            for kc in range(NKC):
                bk = f"PS{2 + (kc % 4)}"
                bank = s.PS[2 + (kc % 4)]
                for m in range(2):
                    s.do("tensor", lambda e, kc=kc, m=m, bank=bank, CKV=CKV,
                         VEF=VEF, c=c:
                         e.matmul(bank[:, 0:128],
                                  CKV[m][:, 128 * kc:128 * (kc + 1)],
                                  VEF[m][:, 128 * c:128 * c + 128],
                                  start=(m == 0), stop=(m == 1)),
                         r=(f"CKV{m}", f"VEF{m}"), w=(bk,))
                s.close_group(bk)
                s.do("vector", lambda e, kc=kc, bank=bank, VFA=VFA:
                     e.tensor_copy(VFA[:, 128 * kc:128 * kc + 64],
                                   bank[:, 0:64]),
                     r=(bk,), w=("WA2",))
                s.do("vector", lambda e, kc=kc, bank=bank, VFB=VFB:
                     e.tensor_copy(VFB[:, 128 * kc + 64:128 * kc + 128],
                                   bank[:, 64:128]),
                     r=(bk,), w=("WA3",))
            for half, (VF, vk_) in enumerate(((VFA, "WA2"), (VFB, "WA3"))):
                h = 2 * c + half
                hs0 = 64 * half
                s.head_bound(QP, "WA0", KHD, "WA1", 20 + h)
                negc = s.SM[:, 20 + h:21 + h]
                xs = s.X[c]
                s.attn_head(
                    k_lhsT=lambda kc, KHD=KHD, hs0=hs0:
                        KHD[hs0:hs0 + 64, 128 * kc:128 * (kc + 1)],
                    kkey="WA1",
                    q_rhs=lambda qq, QP=QP, hs0=hs0:
                        QP[hs0:hs0 + 64, QW * qq:QW * (qq + 1)],
                    qkey="WA0",
                    v_lhsT=lambda kc, VF=VF: VF[:, 128 * kc:128 * (kc + 1)],
                    vkey=vk_,
                    negc=negc, nkey=f"SM{20 + h}",
                    xout=lambda qq, xs=xs, hs0=hs0:
                        xs[hs0:hs0 + 64, QW * qq:QW * (qq + 1)],
                    xkey=lambda qq, c=c: f"X{c}q{qq}",
                    half=half)
        s.flush_ep()
        for qq in range(NQ):
            s.emit_ln_win(qq)
        W1F = [s.ov(f"W1F{i}", [128, 8 * FCW], F32, 80 * 1024 + i * 4096)
               for i in range(2)]
        W1H0 = s.ov("W1H0", [128, 8 * FCW], BF16, 88 * 1024)
        W1L0 = s.ov("W1L0", [128, 8 * FCW], BF16, 90 * 1024)
        for fc in range(2):
            s.do("sync", lambda e, fc=fc, W1F=W1F:
                 e.dma_start(out=W1F[fc][:],
                             in_=s.dr["w1p"][128 * fc:128 * (fc + 1), :]),
                 w=(f"W1F{fc}",), dma="ld")
        s.do("vector", lambda e, W1F=W1F, W1H0=W1H0:
             e.tensor_copy(W1H0[:], W1F[0][:]), r=("W1F0",), w=("W1H0",))
        s.do("vector", lambda e, W1F=W1F, W1H0=W1H0, W1L0=W1L0:
             e.tensor_tensor(W1L0[:], W1F[0][:], W1H0[:], OP.subtract),
             r=("W1F0", "W1H0"), w=("W1L0", "mask"))

    def emit_l1f_prologue(self):
        s = self
        VEC = s.VEC = s.sb("VECL", [128, 192], F32)
        TR32 = s.sb("TR32", [32, 128], F32)

        def col_vec(name, dst_c0, ncols, key):
            nch = ncols // 128
            s.do("sync", lambda e, name=name, nch=nch:
                 e.dma_start(out=TR32[0:nch, :], in_=s.dr[name][0:nch, :]),
                 w=("TR32",), dma="ld")
            s.do("tensor", lambda e, nch=nch:
                 e.transpose(s.PS[7][0:128, 0:nch], TR32[0:nch, :],
                             s.I128[0:nch, 0:nch]),
                 r=("TR32", "ident"), w=("PS7",))
            s.do("vector", lambda e, dst_c0=dst_c0, nch=nch:
                 e.tensor_copy(VEC[:, dst_c0:dst_c0 + nch], s.PS[7][0:128, 0:nch]),
                 r=("PS7",), w=(key,))

        col_vec("b1", 64, DFF, "B1C")
        col_vec("b2", 96, D, "B2C")
        col_vec("qwr", 104, DFF, "QWR")
        col_vec("th1", 0, DFF, "THR")
        col_vec("qwi", 32, DFF, "QWI")
        SCR = s.SC[1]

        def rred(c0, key):
            s.do("vector", lambda e, c0=c0:
                 e.tensor_scalar(SCR[:, 0:32], VEC[:, c0:c0 + 32],
                                 1.0 / (2 * math.pi), 8.5, OP.mult, OP.add),
                 r=(key,), w=("RRS",))
            s.do("vector", lambda e:
                 e.tensor_copy(SCR[:, 32:64], SCR[:, 0:32]),
                 r=("RRS",), w=("RRS",))
            for k in (8.0, 4.0, 2.0, 1.0):
                s.do("vector", lambda e, k=k:
                     e.tensor_scalar(SCR[:, 64:96], SCR[:, 32:64], k, k,
                                     OP.is_ge, OP.mult),
                     r=("RRS",), w=("RRS",))
                s.do("vector", lambda e:
                     e.tensor_tensor(SCR[:, 32:64], SCR[:, 32:64],
                                     SCR[:, 64:96], OP.subtract),
                     r=("RRS",), w=("RRS",))
            s.do("vector", lambda e:
                 e.scalar_tensor_tensor(SCR[:, 64:96], SCR[:, 0:32], -8.0,
                                        SCR[:, 32:64], OP.add, OP.subtract),
                 r=("RRS",), w=("RRS",))
            s.do("vector", lambda e, c0=c0:
                 e.scalar_tensor_tensor(VEC[:, c0:c0 + 32], SCR[:, 64:96],
                                        -TPI_HI, VEC[:, c0:c0 + 32],
                                        OP.mult, OP.add),
                 r=("RRS", key), w=(key,))
            s.do("vector", lambda e, c0=c0:
                 e.scalar_tensor_tensor(VEC[:, c0:c0 + 32], SCR[:, 64:96],
                                        -TPI_LO, VEC[:, c0:c0 + 32],
                                        OP.mult, OP.add),
                 r=("RRS", key), w=(key,))

        s.do("vector", lambda e:
             e.tensor_copy(SCR[:, 96:128], VEC[:, 0:32]), r=("THR",), w=("TSIN",))
        s.do("vector", lambda e:
             e.tensor_copy(VEC[:, 136:168], SCR[:, 96:128]), r=("TSIN",), w=("TS2",))
        rred(136, "TS2")
        s.do("scalar", lambda e:
             e.activation(s.SC[2][:, 0:32], VEC[:, 136:168], AF.Sin),
             r=("TS2",), w=("SC2",))
        s.do("vector", lambda e:
             e.tensor_scalar(VEC[:, 136:168], SCR[:, 96:128], 1.0, PI_F / 2.0,
                             OP.mult, OP.add),
             r=("TSIN",), w=("TS2",))
        rred(136, "TS2")
        s.do("scalar", lambda e:
             e.activation(s.SC[3][:, 0:32], VEC[:, 136:168], AF.Sin),
             r=("TS2",), w=("SC3",))
        s.do("vector", lambda e:
             e.tensor_tensor(s.SC[2][:, 32:64], s.SC[2][:, 0:32], VEC[:, 32:64],
                             OP.mult),
             r=("SC2", "QWI"), w=("SC2",))
        s.do("vector", lambda e:
             e.tensor_tensor(s.SC[3][:, 32:64], s.SC[3][:, 0:32], VEC[:, 104:136],
                             OP.mult),
             r=("SC3", "QWR"), w=("SC3",))
        s.do("vector", lambda e:
             e.tensor_tensor(s.SC[2][:, 32:64], s.SC[2][:, 32:64],
                             s.SC[3][:, 32:64], OP.add),
             r=("SC2", "SC3"), w=("SC2",))
        s.do("vector", lambda e:
             e.scalar_tensor_tensor(VEC[:, 32:64], s.SC[2][:, 32:64], 0.1,
                                    VEC[:, 32:64], OP.mult, OP.mult),
             r=("SC2", "QWI"), w=("PP",))
        s.do("vector", lambda e:
             e.scalar_tensor_tensor(VEC[:, 0:32], VEC[:, 64:96], 0.1,
                                    VEC[:, 0:32], OP.mult, OP.add),
             r=("B1C", "THR"), w=("THR",))
        rred(0, "THR")

    def emit_l1f(self):
        s = self
        XNH = s.ov("XNH", [128, 8 * QW], BF16, 0)
        XNL = s.ov("XNL", [128, 8 * QW], BF16, 8 * 1024)
        ACH = s.ov("ACH", [128, 32 * QW], BF16, 16 * 1024)
        ACL = s.ov("ACL", [128, 32 * QW], BF16, 48 * 1024)
        W1F = [s.ov(f"W1F{i}", [128, 8 * FCW], F32, 80 * 1024 + i * 4096)
               for i in range(2)]
        W1H = [s.ov(f"W1H{i}", [128, 8 * FCW], BF16, 88 * 1024 + i * 4096)
               for i in range(2)]
        W1L = [s.ov(f"W1L{i}", [128, 8 * FCW], BF16, 90 * 1024 + i * 4096)
               for i in range(2)]
        W2F = [s.ov(f"W2F{i}", [128, 1024], F32, 96 * 1024 + i * 4096)
               for i in range(2)]
        W2H, W2L = W1H, W1L
        VEC = s.VEC

        def xn_chunk(qq, c):
            s.xn_into(s.SC[0][:], "SC0", c, qq)
            s.do("vector", lambda e, c=c, XNH=XNH:
                 e.tensor_copy(XNH[:, QW * c:QW * (c + 1)], s.SC[0][:]),
                 r=("SC0",), w=("XNH",))
            s.do("vector", lambda e, c=c, XNH=XNH, XNL=XNL:
                 e.tensor_tensor(XNL[:, QW * c:QW * (c + 1)], s.SC[0][:],
                                 XNH[:, QW * c:QW * (c + 1)], OP.subtract),
                 r=("SC0", "XNH"), w=("XNL",))

        def w1_load(fc):
            w1f, w1k = W1F[fc % 2], f"W1F{fc % 2}"
            s.do("sync", lambda e, fc=fc, w1f=w1f:
                 e.dma_start(out=w1f[:],
                             in_=s.dr["w1p"][128 * fc:128 * (fc + 1), :]),
                 w=(w1k,), dma="ld")

        def w1_split(fc):
            w1f, w1k = W1F[fc % 2], f"W1F{fc % 2}"
            w1h, w1l = W1H[fc % 2], W1L[fc % 2]
            hk, lk2 = f"W1H{fc % 2}", f"W1L{fc % 2}"
            s.do("scalar", lambda e, w1f=w1f, w1h=w1h:
                 e.activation(w1h[:], w1f[:], AF.Copy), r=(w1k,), w=(hk,))
            s.do("vector", lambda e, w1f=w1f, w1h=w1h, w1l=w1l:
                 e.tensor_tensor(w1l[:], w1f[:], w1h[:], OP.subtract),
                 r=(w1k, hk), w=(lk2,))

        def w2_load(kc):
            w2f, w2k = W2F[kc % 2], f"W2F{kc % 2}"
            s.do("sync", lambda e, kc=kc, w2f=w2f:
                 e.dma_start(out=w2f[:],
                             in_=s.dr["w2"][128 * kc:128 * (kc + 1), :]),
                 w=(w2k,), dma="ld")

        def w2_split(kc):
            w2f, w2k = W2F[kc % 2], f"W2F{kc % 2}"
            w2h, w2l = W2H[kc % 2], W2L[kc % 2]
            hk, lk2 = f"W1H{kc % 2}", f"W1L{kc % 2}"
            s.do("scalar", lambda e, w2f=w2f, w2h=w2h:
                 e.activation(w2h[:, 0:1024], w2f[:], AF.Copy),
                 r=(w2k,), w=(hk,))
            s.do("vector", lambda e, w2f=w2f, w2h=w2h, w2l=w2l:
                 e.tensor_tensor(w2l[:, 0:1024], w2f[:], w2h[:, 0:1024],
                                 OP.subtract),
                 r=(w2k, hk), w=(lk2,))

        def fc_matmuls(fc):
            w1h, w1l = W1H[fc % 2], W1L[fc % 2]
            hk, lk2 = f"W1H{fc % 2}", f"W1L{fc % 2}"
            bk = f"PS{fc % 4}"
            bank = s.PS[fc % 4]
            for kc in range(8):
                for pi, (lh, lk, rh, rk) in enumerate(
                        ((w1h, hk, XNH, "XNH"),
                         (w1h, hk, XNL, "XNL"),
                         (w1l, lk2, XNH, "XNH"))):
                    s.do("tensor", lambda e, kc=kc, lh=lh, rh=rh, bank=bank,
                         pi=pi:
                         e.matmul(bank[:], lh[:, FCW * kc:FCW * (kc + 1)],
                                  rh[:, QW * kc:QW * (kc + 1)],
                                  start=(kc == 0 and pi == 0),
                                  stop=(kc == 7 and pi == 2)),
                         r=(lk, rk), w=(bk,))
            s.close_group(bk)

        def quantum(fc, qq):
            bk = f"PS{fc % 4}"
            bank = s.PS[fc % 4]
            colv = fc
            s.do("scalar", lambda e, bank=bank, colv=colv:
                 e.activation(s.SC[0][:], bank[:], AF.Identity,
                              bias=VEC[:, colv:colv + 1], scale=0.1),
                 r=(bk, "THR"), w=("SC0",))
            s.do("scalar", lambda e, bank=bank, colv=colv:
                 e.activation(s.SC[1][:], bank[:], AF.Identity,
                              bias=VEC[:, 64 + colv:65 + colv], scale=1.0),
                 r=(bk, "B1C"), w=("SC1",))
            s.do("vector", lambda e:
                 e.tensor_scalar(s.SC[2][:], s.SC[0][:], PI_F, TPI_HI,
                                 OP.is_ge, OP.mult),
                 r=("SC0",), w=("SC2",))
            s.do("vector", lambda e:
                 e.tensor_scalar(s.SC[3][:], s.SC[0][:], -PI_F, TPI_HI,
                                 OP.is_le, OP.mult),
                 r=("SC0",), w=("SC3",))
            s.do("vector", lambda e:
                 e.tensor_tensor(s.SC[0][:], s.SC[0][:], s.SC[2][:],
                                 OP.subtract),
                 r=("SC0", "SC2"), w=("SC0",))
            s.do("gpsimd", lambda e:
                 e.tensor_tensor(s.SC[0][:], s.SC[0][:], s.SC[3][:], OP.add),
                 r=("SC0", "SC3"), w=("SC0",))
            s.do("scalar", lambda e:
                 e.activation(s.SC[2][:], s.SC[0][:], AF.Sin),
                 r=("SC0",), w=("SC2",))
            s.do("vector", lambda e, colv=colv:
                 e.scalar_tensor_tensor(s.SC[0][:], s.SC[2][:],
                                        VEC[:, 32 + colv:33 + colv],
                                        s.SC[1][:], OP.mult, OP.add),
                 r=("SC2", "PP", "SC1"), w=("SC0",))
            s.do("scalar", lambda e, fc=fc, ACH=ACH:
                 e.activation(ACH[:, QW * fc:QW * (fc + 1)], s.SC[0][:],
                              AF.Relu),
                 r=("SC0",), w=("ACH",))
            s.do("scalar", lambda e:
                 e.activation(s.SC[3][:], s.SC[0][:], AF.Relu),
                 r=("SC0",), w=("SC3",))
            s.do("gpsimd", lambda e, fc=fc, ACH=ACH, ACL=ACL:
                 e.tensor_tensor(ACL[:, QW * fc:QW * (fc + 1)], s.SC[3][:],
                                 ACH[:, QW * fc:QW * (fc + 1)], OP.subtract),
                 r=("SC3", "ACH"), w=("ACL",))

        def w2_matmuls(kc):
            w2h, w2l = W2H[kc % 2], W2L[kc % 2]
            hk, lk2 = f"W1H{kc % 2}", f"W1L{kc % 2}"
            for mch in range(NCH):
                for pi, (lh, lk, rh, rk) in enumerate(
                        ((w2h, hk, ACH, "ACH"),
                         (w2h, hk, ACL, "ACL"),
                         (w2l, lk2, ACH, "ACH"))):
                    s.do("tensor", lambda e, kc=kc, mch=mch, lh=lh, rh=rh,
                         pi=pi:
                         e.matmul(s.PS[mch][:],
                                  lh[:, 128 * mch:128 * (mch + 1)],
                                  rh[:, QW * kc:QW * (kc + 1)],
                                  start=(kc == 0 and pi == 0),
                                  stop=(kc == 31 and pi == 2)),
                         r=(lk, rk), w=(f"PS{mch}",))

        for qq in range(NQ):
            if qq == 0:
                for c in range(NCH):
                    xn_chunk(0, c)
            for fc in range(NFC):
                if fc + 1 < NFC:
                    w1_split(fc + 1)
                if fc + 2 < NFC:
                    w1_load(fc + 2)
                fc_matmuls(fc)
                if fc == NFC - 3:
                    w2_load(0)
                if fc == NFC - 2:
                    w2_split(0)
                    w2_load(1)
                quantum(fc, qq)
            for kc in range(32):
                if kc + 1 < 32:
                    w2_split(kc + 1)
                if kc + 2 < 32:
                    w2_load(kc + 2)
                w2_matmuls(kc)
                if kc < NCH and qq + 1 < NQ:
                    xn_chunk(qq + 1, kc)
                if qq + 1 < NQ:
                    if kc == 29:
                        w1_load(0)
                    if kc == 30:
                        w1_split(0)
                        w1_load(1)
            for mch in range(NCH):
                s.close_group(f"PS{mch}")
            for mch in range(NCH):
                s.do("vector", lambda e, mch=mch, qq=qq:
                     e.scalar_tensor_tensor(s.X[mch][:, QW * qq:QW * (qq + 1)],
                                            s.PS[mch][:], 0.0,
                                            s.X[mch][:, QW * qq:QW * (qq + 1)],
                                            OP.add, OP.add),
                     r=(f"PS{mch}", f"X{mch}q{qq}"), w=(f"X{mch}q{qq}",))
        for mch in range(NCH):
            for qq in range(NQ):
                s.do("vector", lambda e, mch=mch, qq=qq:
                     e.tensor_scalar_add(s.X[mch][:, QW * qq:QW * (qq + 1)],
                                         s.X[mch][:, QW * qq:QW * (qq + 1)],
                                         VEC[:, 96 + mch:97 + mch]),
                     r=(f"X{mch}q{qq}", "B2C"), w=(f"X{mch}q{qq}",))

    def fin(self):
        s = self
        self.emit_exit()
        nc = self.nc
        with nc.Block() as block:
            @block.tensor
            def _(e):
                for f in s.th["tensor"]:
                    f(e)

            @block.vector
            def _(e):
                for f in s.th["vector"]:
                    f(e)

            @block.scalar
            def _(e):
                for f in s.th["scalar"]:
                    f(e)

            @block.gpsimd
            def _(e):
                for f in s.th["gpsimd"]:
                    f(e)

            @block.sync
            def _(e):
                for f in s.th["sync"]:
                    f(e)
        return nc

    def emit_exit(self):
        s = self
        OT = [s.nc.alloc_sbuf_tensor_at(f"XO{i}", [128, 1024], F32,
                                        offset=self.ovl_base + 40 * 1024 + i * 4096)
              for i in range(2)]
        for tt in range(16):
            ot = OT[tt % 2]
            ok = f"OT{tt % 2}"
            for c in range(NCH):
                bk = f"PS{c % 2}"
                bank = s.PS[c % 2]
                s.do("tensor", lambda e, tt=tt, c=c, bank=bank:
                     e.transpose(bank[:, 0:128],
                                 s.X[c][:, 128 * tt:128 * (tt + 1)], s.I128[:]),
                     r=(f"X{c}q{tt // 4}", "ident"), w=(bk,))
                s.do("vector", lambda e, c=c, ot=ot, bank=bank:
                     e.tensor_copy(ot[:, 128 * c:128 * (c + 1)], bank[:, 0:128]),
                     r=(bk,), w=(ok,))
            s.do("sync", lambda e, tt=tt, ot=ot:
                 e.dma_start(out=s.y_out[128 * tt:128 * (tt + 1), :], in_=ot[:]),
                 r=(ok,), dma="st")
        s.do("sync", lambda e: e.nop(),
             xw=[(sn, s.cnt[sn]) for sn in s.STS])


def _host_inputs(inputs, core):
    f32 = np.float32
    x = np.ascontiguousarray(np.asarray(inputs["L0_attn_x"])[core]).astype(
        f32, copy=False)
    m = {"x_in": x}
    t_ = lambda a: np.ascontiguousarray(np.asarray(a, dtype=f32).T)
    c_ = lambda a: np.ascontiguousarray(np.asarray(a, dtype=f32))
    m["wdq"] = c_(inputs["L0_attn_W_DQ"])
    m["wuqT"] = t_(inputs["L0_attn_W_UQ"])
    m["wdkv"] = c_(inputs["L0_attn_W_DKV"])
    m["wukvT"] = t_(inputs["L0_attn_W_UKV"])
    m["wqrT"] = t_(inputs["L0_attn_W_QR"])
    m["wkrT"] = t_(inputs["L0_attn_W_KR"])
    m["woT"] = t_(inputs["L0_attn_W_O"])
    m["qnw"] = c_(inputs["L0_attn_q_norm_w"]).reshape(1, HD)
    m["knw"] = c_(inputs["L0_attn_k_norm_w"]).reshape(1, HD)
    m["w0T"] = t_(inputs["L0_ffn_weight"])
    m["wdq1T"] = t_(inputs["L1_attn_W_dq"])
    m["wdq1"] = c_(inputs["L1_attn_W_dq"])
    m["wuq1"] = c_(inputs["L1_attn_W_uq"])
    m["wuq1T"] = t_(inputs["L1_attn_W_uq"])
    m["wdkv1T"] = t_(inputs["L1_attn_W_dkv"])
    m["wuk1"] = c_(inputs["L1_attn_W_uk"])
    m["wuv1"] = c_(inputs["L1_attn_W_uv"])
    m["wo1T"] = t_(inputs["L1_attn_W_o"])
    w1 = c_(inputs["L1_ffn_W1"])
    m["w1p"] = np.ascontiguousarray(
        w1.reshape(8, 128, 32, 128).transpose(2, 1, 0, 3).reshape(DFF, D))
    m["w2"] = c_(inputs["L1_ffn_W2"])
    m["b1"] = c_(inputs["L1_ffn_b1"]).reshape(32, 128)
    m["th1"] = c_(inputs["L1_ffn_quantum_phases"]).reshape(32, 128)
    m["qwr"] = c_(inputs["L1_ffn_quantum_weights_real"]).reshape(32, 128)
    m["qwi"] = c_(inputs["L1_ffn_quantum_weights_imag"]).reshape(32, 128)
    m["b2"] = c_(inputs["L1_ffn_b2"]).reshape(8, 128)
    import ml_dtypes
    m["ident"] = np.eye(128, dtype=f32)
    m["identb"] = np.eye(128, dtype=ml_dtypes.bfloat16)
    msk = np.zeros((128, 4 * QW), dtype=f32)
    i = np.arange(128)[:, None]
    jj = np.arange(QW)[None, :]
    for j in range(4):
        msk[:, QW * j:QW * (j + 1)] = np.where((i + 128 * j) <= jj, 0.0,
                                               -30000.0).astype(f32)
    m["mask"] = msk.astype(ml_dtypes.bfloat16)
    m["all1"] = np.ones((128, 128), dtype=f32)
    m["one1"] = np.ones((1, 128), dtype=f32)
    o2b = np.zeros((128, 2), dtype=f32)
    o2b[0:64, 0] = 1.0
    o2b[64:128, 1] = 1.0
    m["ones2b"] = o2b
    ob2 = np.zeros((2, 128), dtype=f32)
    ob2[0, 0:64] = 1.0
    ob2[1, 64:128] = 1.0
    m["onesb2"] = ob2
    return m


_NC_CACHE = {}


def _get_nc(stop_after):
    if stop_after not in _NC_CACHE:
        _NC_CACHE[stop_after] = KB(stop_after).build()
    return _NC_CACHE[stop_after]


def kernel(**inputs):
    nc = _get_nc("l1f")
    in_maps = [_host_inputs(inputs, c) for c in range(8)]
    res = run_bass_kernel_spmd(nc, in_maps, core_ids=list(range(8)))
    return np.stack([r["y_out"] for r in res.results]).astype(np.float32)


# version 36
# speedup vs baseline: 1.0067x; 1.0067x over previous
import math
import numpy as np
import ml_dtypes
import concourse.bass as bass
import concourse.mybir as mybir
from concourse.bass_utils import run_bass_kernel_spmd

F32 = mybir.dt.float32
BF16 = mybir.dt.bfloat16
AF = mybir.ActivationFunctionType
OP = mybir.AluOpType
AX = mybir.AxisListType

T = 2048
D = 1024
NCH = 8
NQ = 4
QW = 512
NKC = 16
HD = 64
RMS_EPS = 1.1920929e-07
LN_EPS = 1e-5
DFF = 4096
FCW = 128
NFC = DFF // FCW
PI_F = float(np.float32(math.pi))
TPI_HI = float(np.float32(2.0 * math.pi))
TPI_LO = float(np.float32(2.0 * math.pi - float(np.float32(2.0 * math.pi))))
ENGS = ("tensor", "vector", "scalar", "gpsimd", "sync")
ESEM = {"tensor": "t", "vector": "v", "scalar": "s", "gpsimd": "g", "sync": "q"}


class KB:
    def __init__(self, stop_after="l1f"):
        self.stop_after = stop_after
        nc = bass.Bass("TRN2", target_bir_lowering=False)
        self.nc = nc
        self.th = {e: [] for e in ENGS}
        self.sems = {}
        self.cnt = {}
        self.hw = {e: {} for e in ENGS}
        self.LDS = [f"ld{i}" for i in range(8)]
        self.STS = [f"st{i}" for i in range(4)]
        for sname in ("t", "v", "s", "g", "q", *self.LDS, *self.STS):
            self.sems[sname] = nc.semaphore(sname).__enter__()
            self.cnt[sname] = 0
        self.ld_rr = 0
        self.st_rr = 0
        self.dr = {}
        self.AL = {}
        self.lw = {}
        self.rdrs = {}
        self.pend_ep = None

    def do(self, eng, build, r=(), w=(), dma=None, xw=(), force=False):
        waits = {}
        for k in r:
            for sn, v0 in self.lw.get(k, {}).items():
                waits[sn] = max(waits.get(sn, 0), v0)
        for k in w:
            for sn, v0 in self.lw.get(k, {}).items():
                waits[sn] = max(waits.get(sn, 0), v0)
            for sn, v0 in self.rdrs.get(k, {}).items():
                waits[sn] = max(waits.get(sn, 0), v0)
        for k in w:
            if k not in self.lw and k not in self.rdrs and getattr(self, "floor", None):
                for sn, v0 in self.floor.items():
                    waits[sn] = max(waits.get(sn, 0), v0)
        for sn, v0 in xw:
            if v0:
                waits[sn] = max(waits.get(sn, 0), v0)
        real = []
        for sn, v0 in waits.items():
            if not force:
                if eng == "tensor" and sn == "t":
                    continue
                if self.hw[eng].get(sn, 0) >= v0 and sn in ("q",):
                    continue
            self.hw[eng][sn] = max(self.hw[eng].get(sn, 0), v0)
            real.append((sn, v0))
        if dma is None:
            sn_inc, n = ESEM[eng], 1
        else:
            if dma == "ld":
                sn_inc = self.LDS[self.ld_rr % len(self.LDS)]
                self.ld_rr += 1
            else:
                sn_inc = self.STS[self.st_rr % len(self.STS)]
                self.st_rr += 1
            n = 16
            prev = self.cnt[sn_inc]
            if prev:
                self.hw[eng][sn_inc] = max(self.hw[eng].get(sn_inc, 0), prev)
                real.append((sn_inc, prev))
        self.cnt[sn_inc] += n
        mycnt = self.cnt[sn_inc]
        sems = self.sems

        def f(e, build=build, real=tuple(real), sn_inc=sn_inc, n=n):
            for (sn, v0) in real:
                e.wait_ge(sems[sn], v0)
            build(e).then_inc(sems[sn_inc], n)
        self.th[eng].append(f)
        for k in w:
            d = self.lw.setdefault(k, {})
            d[sn_inc] = max(d.get(sn_inc, 0), mycnt)
        for k in r:
            d = self.rdrs.setdefault(k, {})
            d[sn_inc] = max(d.get(sn_inc, 0), mycnt)
        return mycnt

    def close_group(self, *keys):
        for k in keys:
            d = self.lw.setdefault(k, {})
            d["t"] = max(d.get("t", 0), self.cnt["t"])

    def flush_ep(self):
        if self.pend_ep is not None:
            ep, self.pend_ep = self.pend_ep, None
            ep()

    def soft_barrier(self):
        self.flush_ep()
        self.floor = {ESEM[e]: self.cnt[ESEM[e]] for e in ENGS}
        for sn in (*self.LDS, *self.STS):
            self.floor[sn] = self.cnt[sn]

    def barrier(self):
        self.flush_ep()
        self.do("sync", lambda e: e.nop(),
                xw=[(sn, self.cnt[sn]) for sn in (*self.LDS, *self.STS)])
        vals = {}
        for e in ENGS:
            vals[e] = self.do(e, lambda en: en.nop(),
                              xw=[(ESEM[e], self.cnt[ESEM[e]])], force=True)
        for e in ENGS:
            for o in ENGS:
                if o == e:
                    continue
                sn = ESEM[o]
                if self.hw[e].get(sn, 0) < vals[o]:
                    self.hw[e][sn] = vals[o]
                    self.th[e].append(
                        lambda en, sn=sn, v0=vals[o]: en.wait_ge(self.sems[sn], v0))
            for sn in (*self.LDS, *self.STS):
                self.hw[e][sn] = max(self.hw[e].get(sn, 0), self.cnt[sn])
        self.floor = {ESEM[o]: vals[o] for o in ENGS}
        for sn in (*self.LDS, *self.STS):
            self.floor[sn] = self.cnt[sn]

    def din(self, name, shape, dt_=F32):
        self.dr[name] = self.nc.dram_tensor(name, list(shape), dt_,
                                            kind="ExternalInput").ap()
        return self.dr[name]

    def sb(self, name, shape, dt_=F32):
        self.AL[name] = self.nc.alloc_sbuf_tensor(name, list(shape), dt_)
        return self.AL[name]

    def ov(self, name, shape, dt_, off):
        sz = int(np.prod(shape[1:])) * mybir.dt.size(dt_)
        assert off + sz <= self.ARENA, (name, off, sz, self.ARENA)
        if name in self.AL:
            return self.AL[name]
        self.AL[name] = self.nc.alloc_sbuf_tensor_at(
            name, list(shape), dt_, offset=self.ovl_base + off)
        return self.AL[name]

    def loadw(self, name, dst, dkey, cols=None, rowsl=None, dstcol0=0):
        ap = self.dr[name]
        rows = ap.shape[0] if rowsl is None else (rowsl.stop - rowsl.start)
        r0 = 0 if rowsl is None else rowsl.start
        cols = ap.shape[1] if cols is None else cols
        nch = (rows + 127) // 128
        v = None
        for ch in range(nch):
            v = self.do("sync", lambda e, ap=ap, ch=ch, r0=r0, cols=cols, dst=dst,
                        dstcol0=dstcol0:
                        e.dma_start(out=dst[:, dstcol0 + cols * ch:dstcol0 + cols * (ch + 1)],
                                    in_=ap[r0 + 128 * ch:r0 + 128 * (ch + 1), 0:cols]),
                        w=(dkey,), dma="ld")
        return v

    def build(self):
        nc = self.nc
        s = self
        self.declare_dram()
        self.X = [s.sb(f"X{c}", [128, T]) for c in range(NCH)]
        self.MU = s.sb("MU", [128, T])
        self.RBC = s.sb("RBC", [128, T])
        self.E32 = [s.sb(f"E32_{i}", [128, QW]) for i in range(3)]
        self.I128 = s.sb("I128", [128, 128])
        self.ALL1 = s.sb("ALL1", [128, 128])
        self.ONES2B = s.sb("ONES2B", [128, 2])
        self.ONESB2 = s.sb("ONESB2", [2, 128])
        self.ONE1 = s.sb("ONE1", [1, 128])
        self.SM = s.sb("SM", [128, 64])
        self.ROWS = s.sb("ROWS", [2, QW])
        self.SC = [s.sb(f"SC_{i}", [128, QW]) for i in range(4)]
        self.RBF = s.sb("RBF", [128, QW])
        self.TMPE = s.sb("TMPE", [128, QW])
        self.ARENA = 104 * 1024
        self.ovl_base = nc.sbuf_top - self.ARENA
        self.PS = [nc.alloc_psum_tensor(f"PS{i}", [128, QW], F32) for i in range(8)]
        self.MSK = s.ov("MSK", [128, 4 * QW], BF16, 90 * 1024)
        self.I128B = s.sb("I128B", [128, 128], BF16)

        self.emit_entry()
        for phase in ("l0a", "l0f", "l1a", "l1f"):
            if phase != "l1f":
                self.emit_ln()
            getattr(self, "emit_" + phase)()
            self.soft_barrier()
            if self.stop_after == phase:
                break
        return self.fin()

    def declare_dram(self):
        s = self
        s.din("x_in", [T, D])
        self.y_out = s.nc.dram_tensor("y_out", [T, D], F32,
                                      kind="ExternalOutput").ap()
        s.din("wdq", [512, HD]); s.din("wuqT", [512, HD])
        s.din("wdkv", [1024, HD]); s.din("wukvT", [1024, 2 * HD])
        s.din("wqrT", [HD, HD]); s.din("wkrT", [HD, HD]); s.din("woT", [HD, HD])
        s.din("qnw", [1, HD]); s.din("knw", [1, HD])
        s.din("w0T", [D, D])
        s.din("wdq1T", [D, 256]); s.din("wdq1", [256, D])
        s.din("wuq1", [D, 256]); s.din("wuq1T", [256, D])
        s.din("wdkv1T", [D, 256]); s.din("wuk1", [D, 256]); s.din("wuv1", [D, 256])
        s.din("wo1T", [D, D])
        s.din("w1p", [DFF, D]); s.din("w2", [DFF, D])
        s.din("b1", [32, 128]); s.din("th1", [32, 128])
        s.din("qwr", [32, 128]); s.din("qwi", [32, 128]); s.din("b2", [8, 128])
        s.din("ident", [128, 128]); s.din("mask", [128, 4 * QW], BF16)
        s.din("identb", [128, 128], BF16)
        s.din("all1", [128, 128]); s.din("one1", [1, 128])
        s.din("ones2b", [128, 2]); s.din("onesb2", [2, 128])

    def emit_entry(self):
        s = self
        TM = [s.ov(f"TM{i}", [128, D], F32, i * 4096) for i in range(16)]
        for i in range(16):
            s.do("sync", lambda e, i=i, TM=TM:
                 e.dma_start(out=TM[i][:], in_=s.dr["x_in"][128 * i:128 * (i + 1), :]),
                 w=(f"TM{i}",), dma="ld")
        for nm, dst in (("ident", s.I128), ("mask", s.MSK), ("all1", s.ALL1),
                        ("one1", s.ONE1), ("ones2b", s.ONES2B),
                        ("onesb2", s.ONESB2), ("identb", s.I128B)):
            s.do("sync", lambda e, nm=nm, dst=dst:
                 e.dma_start(out=dst[:, :], in_=s.dr[nm][:, :]),
                 w=(nm,), dma="ld")
        for c in range(NCH):
            for g in range(4):
                bk = f"PS{g % 2}"
                bank = s.PS[g % 2]
                for j in range(4):
                    tt = 4 * g + j
                    s.do("tensor",
                         lambda e, c=c, tt=tt, j=j, bank=bank, TM=TM:
                         e.transpose(bank[:, 128 * j:128 * (j + 1)],
                                     TM[tt][:, 128 * c:128 * (c + 1)], s.I128[:]),
                         r=(f"TM{tt}", "ident"), w=(bk,))
                s.do("vector",
                     lambda e, c=c, g=g, bank=bank:
                     e.tensor_copy(s.X[c][:, QW * g:QW * (g + 1)], bank[:]),
                     r=(bk,), w=(f"X{c}q{g}",))
        s.barrier()

    def emit_ln_win(self, qq):
        s = self
        for c in range(NCH):
            sq = s.SC[c % 2]
            s.do("scalar",
                 lambda e, c=c, qq=qq, sq=sq:
                 e.activation(sq[:], s.X[c][:, QW * qq:QW * (qq + 1)], AF.Square),
                 r=(f"X{c}q{qq}",), w=(f"SC{c % 2}",))
            s.do("tensor",
                 lambda e, c=c, qq=qq:
                 e.matmul(s.PS[4][:], s.ALL1[:], s.X[c][:, QW * qq:QW * (qq + 1)],
                          start=(c == 0), stop=(c == NCH - 1)),
                 r=(f"X{c}q{qq}", "all1"), w=("PS4",))
            s.do("tensor",
                 lambda e, sq=sq, c=c:
                 e.matmul(s.PS[5][:], s.ALL1[:], sq[:],
                          start=(c == 0), stop=(c == NCH - 1)),
                 r=(f"SC{c % 2}", "all1"), w=("PS5",))
        s.close_group("PS4", "PS5")
        mu_s = s.MU[:, QW * qq:QW * (qq + 1)]
        rb_s = s.RBC[:, QW * qq:QW * (qq + 1)]
        s.do("scalar", lambda e, mu_s=mu_s:
             e.activation(mu_s, s.PS[4][:], AF.Identity, scale=1.0 / D),
             r=("PS4",), w=(f"MU{qq}",))
        s.do("scalar", lambda e:
             e.activation(s.SC[2][:], s.PS[5][:], AF.Identity, scale=1.0 / D),
             r=("PS5",), w=("SC2",))
        s.do("scalar", lambda e, mu_s=mu_s:
             e.activation(s.SC[3][:], mu_s, AF.Square),
             r=(f"MU{qq}",), w=("SC3",))
        s.do("vector", lambda e:
             e.tensor_tensor(s.SC[2][:], s.SC[2][:], s.SC[3][:], OP.subtract),
             r=("SC2", "SC3"), w=("SC2",))
        s.do("vector", lambda e:
             e.tensor_single_scalar(s.SC[2][:], s.SC[2][:], LN_EPS, OP.add),
             r=("SC2",), w=("SC2",))
        s.do("scalar", lambda e:
             e.activation(s.SC[3][:], s.SC[2][:], AF.Ln), r=("SC2",), w=("SC3",))
        s.do("scalar", lambda e, rb_s=rb_s:
             e.activation(rb_s, s.SC[3][:], AF.Exp, scale=-0.5),
             r=("SC3",), w=(f"RB{qq}",))

    def emit_ln(self):
        for qq in range(NQ):
            self.emit_ln_win(qq)

    def xn_into(self, dst, dkey, c, qq, eng="vector", wx=()):
        s = self
        s.do(eng, lambda e, c=c, qq=qq, dst=dst:
             e.tensor_tensor(dst, s.X[c][:, QW * qq:QW * (qq + 1)],
                             s.MU[:, QW * qq:QW * (qq + 1)], OP.subtract),
             r=(f"X{c}q{qq}", f"MU{qq}"), w=(dkey, *wx))
        s.do(eng, lambda e, qq=qq, dst=dst:
             e.tensor_tensor(dst, dst, s.RBC[:, QW * qq:QW * (qq + 1)], OP.mult),
             r=(dkey, f"RB{qq}"), w=(dkey,))

    def attn_head(self, k_lhsT, kkey, q_rhs, qkey, v_lhsT, vkey, negc, nkey,
                  xout, xkey, half):
        s = self
        hs_ctx = slice(64, 128) if half else slice(0, 64)
        hs_z = slice(0, 64) if half else slice(64, 128)
        for qq in range(NQ):
            kcmax = 4 * qq + 4
            s.pvc = getattr(s, "pvc", 0) + 1
            pvb = (2, 3, 7)[s.pvc % 3]
            pvbank = s.PS[pvb]

            def scores(kc, qq=qq):
                sb = (0, 1, 4, 5)[kc % 4]
                diag = kc >= 4 * qq
                s.do("tensor", lambda e, kc=kc, sb=sb, qq=qq, diag=diag:
                     e.matmul(s.PS[sb][:], k_lhsT(kc), q_rhs(qq),
                              start=True, stop=not diag),
                     r=(kkey, qkey), w=(f"PS{sb}",))
                if diag:
                    j = kc - 4 * qq
                    s.do("tensor", lambda e, sb=sb, j=j:
                         e.matmul(s.PS[sb][:], s.I128B[:],
                                  s.MSK[:, QW * j:QW * (j + 1)],
                                  start=False, stop=True),
                         r=("identb", "mask"), w=(f"PS{sb}",))
                eb = kc % 3
                s.do("scalar", lambda e, sb=sb, eb=eb:
                     e.activation(s.E32[eb][:], s.PS[sb][:], AF.Exp, bias=negc),
                     r=(f"PS{sb}", nkey), w=(f"E{eb}",))

            def pv(kc, qq=qq, kcmax=kcmax, pvbank=pvbank, pvb=pvb):
                eb = kc % 3
                s.do("tensor", lambda e, kc=kc, eb=eb, pvbank=pvbank, kcmax=kcmax:
                     e.matmul(pvbank[:], v_lhsT(kc), s.E32[eb][:],
                              start=(kc == 0), stop=(kc == kcmax - 1)),
                     r=(vkey, f"E{eb}"), w=(f"PS{pvb}",))

            def epilogue(qq=qq, pvb=pvb, pvbank=pvbank,
                         hs_ctx=hs_ctx, hs_z=hs_z, xout=xout, xkey=xkey):
                s.do("tensor", lambda e:
                     e.matmul(s.PS[6][hs_ctx, :], s.I128[hs_z, hs_z],
                              s.RBF[hs_z, :], start=True, stop=True),
                     r=("RBFz", "ident"), w=("PS6",))
                s.do("vector", lambda e:
                     e.tensor_copy(s.RBF[hs_ctx, :], s.PS[6][hs_ctx, :]),
                     r=("PS6",), w=("RBFc",))
                s.do("vector", lambda e:
                     e.tensor_tensor(s.TMPE[hs_ctx, :], pvbank[hs_ctx, :],
                                     s.RBF[hs_ctx, :], OP.mult),
                     r=(f"PS{pvb}", "RBFc"), w=("TMPK",))
                s.do("vector", lambda e:
                     e.tensor_tensor(xout(qq), xout(qq), s.TMPE[hs_ctx, :],
                                     OP.add),
                     r=("TMPK", xkey(qq)), w=(xkey(qq),))

            for kc in range(kcmax):
                scores(kc)
                if kc >= 2:
                    pv(kc - 2)
                if kc == 3:
                    s.flush_ep()
            if kcmax >= 2:
                pv(kcmax - 2)
            pv(kcmax - 1)
            s.close_group(f"PS{pvb}")
            s.do("scalar", lambda e, pvbank=pvbank:
                 e.activation(s.TMPE[hs_z, :], pvbank[hs_z, :], AF.Ln),
                 r=(f"PS{pvb}",), w=("TMPK",))
            s.do("scalar", lambda e:
                 e.activation(s.RBF[hs_z, :], s.TMPE[hs_z, :], AF.Exp,
                              scale=-1.0),
                 r=("TMPK",), w=("RBFz",))
            s.pend_ep = epilogue

    def head_bound(self, qt, qkey, kt, kkey, col):
        s = self
        s.do("vector", lambda e, qt=qt:
             e.tensor_reduce(s.SC[0][:, 0:1], qt[:, 0:T], AX.X, OP.max,
                             apply_absolute_value=True),
             r=(qkey,), w=("SC0",))
        s.do("vector", lambda e, kt=kt:
             e.tensor_reduce(s.SC[0][:, 1:2], kt[:, 0:T], AX.X, OP.max,
                             apply_absolute_value=True),
             r=(kkey,), w=("SC0",))
        s.do("scalar", lambda e:
             e.activation(s.SC[0][:, 2:4], s.SC[0][:, 0:2], AF.Square),
             r=("SC0",), w=("SC0",))
        s.do("tensor", lambda e:
             e.matmul(s.PS[6][0:128, 0:2], s.ALL1[:], s.SC[0][:, 2:4],
                      start=True, stop=True),
             r=("SC0", "all1"), w=("PS6",))
        s.do("scalar", lambda e:
             e.activation(s.SC[0][:, 4:6], s.PS[6][0:128, 0:2], AF.Sqrt),
             r=("PS6",), w=("SC0",))
        s.do("vector", lambda e:
             e.tensor_tensor(s.SC[0][:, 6:7], s.SC[0][:, 4:5], s.SC[0][:, 5:6],
                             OP.mult),
             r=("SC0",), w=("SC0",))
        s.do("vector", lambda e, col=col:
             e.tensor_scalar(s.SM[:, col:col + 1], s.SC[0][:, 6:7],
                             43.0, -1.0, OP.min, OP.mult),
             r=("SC0",), w=(f"SM{col}",))

    def emit_l0a(self):
        s = self
        WT = {}
        off = 70 * 1024
        for name, rows, cols in (("wdq", 512, HD), ("wuqT", 512, HD),
                                 ("wdkv", 1024, HD), ("wukvT", 1024, 2 * HD)):
            nch = rows // 128
            WT[name] = s.ov("w_" + name, [128, nch * cols], F32, off)
            off += nch * cols * 4
            s.loadw(name, WT[name], "w_" + name, cols=cols)
        for name in ("wqrT", "wkrT", "woT"):
            WT[name] = s.ov("w_" + name, [128, HD], F32, off)
            off += HD * 4
            for hh in range(2):
                s.do("sync", lambda e, name=name, hh=hh, WT=WT:
                     e.dma_start(out=WT[name][64 * hh:64 * (hh + 1), :],
                                 in_=s.dr[name][:, :]),
                     w=("w_" + name,), dma="ld")
        for col, name in ((0, "qnw"), (1, "knw")):
            for hh in range(2):
                s.do("sync", lambda e, name=name, col=col, hh=hh:
                     e.dma_start(out=s.SM[64 * hh:64 * (hh + 1), col:col + 1],
                                 in_=s.dr[name][0:1, :]),
                     w=(f"SM{col}",), dma="ld")
        EQT = s.ov("EQT", [128, 128], F32, 84 * 1024)
        EKT = s.ov("EKT", [128, 128], F32, 84 * 1024 + 512)
        WOE2 = s.ov("WOE2", [128, 128], F32, 84 * 1024 + 1024)
        WQR2 = s.ov("WQR2", [128, HD], F32, 84 * 1024 + 1536)
        WKR2 = s.ov("WKR2", [128, HD], F32, 84 * 1024 + 1792)
        EVT = s.ov("EVT", [64, 64], F32, 84 * 1024 + 2048)
        STG = s.ov("STG", [64, 192], F32, 84 * 1024 + 2304)

        for ch in range(4):
            s.do("tensor", lambda e, ch=ch, WT=WT:
                 e.matmul(s.PS[6][0:64, 0:64], WT["wdq"][:, HD * ch:HD * (ch + 1)],
                          WT["wuqT"][:, HD * ch:HD * (ch + 1)],
                          start=(ch == 0), stop=(ch == 3)),
                 r=("w_wdq", "w_wuqT"), w=("PS6",))
        for ch in range(8):
            s.do("tensor", lambda e, ch=ch, WT=WT:
                 e.matmul(s.PS[6][0:64, 64:128], WT["wdkv"][:, HD * ch:HD * (ch + 1)],
                          WT["wukvT"][:, 128 * ch:128 * ch + 64],
                          start=(ch == 0), stop=(ch == 7)),
                 r=("w_wdkv", "w_wukvT"))
        for ch in range(8):
            s.do("tensor", lambda e, ch=ch, WT=WT:
                 e.matmul(s.PS[6][0:64, 128:192],
                          WT["wukvT"][:, 128 * ch + 64:128 * (ch + 1)],
                          WT["wdkv"][:, HD * ch:HD * (ch + 1)],
                          start=(ch == 0), stop=(ch == 7)),
                 r=("w_wdkv", "w_wukvT"))
        s.close_group("PS6")
        s.do("vector", lambda e, STG=STG:
             e.tensor_copy(STG[:, 0:192], s.PS[6][0:64, 0:192]),
             r=("PS6",), w=("STG",))
        s.do("vector", lambda e, STG=STG, EVT=EVT:
             e.tensor_copy(EVT[:], STG[:, 128:192]), r=("STG",), w=("EVT",))
        s.do("tensor", lambda e, EVT=EVT, WT=WT:
             e.matmul(s.PS[7][0:64, 0:64], EVT[:], WT["woT"][0:64, :],
                      start=True, stop=True),
             r=("EVT", "w_woT"), w=("PS7",))
        s.do("vector", lambda e, STG=STG:
             e.tensor_copy(STG[:, 128:192], s.PS[7][0:64, 0:64]),
             r=("PS7",), w=("STG",))
        for tn, key in ((EQT, "EQT"), (EKT, "EKT"), (WOE2, "WOE2")):
            s.do("vector", lambda e, tn=tn: e.memset(tn[:], 0.0), w=(key,))
        for tn, key, sl in ((EQT, "EQT", slice(0, 64)), (EKT, "EKT", slice(64, 128)),
                            (WOE2, "WOE2", slice(128, 192))):
            s.do("vector", lambda e, tn=tn, sl=sl, STG=STG:
                 e.tensor_copy(tn[0:64, 0:64], STG[:, sl]), r=("STG",), w=(key,))
            s.do("sync", lambda e, tn=tn, sl=sl, STG=STG:
                 e.dma_start(out=tn[64:128, 64:128], in_=STG[:, sl]),
                 r=("STG",), w=(key,), dma="ld")
        s.do("vector", lambda e:
             e.tensor_tensor(s.SM[:, 2:3], s.SM[:, 0:1], s.SM[:, 1:2], OP.mult),
             r=("SM0", "SM1"), w=("SM2",))
        s.do("vector", lambda e:
             e.tensor_single_scalar(s.SM[:, 2:3], s.SM[:, 2:3], HD ** -0.5, OP.mult),
             r=("SM2",), w=("SM2",))
        s.do("vector", lambda e, WQR2=WQR2, WT=WT:
             e.tensor_scalar_mul(WQR2[:], WT["wqrT"][:, :], s.SM[:, 0:1]),
             r=("w_wqrT", "SM0"), w=("WQR2",))
        s.do("vector", lambda e, WKR2=WKR2, WT=WT:
             e.tensor_scalar_mul(WKR2[:], WT["wkrT"][:, :], HD ** -0.5),
             r=("w_wkrT",), w=("WKR2",))

        NC_ = s.ov("NC_", [128, T], F32, 0)
        QA = s.ov("QA", [128, T], F32, 8 * 1024)
        QB = s.ov("QB", [128, T], F32, 16 * 1024)
        KA = s.ov("KA", [128, T], F32, 24 * 1024)
        KB = s.ov("KB", [128, T], F32, 32 * 1024)
        VFA = s.ov("VFA", [128, T], F32, 40 * 1024)
        VFB = s.ov("VFB", [128, T], F32, 48 * 1024)
        RSBP = [s.ov("RSB", [128, QW], F32, 56 * 1024),
                s.ov("RSB2", [128, QW], F32, 60 * 1024)]
        SQSP = [s.ov("SQS", [128, QW], F32, 58 * 1024),
                s.ov("SQS2", [128, QW], F32, 62 * 1024)]
        ROWP = [s.ROWS, s.ov("ROW2", [2, QW], F32, 64 * 1024)]

        units = ((EQT, "EQT", QA, "QA", QB, "QB", False),
                 (EKT, "EKT", KA, "KA", KB, "KB", True))

        def stA(i, NC_=NC_, SQSP=SQSP):
            (WTm, wkey, dA, kA_, dB, kB_, is_k) = units[i % 2]
            qq = i // 2
            bk = f"PS{4 + (i % 2)}"
            bank = s.PS[4 + (i % 2)]
            sq, sqk = SQSP[i % 2], f"SQSp{i % 2}"
            s.do("tensor", lambda e, WTm=WTm, bank=bank, qq=qq:
                 e.matmul(bank[:], WTm[:], NC_[:, QW * qq:QW * (qq + 1)],
                          start=True, stop=True),
                 r=(wkey, "NC"), w=(bk,))
            s.do("vector", lambda e, qq=qq, dA=dA, bank=bank:
                 e.tensor_copy(dA[0:64, QW * qq:QW * (qq + 1)], bank[0:64, :]),
                 r=(bk,), w=(kA_,))
            s.do("vector", lambda e, qq=qq, dB=dB, bank=bank:
                 e.tensor_copy(dB[64:128, QW * qq:QW * (qq + 1)],
                               bank[64:128, :]),
                 r=(bk,), w=(kB_,))
            s.do("scalar", lambda e, qq=qq, dA=dA, sq=sq:
                 e.activation(sq[0:64, :], dA[0:64, QW * qq:QW * (qq + 1)],
                              AF.Square),
                 r=(kA_,), w=(sqk,))
            s.do("scalar", lambda e, qq=qq, dB=dB, sq=sq:
                 e.activation(sq[64:128, :], dB[64:128, QW * qq:QW * (qq + 1)],
                              AF.Square),
                 r=(kB_,), w=(sqk,))

        def stB(i, SQSP=SQSP, ROWP=ROWP):
            pb = f"PS{6 + (i % 2)}"
            pbank = s.PS[6 + (i % 2)]
            sq, sqk = SQSP[i % 2], f"SQSp{i % 2}"
            rw, rwk = ROWP[i % 2], f"ROWSp{i % 2}"
            s.do("tensor", lambda e, pbank=pbank, sq=sq:
                 e.matmul(pbank[0:2, :], s.ONES2B[:], sq[:],
                          start=True, stop=True),
                 r=(sqk, "ones2b"), w=(pb,))
            s.do("vector", lambda e, pbank=pbank, rw=rw:
                 e.tensor_scalar(rw[:], pbank[0:2, :], 1.0 / HD, RMS_EPS,
                                 OP.mult, OP.add),
                 r=(pb,), w=(rwk,))
            s.do("scalar", lambda e, sq=sq, rw=rw:
                 e.activation(sq[0:2, :], rw[:], AF.Ln), r=(rwk,), w=(sqk,))
            s.do("scalar", lambda e, sq=sq, rw=rw:
                 e.activation(rw[:], sq[0:2, :], AF.Exp, scale=-0.5),
                 r=(sqk,), w=(rwk,))

        def stC(i, ROWP=ROWP, RSBP=RSBP):
            (WTm, wkey, dA, kA_, dB, kB_, is_k) = units[i % 2]
            qq = i // 2
            pb = f"PS{6 + (i % 2)}"
            pbank = s.PS[6 + (i % 2)]
            rw, rwk = ROWP[i % 2], f"ROWSp{i % 2}"
            rb, rbk = RSBP[i % 2], f"RSBp{i % 2}"
            s.do("tensor", lambda e, pbank=pbank, rw=rw:
                 e.matmul(pbank[:], s.ONESB2[:], rw[:], start=True, stop=True),
                 r=(rwk, "onesb2"), w=(pb,))
            s.do("vector", lambda e, pbank=pbank, rb=rb:
                 e.tensor_copy(rb[:], pbank[:]), r=(pb,), w=(rbk,))
            for half, (dd, dk) in enumerate(((dA, kA_), (dB, kB_))):
                hcn = slice(0, 64) if half == 0 else slice(64, 128)
                if is_k:
                    s.do("vector", lambda e, qq=qq, dd=dd, hcn=hcn, rb=rb:
                         e.scalar_tensor_tensor(
                             dd[hcn, QW * qq:QW * (qq + 1)],
                             dd[hcn, QW * qq:QW * (qq + 1)],
                             s.SM[hcn, 2:3], rb[hcn, :], OP.mult, OP.mult),
                         r=(dk, "SM2", rbk), w=(dk,))
                else:
                    s.do("vector", lambda e, qq=qq, dd=dd, hcn=hcn, rb=rb:
                         e.tensor_tensor(
                             dd[hcn, QW * qq:QW * (qq + 1)],
                             dd[hcn, QW * qq:QW * (qq + 1)],
                             rb[hcn, :], OP.mult),
                         r=(dk, rbk), w=(dk,))

        def stD(i, NC_=NC_, WQR2=WQR2, WKR2=WKR2):
            (WTm, wkey, dA, kA_, dB, kB_, is_k) = units[i % 2]
            qq = i // 2
            for half, (dd, dk) in enumerate(((dA, kA_), (dB, kB_))):
                hcn = slice(0, 64) if half == 0 else slice(64, 128)
                hr = slice(64, 128) if half == 0 else slice(0, 64)
                bk = f"PS{half}"
                bank = s.PS[half]
                if is_k:
                    s.do("tensor", lambda e, qq=qq, hcn=hcn, hr=hr, bank=bank:
                         e.matmul(bank[hr, :], WKR2[hcn, :],
                                  NC_[hcn, QW * qq:QW * (qq + 1)],
                                  start=True, stop=True),
                         r=("WKR2", "NC"), w=(bk,))
                else:
                    s.do("tensor", lambda e, qq=qq, dd=dd, hcn=hcn, hr=hr,
                         bank=bank:
                         e.matmul(bank[hr, :], WQR2[hcn, :],
                                  dd[hcn, QW * qq:QW * (qq + 1)],
                                  start=True, stop=True),
                         r=("WQR2", dk), w=(bk,))
                s.do("vector", lambda e, qq=qq, dd=dd, hr=hr, bank=bank:
                     e.tensor_copy(dd[hr, QW * qq:QW * (qq + 1)], bank[hr, :]),
                     r=(bk,), w=(dk,))

        def veff(kc, NC_=NC_, WOE2=WOE2, VFA=VFA, VFB=VFB):
            bk = f"PS{2 + (kc % 2)}"
            bank = s.PS[2 + (kc % 2)]
            s.do("tensor", lambda e, kc=kc, bank=bank:
                 e.matmul(bank[:, 0:128], NC_[:, 128 * kc:128 * (kc + 1)],
                          WOE2[:], start=True, stop=True),
                 r=("NC", "WOE2"), w=(bk,))
            s.do("vector", lambda e, kc=kc, bank=bank:
                 e.tensor_copy(VFA[:, 128 * kc:128 * kc + 64], bank[:, 0:64]),
                 r=(bk,), w=("VFA",))
            s.do("vector", lambda e, kc=kc, bank=bank:
                 e.tensor_copy(VFB[:, 128 * kc + 64:128 * (kc + 1)],
                               bank[:, 64:128]),
                 r=(bk,), w=("VFB",))

        for c in range(NCH):
            s.flush_ep()
            if c == 0:
                s.do("vector", lambda e, VFA=VFA: e.memset(VFA[:], 1.0),
                     w=("VFA",))
                s.do("vector", lambda e, VFB=VFB: e.memset(VFB[:], 1.0),
                     w=("VFB",))
            s.xn_into(NC_[:, 0:QW], "NC", c, 0, eng="gpsimd")
            s.xn_into(NC_[:, QW:2 * QW], "NC", c, 1, eng="gpsimd")
            for i in range(11):
                if i in (2, 4):
                    w = i // 2 + 1
                    s.xn_into(NC_[:, QW * w:QW * (w + 1)], "NC", c, w,
                              eng="gpsimd")
                if i < 8:
                    stA(i)
                    veff(2 * i)
                    veff(2 * i + 1)
                if 1 <= i <= 8:
                    stB(i - 1)
                if 2 <= i <= 9:
                    stC(i - 2)
                if i >= 3:
                    stD(i - 3)
            s.head_bound(QA, "QA", KA, "KA", 3 + 2 * c)
            s.head_bound(QB, "QB", KB, "KB", 3 + 2 * c + 1)
            for half, (qt, qk_, kt, kk_, vt, vk_) in enumerate(
                    ((QA, "QA", KA, "KA", VFA, "VFA"),
                     (QB, "QB", KB, "KB", VFB, "VFB"))):
                h = 2 * c + half
                negc = s.SM[:, 3 + h:4 + h]
                xs = s.X[c]
                hs0 = 64 * half
                s.attn_head(
                    k_lhsT=lambda kc, kt=kt: kt[:, 128 * kc:128 * (kc + 1)],
                    kkey=kk_,
                    q_rhs=lambda qq, qt=qt: qt[:, QW * qq:QW * (qq + 1)],
                    qkey=qk_,
                    v_lhsT=lambda kc, vt=vt: vt[:, 128 * kc:128 * (kc + 1)],
                    vkey=vk_,
                    negc=negc, nkey=f"SM{3 + h}",
                    xout=lambda qq, xs=xs, hs0=hs0:
                        xs[hs0:hs0 + 64, QW * qq:QW * (qq + 1)],
                    xkey=lambda qq, c=c: f"X{c}q{qq}",
                    half=half)

    def emit_l0f(self):
        s = self
        WQ = s.ov("WQB", [128, 8 * 1024], BF16, 0)
        AQW = s.ov("AQW", [128, 8 * QW], BF16, 16 * 1024)
        WPB = s.ov("WPB", [128, T], F32, 24 * 1024)
        WTR = [s.ov(f"WTR{i}", [128, 1024], F32, 48 * 1024 + i * 4096)
               for i in range(2)]
        AMX = s.ov("AMX", [128, 64], F32, 56 * 1024)
        SB32 = s.ov("SB32", [32, 128], F32, 57 * 1024)
        RAB = s.ov("RAB", [128, QW], F32, 58 * 1024)
        ASB = s.ov("ASB", [128, QW], F32, 60 * 1024)
        ROWR = s.ov("ROWR", [1, 4096], F32, 64 * 1024)
        WPA = s.ov("WPA", [128, T], F32, 82 * 1024)
        U_ = s.ov("U_", [128, 2 * QW], F32, 24 * 1024)
        R_ = s.ov("R_", [128, 2 * QW], F32, 28 * 1024)
        B_ = s.ov("B_", [128, 2 * QW], F32, 94 * 1024)
        XN_ = s.ov("XN_", [128, 2 * QW], F32, 98 * 1024)
        s.KEF = [s.ov(f"KEF{i}", [128, 1024], F32, 32 * 1024 + i * 4096)
                 for i in range(2)]
        s.VEF = [s.ov(f"VEF{i}", [128, 1024], F32, 40 * 1024 + i * 4096)
                 for i in range(2)]
        AT = [s.ov(f"AT{i}", [128, 256], F32, 80 * 1024 + i * 1024)
              for i in range(2)]
        s.do("vector", lambda e: e.memset(s.SM[:, 30:31], 128.5), w=("SM30",))

        KEF, VEF = s.KEF, s.VEF
        s.loadw("wuq1", WPA, "WPA", cols=256)
        s.loadw("wuk1", WPB, "WPB", cols=256)
        for m in range(2):
            for ch in range(8):
                s.do("tensor", lambda e, m=m, ch=ch, WPA=WPA, WPB=WPB:
                     e.matmul(s.PS[2][0:128, 0:256],
                              WPA[:, 256 * ch + 128 * m:256 * ch + 128 * (m + 1)],
                              WPB[:, 256 * ch:256 * (ch + 1)],
                              start=(ch == 0), stop=(ch == 7)),
                     r=("WPA", "WPB"), w=("PS2",))
            s.close_group("PS2")
            s.do("vector", lambda e, m=m, AT=AT:
                 e.tensor_copy(AT[m][:], s.PS[2][0:128, 0:256]),
                 r=("PS2",), w=(f"AT{m}",))
        s.loadw("wdq1", WPA, "WPA", cols=1024)
        for m in range(2):
            for nn in range(2):
                for ch in range(2):
                    s.do("tensor", lambda e, m=m, nn=nn, ch=ch, AT=AT, WPA=WPA:
                         e.matmul(s.PS[2 + nn][:],
                                  AT[ch][:, 128 * m:128 * (m + 1)],
                                  WPA[:, 1024 * ch + 512 * nn:1024 * ch + 512 * (nn + 1)],
                                  start=(ch == 0), stop=(ch == 1)),
                         r=(f"AT{ch}", "WPA"), w=(f"PS{2 + nn}",))
                s.close_group(f"PS{2 + nn}")
            for nn in range(2):
                s.do("vector", lambda e, m=m, nn=nn, KEF=KEF:
                     e.tensor_copy(KEF[m][:, 512 * nn:512 * (nn + 1)],
                                   s.PS[2 + nn][:]),
                     r=(f"PS{2 + nn}",), w=(f"KEF{m}",))
        s.loadw("wuv1", WPB, "WPB", cols=256)
        for nn in range(2):
            for g4 in range(2):
                for ci in range(4):
                    ch = 4 * g4 + ci
                    s.do("sync", lambda e, ch=ch, ci=ci, nn=nn, WPA=WPA:
                         e.dma_start(out=WPA[:, 512 * ci:512 * (ci + 1)],
                                     in_=s.dr["wo1T"][128 * ch:128 * (ch + 1),
                                                      512 * nn:512 * (nn + 1)]),
                         w=("WPA",), dma="ld")
                for m in range(2):
                    for ci in range(4):
                        ch = 4 * g4 + ci
                        s.do("tensor", lambda e, m=m, ci=ci, ch=ch, WPA=WPA,
                             WPB=WPB:
                             e.matmul(s.PS[2 + m][:],
                                      WPB[:, 256 * ch + 128 * m:256 * ch + 128 * (m + 1)],
                                      WPA[:, 512 * ci:512 * (ci + 1)],
                                      start=(ch == 0), stop=(ch == 7)),
                             r=("WPB", "WPA"),
                             w=(f"PS{2 + m}",))
            s.close_group("PS2", "PS3")
            for m in range(2):
                s.do("vector", lambda e, m=m, nn=nn, VEF=VEF:
                     e.tensor_copy(VEF[m][:, 512 * nn:512 * (nn + 1)],
                                   s.PS[2 + m][:]),
                     r=(f"PS{2 + m}",), w=(f"VEF{m}",))
        for ch in range(NCH):
            wt = WTR[ch % 2]
            s.loadw("w0T", wt, f"WTR{ch % 2}", rowsl=slice(128 * ch, 128 * (ch + 1)))
            s.do("vector", lambda e, ch=ch, wt=wt:
                 e.tensor_reduce(s.SC[0][:, ch:ch + 1], wt[:], AX.X, OP.add,
                                 apply_absolute_value=True),
                 r=(f"WTR{ch % 2}",), w=("SC0",))
        s.do("tensor", lambda e:
             e.matmul(s.PS[6][0:128, 0:8], s.ALL1[:], s.SC[0][:, 0:8],
                      start=True, stop=True),
             r=("SC0", "all1"), w=("PS6",))
        s.do("vector", lambda e:
             e.tensor_reduce(s.SC[1][:, 0:1], s.PS[6][0:128, 0:8], AX.X, OP.add),
             r=("PS6",), w=("SC1",))
        s.do("vector", lambda e:
             e.tensor_scalar(s.SC[1][:, 1:2], s.SC[1][:, 0:1],
                             1.0 / (1024.0 * 1024.0), 1e-8, OP.mult, OP.add),
             r=("SC1",), w=("SC1",))
        s.do("vector", lambda e:
             e.reciprocal(s.SM[:, 20:21], s.SC[1][:, 1:2]),
             r=("SC1",), w=("SM20",))
        s.do("vector", lambda e:
             e.tensor_scalar_mul(s.SM[:, 21:22], s.SC[1][:, 1:2], 1.0 / 127.0),
             r=("SC1",), w=("SM21",))
        for ch in range(NCH):
            wt = WTR[ch % 2]
            s.loadw("w0T", wt, f"WTR{ch % 2}", rowsl=slice(128 * ch, 128 * (ch + 1)))
            for hf in range(2):
                sl0 = 512 * hf
                s.do("vector", lambda e, wt=wt, sl0=sl0:
                     e.tensor_scalar(s.SC[0][:], wt[:, sl0:sl0 + 512],
                                     s.SM[:, 20:21], 0.5, OP.mult, OP.is_ge),
                     r=(f"WTR{ch % 2}", "SM20"), w=("SC0",))
                s.do("vector", lambda e, wt=wt, sl0=sl0:
                     e.tensor_scalar(s.SC[1][:], wt[:, sl0:sl0 + 512],
                                     s.SM[:, 20:21], -0.5, OP.mult, OP.is_le),
                     r=(f"WTR{ch % 2}", "SM20"), w=("SC1",))
                s.do("vector", lambda e, ch=ch, sl0=sl0, WQ=WQ:
                     e.tensor_tensor(WQ[:, 1024 * ch + sl0:1024 * ch + sl0 + 512],
                                     s.SC[0][:], s.SC[1][:], OP.subtract),
                     r=("SC0", "SC1"), w=("WQ",))
        for tt in range(16):
            ab = 4 + 2 * (tt % 2)
            for grp in range(2):
                bk = f"PS{ab + grp}"
                bank = s.PS[ab + grp]
                dst = XN_ if grp == 0 else U_
                eng = "vector" if grp == 0 else "gpsimd"
                for ci in range(4):
                    c = 4 * grp + ci
                    xk = f"XN{grp}{ci}"
                    xwx = ("WPB",) if grp == 1 and tt == 0 else ()
                    s.do(eng, lambda e, c=c, tt=tt, dst=dst, ci=ci:
                         e.tensor_tensor(dst[:, 128 * ci:128 * (ci + 1)],
                                         s.X[c][:, 128 * tt:128 * (tt + 1)],
                                         s.MU[:, 128 * tt:128 * (tt + 1)],
                                         OP.subtract),
                         r=(f"X{c}q{tt // 4}", f"MU{tt // 4}"), w=(xk,) + xwx)
                    s.do(eng, lambda e, tt=tt, dst=dst, ci=ci:
                         e.tensor_tensor(dst[:, 128 * ci:128 * (ci + 1)],
                                         dst[:, 128 * ci:128 * (ci + 1)],
                                         s.RBC[:, 128 * tt:128 * (tt + 1)],
                                         OP.mult),
                         r=(xk, f"RB{tt // 4}"), w=(xk,))
                    s.do("tensor", lambda e, ci=ci, bank=bank, dst=dst:
                         e.transpose(bank[:, 128 * ci:128 * (ci + 1)],
                                     dst[:, 128 * ci:128 * (ci + 1)], s.I128[:]),
                         r=(xk, "ident"), w=(bk,))
                s.close_group(bk)
            s.do("vector", lambda e, tt=tt, AMX=AMX, ab=ab:
                 e.tensor_reduce(AMX[:, tt:tt + 1], s.PS[ab][:], AX.X, OP.max,
                                 apply_absolute_value=True),
                 r=(f"PS{ab}",), w=("AMXa",))
            s.do("vector", lambda e, tt=tt, AMX=AMX, ab=ab:
                 e.tensor_reduce(AMX[:, 32 + tt:33 + tt], s.PS[ab + 1][:], AX.X,
                                 OP.max, apply_absolute_value=True),
                 r=(f"PS{ab + 1}",), w=("AMXb",))
            s.do("vector", lambda e, tt=tt, AMX=AMX:
                 e.tensor_tensor(AMX[:, tt:tt + 1], AMX[:, tt:tt + 1],
                                 AMX[:, 32 + tt:33 + tt], OP.max),
                 r=("AMXa", "AMXb"), w=("AMXa",))
        s.do("vector", lambda e, AMX=AMX:
             e.tensor_single_scalar(AMX[:, 0:16], AMX[:, 0:16], 1e-8, OP.add),
             r=("AMXa",), w=("AMXa",))
        s.do("vector", lambda e, AMX=AMX:
             e.tensor_scalar_mul(AMX[:, 16:32], AMX[:, 0:16], s.SM[:, 21:22]),
             r=("AMXa", "SM21"), w=("AMXa",))
        s.do("vector", lambda e, AMX=AMX:
             e.reciprocal(AMX[:, 0:16], AMX[:, 0:16]), r=("AMXa",), w=("AMXa",))
        s.do("vector", lambda e, AMX=AMX:
             e.tensor_single_scalar(AMX[:, 0:16], AMX[:, 0:16], 127.0, OP.mult),
             r=("AMXa",), w=("AMXa",))
        s.do("tensor", lambda e, AMX=AMX:
             e.transpose(s.PS[6][0:32, 0:128], AMX[:, 0:32], s.I128[:]),
             r=("AMXa", "ident"), w=("PS6",))
        s.do("vector", lambda e, SB32=SB32:
             e.tensor_copy(SB32[:], s.PS[6][0:32, 0:128]), r=("PS6",), w=("SB32",))
        s.do("sync", lambda e, SB32=SB32, ROWR=ROWR:
             e.dma_start(out=ROWR[0:1, :].rearrange("p (r j) -> p r j", r=32),
                         in_=SB32[:, :]),
             r=("SB32",), w=("ROWR",), dma="ld")
        RABP = [RAB, s.ov("RAB2", [128, QW], F32, 102 * 1024)]

        def rab_bcast(qq):
            rab, rk = RABP[qq % 2], f"RABp{qq % 2}"
            s.do("tensor", lambda e, qq=qq, ROWR=ROWR:
                 e.matmul(s.PS[6][:], s.ONE1[0:1, :],
                          ROWR[0:1, QW * qq:QW * (qq + 1)], start=True, stop=True),
                 r=("ROWR", "one1"), w=("PS6",))
            s.do("vector", lambda e, rab=rab:
                 e.tensor_copy(rab[:], s.PS[6][:]), r=("PS6",), w=(rk,))

        rab_bcast(0)
        for qq in range(NQ):
            rab, rk = RABP[qq % 2], f"RABp{qq % 2}"
            s.do("tensor", lambda e, qq=qq, ROWR=ROWR:
                 e.matmul(s.PS[6][:], s.ONE1[0:1, :],
                          ROWR[0:1, 2048 + QW * qq:2048 + QW * (qq + 1)],
                          start=True, stop=True),
                 r=("ROWR", "one1"), w=("PS6",))
            s.do("vector", lambda e, ASB=ASB:
                 e.tensor_copy(ASB[:], s.PS[6][:]), r=("PS6",), w=("ASB",))
            if qq + 1 < NQ:
                rab_bcast(qq + 1)
            for cp in range(0, NCH, 2):
                wx = ("XN00", "XN01", "XN02", "XN03") if qq == 0 and cp == 0 \
                    else ()
                s.xn_into(XN_[:, 0:QW], "XN", cp, qq, eng="gpsimd", wx=wx)
                s.xn_into(XN_[:, QW:2 * QW], "XN", cp + 1, qq, eng="gpsimd")
                for j in range(2):
                    s.do("gpsimd", lambda e, j=j, XN_=XN_, rab=rab:
                         e.tensor_tensor(XN_[:, QW * j:QW * (j + 1)],
                                         XN_[:, QW * j:QW * (j + 1)],
                                         rab[:], OP.mult),
                         r=("XN", rk), w=("XN",))
                uwx = ("XN10", "XN11", "XN12", "XN13", "WPB") \
                    if qq == 0 and cp == 0 else ()
                s.do("scalar", lambda e, U_=U_, XN_=XN_:
                     e.activation(U_[:], XN_[:], AF.Identity,
                                  bias=s.SM[:, 30:31], scale=1.0),
                     r=("XN", "SM30"), w=("U",) + uwx)
                rwx = ("WPB",) if qq == 0 and cp == 0 else ()
                s.do("scalar", lambda e, R_=R_, U_=U_:
                     e.activation(R_[:], U_[:], AF.Copy),
                     r=("U",), w=("R",) + rwx)
                for k in range(7, -1, -1):
                    val = float(1 << k)
                    s.do("vector", lambda e, val=val, B_=B_, R_=R_:
                         e.tensor_scalar(B_[:], R_[:], val, val, OP.is_ge, OP.mult),
                         r=("R",), w=("B",))
                    s.do("vector", lambda e, R_=R_, B_=B_:
                         e.tensor_tensor(R_[:], R_[:], B_[:], OP.subtract),
                         r=("R", "B"), w=("R",))
                s.do("vector", lambda e, cp=cp, U_=U_, R_=R_, AQW=AQW:
                     e.scalar_tensor_tensor(
                         AQW[:, QW * cp:QW * (cp + 2)],
                         U_[:], -128.0, R_[:], OP.add, OP.subtract),
                     r=("U", "R"), w=(f"AQc{cp}", f"AQc{cp + 1}"))
                for mch in range(NCH):
                    for kc in (cp, cp + 1):
                        s.do("tensor", lambda e, mch=mch, kc=kc,
                             WQ=WQ, AQW=AQW:
                             e.matmul(s.PS[mch][:],
                                      WQ[:, 1024 * kc + 128 * mch:1024 * kc + 128 * (mch + 1)],
                                      AQW[:, QW * kc:QW * (kc + 1)],
                                      start=(kc == 0), stop=(kc == NCH - 1)),
                             r=("WQ", f"AQc{kc}"), w=(f"PS{mch}",))
            for mch in range(NCH):
                s.close_group(f"PS{mch}")
            for mch in range(NCH):
                s.do("vector", lambda e, mch=mch, ASB=ASB:
                     e.tensor_tensor(s.SC[2 + mch % 2][:], s.PS[mch][:], ASB[:],
                                     OP.mult),
                     r=(f"PS{mch}", "ASB"), w=(f"SC{2 + mch % 2}",))
                s.do("gpsimd", lambda e, mch=mch, qq=qq:
                     e.tensor_tensor(s.X[mch][:, QW * qq:QW * (qq + 1)],
                                     s.X[mch][:, QW * qq:QW * (qq + 1)],
                                     s.SC[2 + mch % 2][:], OP.add),
                     r=(f"SC{2 + mch % 2}", f"X{mch}q{qq}"), w=(f"X{mch}q{qq}",))

    def emit_l1a(self):
        s = self
        CKV = [s.ov(f"CKV{i}", [128, T], F32, i * 8192) for i in range(2)]
        UU = [s.ov(f"UU{i}", [128, T], F32, 16 * 1024 + i * 8192) for i in range(2)]
        KEF, VEF = s.KEF, s.VEF
        WA = [s.ov(f"L1W{i}", [128, T], F32, 48 * 1024 + i * 8192) for i in range(4)]
        WUQT = s.ov("WUQT", [128, T], F32, 94 * 1024)

        s.loadw("wdq1T", WA[0], "WA0", cols=256)
        s.loadw("wdkv1T", WA[1], "WA1", cols=256)
        for qq in range(NQ):
            ub = 2 if qq % 2 == 0 else 0
            cb = 4 if qq % 2 == 0 else 6
            for c in range(NCH):
                s.xn_into(s.SC[0][:], "SC0", c, qq)
                for m in range(2):
                    s.do("tensor", lambda e, c=c, m=m, ub=ub:
                         e.matmul(s.PS[ub + m][:],
                                  WA[0][:, 256 * c + 128 * m:256 * c + 128 * (m + 1)],
                                  s.SC[0][:], start=(c == 0), stop=(c == NCH - 1)),
                         r=("WA0", "SC0"), w=(f"PS{ub + m}",))
                    s.do("tensor", lambda e, c=c, m=m, cb=cb:
                         e.matmul(s.PS[cb + m][:],
                                  WA[1][:, 256 * c + 128 * m:256 * c + 128 * (m + 1)],
                                  s.SC[0][:], start=(c == 0), stop=(c == NCH - 1)),
                         r=("WA1", "SC0"), w=(f"PS{cb + m}",))
            s.close_group(f"PS{ub}", f"PS{ub + 1}", f"PS{cb}", f"PS{cb + 1}")
            for m in range(2):
                s.do("vector", lambda e, m=m, qq=qq, ub=ub:
                     e.tensor_copy(UU[m][:, QW * qq:QW * (qq + 1)], s.PS[ub + m][:]),
                     r=(f"PS{ub + m}",), w=(f"UU{m}",))
                s.do("vector", lambda e, m=m, qq=qq, cb=cb:
                     e.tensor_copy(CKV[m][:, QW * qq:QW * (qq + 1)], s.PS[cb + m][:]),
                     r=(f"PS{cb + m}",), w=(f"CKV{m}",))
        s.loadw("wuq1T", WUQT, "WUQT", cols=1024)
        for m in range(2):
            s.do("vector", lambda e, m=m, WUQT=WUQT:
                 e.tensor_single_scalar(WUQT[:, 1024 * m:1024 * (m + 1)],
                                        WUQT[:, 1024 * m:1024 * (m + 1)],
                                        0.125, OP.mult),
                 r=("WUQT",), w=("WUQT",))
        QP = s.ov("QP", [128, T], F32, 48 * 1024)
        KHD = s.ov("KHD", [128, T], F32, 56 * 1024)
        VFA = s.ov("VFA1", [128, T], F32, 64 * 1024)
        VFB = s.ov("VFB1", [128, T], F32, 72 * 1024)
        s.do("vector", lambda e, VFA=VFA: e.memset(VFA[:], 1.0), w=("WA2",))
        s.do("vector", lambda e, VFB=VFB: e.memset(VFB[:], 1.0), w=("WA3",))
        for c in range(NCH):
            s.flush_ep()
            if c == 2:
                s.emit_l1f_prologue()
            for qq in range(NQ):
                bk = f"PS{2 + (qq % 4)}"
                bank = s.PS[2 + (qq % 4)]
                for m in range(2):
                    s.do("tensor", lambda e, c=c, qq=qq, m=m, bank=bank, WUQT=WUQT,
                         UU=UU:
                         e.matmul(bank[:],
                                  WUQT[:, 1024 * m + 128 * c:1024 * m + 128 * (c + 1)],
                                  UU[m][:, QW * qq:QW * (qq + 1)],
                                  start=(m == 0), stop=(m == 1)),
                         r=("WUQT", f"UU{m}"), w=(bk, "WA0"))
                s.close_group(bk)
                s.do("vector", lambda e, qq=qq, bank=bank, QP=QP:
                     e.tensor_copy(QP[:, QW * qq:QW * (qq + 1)], bank[:]),
                     r=(bk,), w=("WA0",))
            for nn in range(NQ):
                bk = f"PS{2 + (nn % 4)}"
                bank = s.PS[2 + (nn % 4)]
                for m in range(2):
                    s.do("tensor", lambda e, c=c, nn=nn, m=m, bank=bank, KEF=KEF,
                         CKV=CKV:
                         e.matmul(bank[:], KEF[m][:, 128 * c:128 * (c + 1)],
                                  CKV[m][:, QW * nn:QW * (nn + 1)],
                                  start=(m == 0), stop=(m == 1)),
                         r=(f"KEF{m}", f"CKV{m}"), w=(bk, "WA1"))
                s.close_group(bk)
                s.do("vector", lambda e, nn=nn, bank=bank, KHD=KHD:
                     e.tensor_copy(KHD[:, QW * nn:QW * (nn + 1)], bank[:]),
                     r=(bk,), w=("WA1",))
            for kc in range(NKC):
                bk = f"PS{2 + (kc % 4)}"
                bank = s.PS[2 + (kc % 4)]
                for m in range(2):
                    s.do("tensor", lambda e, kc=kc, m=m, bank=bank, CKV=CKV,
                         VEF=VEF, c=c:
                         e.matmul(bank[:, 0:128],
                                  CKV[m][:, 128 * kc:128 * (kc + 1)],
                                  VEF[m][:, 128 * c:128 * c + 128],
                                  start=(m == 0), stop=(m == 1)),
                         r=(f"CKV{m}", f"VEF{m}"), w=(bk,))
                s.close_group(bk)
                s.do("vector", lambda e, kc=kc, bank=bank, VFA=VFA:
                     e.tensor_copy(VFA[:, 128 * kc:128 * kc + 64],
                                   bank[:, 0:64]),
                     r=(bk,), w=("WA2",))
                s.do("vector", lambda e, kc=kc, bank=bank, VFB=VFB:
                     e.tensor_copy(VFB[:, 128 * kc + 64:128 * kc + 128],
                                   bank[:, 64:128]),
                     r=(bk,), w=("WA3",))
            for half, (VF, vk_) in enumerate(((VFA, "WA2"), (VFB, "WA3"))):
                h = 2 * c + half
                hs0 = 64 * half
                s.head_bound(QP, "WA0", KHD, "WA1", 20 + h)
                negc = s.SM[:, 20 + h:21 + h]
                xs = s.X[c]
                s.attn_head(
                    k_lhsT=lambda kc, KHD=KHD, hs0=hs0:
                        KHD[hs0:hs0 + 64, 128 * kc:128 * (kc + 1)],
                    kkey="WA1",
                    q_rhs=lambda qq, QP=QP, hs0=hs0:
                        QP[hs0:hs0 + 64, QW * qq:QW * (qq + 1)],
                    qkey="WA0",
                    v_lhsT=lambda kc, VF=VF: VF[:, 128 * kc:128 * (kc + 1)],
                    vkey=vk_,
                    negc=negc, nkey=f"SM{20 + h}",
                    xout=lambda qq, xs=xs, hs0=hs0:
                        xs[hs0:hs0 + 64, QW * qq:QW * (qq + 1)],
                    xkey=lambda qq, c=c: f"X{c}q{qq}",
                    half=half)
        s.flush_ep()
        for qq in range(NQ):
            s.emit_ln_win(qq)
        W1F = [s.ov(f"W1F{i}", [128, 8 * FCW], F32, 80 * 1024 + i * 4096)
               for i in range(2)]
        W1H0 = s.ov("W1H0", [128, 8 * FCW], BF16, 88 * 1024)
        W1L0 = s.ov("W1L0", [128, 8 * FCW], BF16, 90 * 1024)
        for fc in range(2):
            s.do("sync", lambda e, fc=fc, W1F=W1F:
                 e.dma_start(out=W1F[fc][:],
                             in_=s.dr["w1p"][128 * fc:128 * (fc + 1), :]),
                 w=(f"W1F{fc}",), dma="ld")
        s.do("vector", lambda e, W1F=W1F, W1H0=W1H0:
             e.tensor_copy(W1H0[:], W1F[0][:]), r=("W1F0",), w=("W1H0",))
        s.do("vector", lambda e, W1F=W1F, W1H0=W1H0, W1L0=W1L0:
             e.tensor_tensor(W1L0[:], W1F[0][:], W1H0[:], OP.subtract),
             r=("W1F0", "W1H0"), w=("W1L0", "mask"))

    def emit_l1f_prologue(self):
        s = self
        VEC = s.VEC = s.sb("VECL", [128, 192], F32)
        TR32 = s.sb("TR32", [32, 128], F32)

        def col_vec(name, dst_c0, ncols, key):
            nch = ncols // 128
            s.do("sync", lambda e, name=name, nch=nch:
                 e.dma_start(out=TR32[0:nch, :], in_=s.dr[name][0:nch, :]),
                 w=("TR32",), dma="ld")
            s.do("tensor", lambda e, nch=nch:
                 e.transpose(s.PS[7][0:128, 0:nch], TR32[0:nch, :],
                             s.I128[0:nch, 0:nch]),
                 r=("TR32", "ident"), w=("PS7",))
            s.do("vector", lambda e, dst_c0=dst_c0, nch=nch:
                 e.tensor_copy(VEC[:, dst_c0:dst_c0 + nch], s.PS[7][0:128, 0:nch]),
                 r=("PS7",), w=(key,))

        col_vec("b1", 64, DFF, "B1C")
        col_vec("b2", 96, D, "B2C")
        col_vec("qwr", 104, DFF, "QWR")
        col_vec("th1", 0, DFF, "THR")
        col_vec("qwi", 32, DFF, "QWI")
        SCR = s.SC[1]

        def rred(c0, key):
            s.do("vector", lambda e, c0=c0:
                 e.tensor_scalar(SCR[:, 0:32], VEC[:, c0:c0 + 32],
                                 1.0 / (2 * math.pi), 8.5, OP.mult, OP.add),
                 r=(key,), w=("RRS",))
            s.do("vector", lambda e:
                 e.tensor_copy(SCR[:, 32:64], SCR[:, 0:32]),
                 r=("RRS",), w=("RRS",))
            for k in (8.0, 4.0, 2.0, 1.0):
                s.do("vector", lambda e, k=k:
                     e.tensor_scalar(SCR[:, 64:96], SCR[:, 32:64], k, k,
                                     OP.is_ge, OP.mult),
                     r=("RRS",), w=("RRS",))
                s.do("vector", lambda e:
                     e.tensor_tensor(SCR[:, 32:64], SCR[:, 32:64],
                                     SCR[:, 64:96], OP.subtract),
                     r=("RRS",), w=("RRS",))
            s.do("vector", lambda e:
                 e.scalar_tensor_tensor(SCR[:, 64:96], SCR[:, 0:32], -8.0,
                                        SCR[:, 32:64], OP.add, OP.subtract),
                 r=("RRS",), w=("RRS",))
            s.do("vector", lambda e, c0=c0:
                 e.scalar_tensor_tensor(VEC[:, c0:c0 + 32], SCR[:, 64:96],
                                        -TPI_HI, VEC[:, c0:c0 + 32],
                                        OP.mult, OP.add),
                 r=("RRS", key), w=(key,))
            s.do("vector", lambda e, c0=c0:
                 e.scalar_tensor_tensor(VEC[:, c0:c0 + 32], SCR[:, 64:96],
                                        -TPI_LO, VEC[:, c0:c0 + 32],
                                        OP.mult, OP.add),
                 r=("RRS", key), w=(key,))

        s.do("vector", lambda e:
             e.tensor_copy(SCR[:, 96:128], VEC[:, 0:32]), r=("THR",), w=("TSIN",))
        s.do("vector", lambda e:
             e.tensor_copy(VEC[:, 136:168], SCR[:, 96:128]), r=("TSIN",), w=("TS2",))
        rred(136, "TS2")
        s.do("scalar", lambda e:
             e.activation(s.SC[2][:, 0:32], VEC[:, 136:168], AF.Sin),
             r=("TS2",), w=("SC2",))
        s.do("vector", lambda e:
             e.tensor_scalar(VEC[:, 136:168], SCR[:, 96:128], 1.0, PI_F / 2.0,
                             OP.mult, OP.add),
             r=("TSIN",), w=("TS2",))
        rred(136, "TS2")
        s.do("scalar", lambda e:
             e.activation(s.SC[3][:, 0:32], VEC[:, 136:168], AF.Sin),
             r=("TS2",), w=("SC3",))
        s.do("vector", lambda e:
             e.tensor_tensor(s.SC[2][:, 32:64], s.SC[2][:, 0:32], VEC[:, 32:64],
                             OP.mult),
             r=("SC2", "QWI"), w=("SC2",))
        s.do("vector", lambda e:
             e.tensor_tensor(s.SC[3][:, 32:64], s.SC[3][:, 0:32], VEC[:, 104:136],
                             OP.mult),
             r=("SC3", "QWR"), w=("SC3",))
        s.do("vector", lambda e:
             e.tensor_tensor(s.SC[2][:, 32:64], s.SC[2][:, 32:64],
                             s.SC[3][:, 32:64], OP.add),
             r=("SC2", "SC3"), w=("SC2",))
        s.do("vector", lambda e:
             e.scalar_tensor_tensor(VEC[:, 32:64], s.SC[2][:, 32:64], 0.1,
                                    VEC[:, 32:64], OP.mult, OP.mult),
             r=("SC2", "QWI"), w=("PP",))
        s.do("vector", lambda e:
             e.scalar_tensor_tensor(VEC[:, 0:32], VEC[:, 64:96], 0.1,
                                    VEC[:, 0:32], OP.mult, OP.add),
             r=("B1C", "THR"), w=("THR",))
        rred(0, "THR")

    def emit_l1f(self):
        s = self
        XNH = s.ov("XNH", [128, 8 * QW], BF16, 0)
        XNL = s.ov("XNL", [128, 8 * QW], BF16, 8 * 1024)
        ACH = s.ov("ACH", [128, 32 * QW], BF16, 16 * 1024)
        ACL = s.ov("ACL", [128, 32 * QW], BF16, 48 * 1024)
        W1F = [s.ov(f"W1F{i}", [128, 8 * FCW], F32, 80 * 1024 + i * 4096)
               for i in range(2)]
        W1H = [s.ov(f"W1H{i}", [128, 8 * FCW], BF16, 88 * 1024 + i * 4096)
               for i in range(2)]
        W1L = [s.ov(f"W1L{i}", [128, 8 * FCW], BF16, 90 * 1024 + i * 4096)
               for i in range(2)]
        W2F = [s.ov(f"W2F{i}", [128, 1024], F32, 96 * 1024 + i * 4096)
               for i in range(2)]
        W2H, W2L = W1H, W1L
        VEC = s.VEC

        def xn_chunk(qq, c):
            s.xn_into(s.SC[0][:], "SC0", c, qq)
            s.do("vector", lambda e, c=c, XNH=XNH:
                 e.tensor_copy(XNH[:, QW * c:QW * (c + 1)], s.SC[0][:]),
                 r=("SC0",), w=("XNH",))
            s.do("vector", lambda e, c=c, XNH=XNH, XNL=XNL:
                 e.tensor_tensor(XNL[:, QW * c:QW * (c + 1)], s.SC[0][:],
                                 XNH[:, QW * c:QW * (c + 1)], OP.subtract),
                 r=("SC0", "XNH"), w=("XNL",))

        def w1_load(fc):
            w1f, w1k = W1F[fc % 2], f"W1F{fc % 2}"
            s.do("sync", lambda e, fc=fc, w1f=w1f:
                 e.dma_start(out=w1f[:],
                             in_=s.dr["w1p"][128 * fc:128 * (fc + 1), :]),
                 w=(w1k,), dma="ld")

        def w1_split(fc):
            w1f, w1k = W1F[fc % 2], f"W1F{fc % 2}"
            w1h, w1l = W1H[fc % 2], W1L[fc % 2]
            hk, lk2 = f"W1H{fc % 2}", f"W1L{fc % 2}"
            s.do("scalar", lambda e, w1f=w1f, w1h=w1h:
                 e.activation(w1h[:], w1f[:], AF.Copy), r=(w1k,), w=(hk,))
            s.do("vector", lambda e, w1f=w1f, w1h=w1h, w1l=w1l:
                 e.tensor_tensor(w1l[:], w1f[:], w1h[:], OP.subtract),
                 r=(w1k, hk), w=(lk2,))

        def w2_load(kc):
            w2f, w2k = W2F[kc % 2], f"W2F{kc % 2}"
            s.do("sync", lambda e, kc=kc, w2f=w2f:
                 e.dma_start(out=w2f[:],
                             in_=s.dr["w2"][128 * kc:128 * (kc + 1), :]),
                 w=(w2k,), dma="ld")

        def w2_split(kc):
            w2f, w2k = W2F[kc % 2], f"W2F{kc % 2}"
            w2h, w2l = W2H[kc % 2], W2L[kc % 2]
            hk, lk2 = f"W1H{kc % 2}", f"W1L{kc % 2}"
            s.do("scalar", lambda e, w2f=w2f, w2h=w2h:
                 e.activation(w2h[:, 0:1024], w2f[:], AF.Copy),
                 r=(w2k,), w=(hk,))
            s.do("vector", lambda e, w2f=w2f, w2h=w2h, w2l=w2l:
                 e.tensor_tensor(w2l[:, 0:1024], w2f[:], w2h[:, 0:1024],
                                 OP.subtract),
                 r=(w2k, hk), w=(lk2,))

        def fc_matmuls(fc):
            w1h, w1l = W1H[fc % 2], W1L[fc % 2]
            hk, lk2 = f"W1H{fc % 2}", f"W1L{fc % 2}"
            bk = f"PS{fc % 4}"
            bank = s.PS[fc % 4]
            for kc in range(8):
                for pi, (lh, lk, rh, rk) in enumerate(
                        ((w1h, hk, XNH, "XNH"),
                         (w1h, hk, XNL, "XNL"),
                         (w1l, lk2, XNH, "XNH"))):
                    s.do("tensor", lambda e, kc=kc, lh=lh, rh=rh, bank=bank,
                         pi=pi:
                         e.matmul(bank[:], lh[:, FCW * kc:FCW * (kc + 1)],
                                  rh[:, QW * kc:QW * (kc + 1)],
                                  start=(kc == 0 and pi == 0),
                                  stop=(kc == 7 and pi == 2)),
                         r=(lk, rk), w=(bk,))
            s.close_group(bk)

        def quantum(fc, qq):
            bk = f"PS{fc % 4}"
            bank = s.PS[fc % 4]
            colv = fc
            s.do("scalar", lambda e, bank=bank, colv=colv:
                 e.activation(s.SC[0][:], bank[:], AF.Identity,
                              bias=VEC[:, colv:colv + 1], scale=0.1),
                 r=(bk, "THR"), w=("SC0",))
            s.do("scalar", lambda e, bank=bank, colv=colv:
                 e.activation(s.SC[1][:], bank[:], AF.Identity,
                              bias=VEC[:, 64 + colv:65 + colv], scale=1.0),
                 r=(bk, "B1C"), w=("SC1",))
            s.do("vector", lambda e:
                 e.tensor_scalar(s.SC[2][:], s.SC[0][:], PI_F, TPI_HI,
                                 OP.is_ge, OP.mult),
                 r=("SC0",), w=("SC2",))
            s.do("vector", lambda e:
                 e.tensor_scalar(s.SC[3][:], s.SC[0][:], -PI_F, TPI_HI,
                                 OP.is_le, OP.mult),
                 r=("SC0",), w=("SC3",))
            s.do("vector", lambda e:
                 e.tensor_tensor(s.SC[0][:], s.SC[0][:], s.SC[2][:],
                                 OP.subtract),
                 r=("SC0", "SC2"), w=("SC0",))
            s.do("gpsimd", lambda e:
                 e.tensor_tensor(s.SC[0][:], s.SC[0][:], s.SC[3][:], OP.add),
                 r=("SC0", "SC3"), w=("SC0",))
            s.do("scalar", lambda e:
                 e.activation(s.SC[2][:], s.SC[0][:], AF.Sin),
                 r=("SC0",), w=("SC2",))
            s.do("vector", lambda e, colv=colv:
                 e.scalar_tensor_tensor(s.SC[0][:], s.SC[2][:],
                                        VEC[:, 32 + colv:33 + colv],
                                        s.SC[1][:], OP.mult, OP.add),
                 r=("SC2", "PP", "SC1"), w=("SC0",))
            s.do("scalar", lambda e, fc=fc, ACH=ACH:
                 e.activation(ACH[:, QW * fc:QW * (fc + 1)], s.SC[0][:],
                              AF.Relu),
                 r=("SC0",), w=("ACH",))
            s.do("scalar", lambda e:
                 e.activation(s.SC[3][:], s.SC[0][:], AF.Relu),
                 r=("SC0",), w=("SC3",))
            s.do("gpsimd", lambda e, fc=fc, ACH=ACH, ACL=ACL:
                 e.tensor_tensor(ACL[:, QW * fc:QW * (fc + 1)], s.SC[3][:],
                                 ACH[:, QW * fc:QW * (fc + 1)], OP.subtract),
                 r=("SC3", "ACH"), w=("ACL",))

        def w2_matmuls(kc):
            w2h, w2l = W2H[kc % 2], W2L[kc % 2]
            hk, lk2 = f"W1H{kc % 2}", f"W1L{kc % 2}"
            for mch in range(NCH):
                for pi, (lh, lk, rh, rk) in enumerate(
                        ((w2h, hk, ACH, "ACH"),
                         (w2h, hk, ACL, "ACL"),
                         (w2l, lk2, ACH, "ACH"))):
                    s.do("tensor", lambda e, kc=kc, mch=mch, lh=lh, rh=rh,
                         pi=pi:
                         e.matmul(s.PS[mch][:],
                                  lh[:, 128 * mch:128 * (mch + 1)],
                                  rh[:, QW * kc:QW * (kc + 1)],
                                  start=(kc == 0 and pi == 0),
                                  stop=(kc == 31 and pi == 2)),
                         r=(lk, rk), w=(f"PS{mch}",))

        for qq in range(NQ):
            if qq == 0:
                for c in range(NCH):
                    xn_chunk(0, c)
            for fc in range(NFC):
                if fc + 1 < NFC:
                    w1_split(fc + 1)
                if fc + 2 < NFC:
                    w1_load(fc + 2)
                fc_matmuls(fc)
                if fc == NFC - 3:
                    w2_load(0)
                if fc == NFC - 2:
                    w2_split(0)
                    w2_load(1)
                quantum(fc, qq)
            for kc in range(32):
                if kc + 1 < 32:
                    w2_split(kc + 1)
                if kc + 2 < 32:
                    w2_load(kc + 2)
                w2_matmuls(kc)
                if kc < NCH and qq + 1 < NQ:
                    xn_chunk(qq + 1, kc)
                if qq + 1 < NQ:
                    if kc == 29:
                        w1_load(0)
                    if kc == 30:
                        w1_split(0)
                        w1_load(1)
            for mch in range(NCH):
                s.close_group(f"PS{mch}")
            for mch in range(NCH):
                s.do("vector", lambda e, mch=mch, qq=qq:
                     e.scalar_tensor_tensor(s.X[mch][:, QW * qq:QW * (qq + 1)],
                                            s.PS[mch][:], 0.0,
                                            s.X[mch][:, QW * qq:QW * (qq + 1)],
                                            OP.add, OP.add),
                     r=(f"PS{mch}", f"X{mch}q{qq}"), w=(f"X{mch}q{qq}",))
        for mch in range(NCH):
            for qq in range(NQ):
                s.do("vector", lambda e, mch=mch, qq=qq:
                     e.tensor_scalar_add(s.X[mch][:, QW * qq:QW * (qq + 1)],
                                         s.X[mch][:, QW * qq:QW * (qq + 1)],
                                         VEC[:, 96 + mch:97 + mch]),
                     r=(f"X{mch}q{qq}", "B2C"), w=(f"X{mch}q{qq}",))

    def fin(self):
        s = self
        self.emit_exit()
        nc = self.nc
        with nc.Block() as block:
            @block.tensor
            def _(e):
                for f in s.th["tensor"]:
                    f(e)

            @block.vector
            def _(e):
                for f in s.th["vector"]:
                    f(e)

            @block.scalar
            def _(e):
                for f in s.th["scalar"]:
                    f(e)

            @block.gpsimd
            def _(e):
                for f in s.th["gpsimd"]:
                    f(e)

            @block.sync
            def _(e):
                for f in s.th["sync"]:
                    f(e)
        return nc

    def emit_exit(self):
        s = self
        OT = [s.nc.alloc_sbuf_tensor_at(f"XO{i}", [128, 1024], F32,
                                        offset=self.ovl_base + 40 * 1024 + i * 4096)
              for i in range(2)]
        for tt in range(16):
            ot = OT[tt % 2]
            ok = f"OT{tt % 2}"
            for c in range(NCH):
                bk = f"PS{c % 2}"
                bank = s.PS[c % 2]
                s.do("tensor", lambda e, tt=tt, c=c, bank=bank:
                     e.transpose(bank[:, 0:128],
                                 s.X[c][:, 128 * tt:128 * (tt + 1)], s.I128[:]),
                     r=(f"X{c}q{tt // 4}", "ident"), w=(bk,))
                s.do("vector", lambda e, c=c, ot=ot, bank=bank:
                     e.tensor_copy(ot[:, 128 * c:128 * (c + 1)], bank[:, 0:128]),
                     r=(bk,), w=(ok,))
            s.do("sync", lambda e, tt=tt, ot=ot:
                 e.dma_start(out=s.y_out[128 * tt:128 * (tt + 1), :], in_=ot[:]),
                 r=(ok,), dma="st")
        s.do("sync", lambda e: e.nop(),
             xw=[(sn, s.cnt[sn]) for sn in s.STS])


def _host_inputs(inputs, core):
    f32 = np.float32
    x = np.ascontiguousarray(np.asarray(inputs["L0_attn_x"])[core]).astype(
        f32, copy=False)
    m = {"x_in": x}
    t_ = lambda a: np.ascontiguousarray(np.asarray(a, dtype=f32).T)
    c_ = lambda a: np.ascontiguousarray(np.asarray(a, dtype=f32))
    m["wdq"] = c_(inputs["L0_attn_W_DQ"])
    m["wuqT"] = t_(inputs["L0_attn_W_UQ"])
    m["wdkv"] = c_(inputs["L0_attn_W_DKV"])
    m["wukvT"] = t_(inputs["L0_attn_W_UKV"])
    m["wqrT"] = t_(inputs["L0_attn_W_QR"])
    m["wkrT"] = t_(inputs["L0_attn_W_KR"])
    m["woT"] = t_(inputs["L0_attn_W_O"])
    m["qnw"] = c_(inputs["L0_attn_q_norm_w"]).reshape(1, HD)
    m["knw"] = c_(inputs["L0_attn_k_norm_w"]).reshape(1, HD)
    m["w0T"] = t_(inputs["L0_ffn_weight"])
    m["wdq1T"] = t_(inputs["L1_attn_W_dq"])
    m["wdq1"] = c_(inputs["L1_attn_W_dq"])
    m["wuq1"] = c_(inputs["L1_attn_W_uq"])
    m["wuq1T"] = t_(inputs["L1_attn_W_uq"])
    m["wdkv1T"] = t_(inputs["L1_attn_W_dkv"])
    m["wuk1"] = c_(inputs["L1_attn_W_uk"])
    m["wuv1"] = c_(inputs["L1_attn_W_uv"])
    m["wo1T"] = t_(inputs["L1_attn_W_o"])
    w1 = c_(inputs["L1_ffn_W1"])
    m["w1p"] = np.ascontiguousarray(
        w1.reshape(8, 128, 32, 128).transpose(2, 1, 0, 3).reshape(DFF, D))
    m["w2"] = c_(inputs["L1_ffn_W2"])
    m["b1"] = c_(inputs["L1_ffn_b1"]).reshape(32, 128)
    m["th1"] = c_(inputs["L1_ffn_quantum_phases"]).reshape(32, 128)
    m["qwr"] = c_(inputs["L1_ffn_quantum_weights_real"]).reshape(32, 128)
    m["qwi"] = c_(inputs["L1_ffn_quantum_weights_imag"]).reshape(32, 128)
    m["b2"] = c_(inputs["L1_ffn_b2"]).reshape(8, 128)
    import ml_dtypes
    m["ident"] = np.eye(128, dtype=f32)
    m["identb"] = np.eye(128, dtype=ml_dtypes.bfloat16)
    msk = np.zeros((128, 4 * QW), dtype=f32)
    i = np.arange(128)[:, None]
    jj = np.arange(QW)[None, :]
    for j in range(4):
        msk[:, QW * j:QW * (j + 1)] = np.where((i + 128 * j) <= jj, 0.0,
                                               -30000.0).astype(f32)
    m["mask"] = msk.astype(ml_dtypes.bfloat16)
    m["all1"] = np.ones((128, 128), dtype=f32)
    m["one1"] = np.ones((1, 128), dtype=f32)
    o2b = np.zeros((128, 2), dtype=f32)
    o2b[0:64, 0] = 1.0
    o2b[64:128, 1] = 1.0
    m["ones2b"] = o2b
    ob2 = np.zeros((2, 128), dtype=f32)
    ob2[0, 0:64] = 1.0
    ob2[1, 64:128] = 1.0
    m["onesb2"] = ob2
    return m


_NC_CACHE = {}


def _get_nc(stop_after):
    if stop_after not in _NC_CACHE:
        _NC_CACHE[stop_after] = KB(stop_after).build()
    return _NC_CACHE[stop_after]


def kernel(**inputs):
    nc = _get_nc("l1f")
    in_maps = [_host_inputs(inputs, c) for c in range(8)]
    res = run_bass_kernel_spmd(nc, in_maps, core_ids=list(range(8)))
    return np.stack([r["y_out"] for r in res.results]).astype(np.float32)


# version 37
# speedup vs baseline: 1.0112x; 1.0044x over previous
import math
import numpy as np
import ml_dtypes
import concourse.bass as bass
import concourse.mybir as mybir
from concourse.bass_utils import run_bass_kernel_spmd

F32 = mybir.dt.float32
BF16 = mybir.dt.bfloat16
AF = mybir.ActivationFunctionType
OP = mybir.AluOpType
AX = mybir.AxisListType

T = 2048
D = 1024
NCH = 8
NQ = 4
QW = 512
NKC = 16
HD = 64
RMS_EPS = 1.1920929e-07
LN_EPS = 1e-5
DFF = 4096
FCW = 128
NFC = DFF // FCW
PI_F = float(np.float32(math.pi))
TPI_HI = float(np.float32(2.0 * math.pi))
TPI_LO = float(np.float32(2.0 * math.pi - float(np.float32(2.0 * math.pi))))
ENGS = ("tensor", "vector", "scalar", "gpsimd", "sync")
ESEM = {"tensor": "t", "vector": "v", "scalar": "s", "gpsimd": "g", "sync": "q"}


class KB:
    def __init__(self, stop_after="l1f"):
        self.stop_after = stop_after
        nc = bass.Bass("TRN2", target_bir_lowering=False)
        self.nc = nc
        self.th = {e: [] for e in ENGS}
        self.sems = {}
        self.cnt = {}
        self.hw = {e: {} for e in ENGS}
        self.LDS = [f"ld{i}" for i in range(8)]
        self.STS = [f"st{i}" for i in range(4)]
        for sname in ("t", "v", "s", "g", "q", *self.LDS, *self.STS):
            self.sems[sname] = nc.semaphore(sname).__enter__()
            self.cnt[sname] = 0
        self.ld_rr = 0
        self.st_rr = 0
        self.dr = {}
        self.AL = {}
        self.lw = {}
        self.rdrs = {}
        self.pend_ep = None

    def do(self, eng, build, r=(), w=(), dma=None, xw=(), force=False):
        waits = {}
        for k in r:
            for sn, v0 in self.lw.get(k, {}).items():
                waits[sn] = max(waits.get(sn, 0), v0)
        for k in w:
            for sn, v0 in self.lw.get(k, {}).items():
                waits[sn] = max(waits.get(sn, 0), v0)
            for sn, v0 in self.rdrs.get(k, {}).items():
                waits[sn] = max(waits.get(sn, 0), v0)
        for k in w:
            if k not in self.lw and k not in self.rdrs and getattr(self, "floor", None):
                for sn, v0 in self.floor.items():
                    waits[sn] = max(waits.get(sn, 0), v0)
        for sn, v0 in xw:
            if v0:
                waits[sn] = max(waits.get(sn, 0), v0)
        real = []
        for sn, v0 in waits.items():
            if not force:
                if eng == "tensor" and sn == "t":
                    continue
                if self.hw[eng].get(sn, 0) >= v0 and sn in ("q",):
                    continue
            self.hw[eng][sn] = max(self.hw[eng].get(sn, 0), v0)
            real.append((sn, v0))
        if dma is None:
            sn_inc, n = ESEM[eng], 1
        else:
            if dma == "ld":
                sn_inc = self.LDS[self.ld_rr % len(self.LDS)]
                self.ld_rr += 1
            else:
                sn_inc = self.STS[self.st_rr % len(self.STS)]
                self.st_rr += 1
            n = 16
            prev = self.cnt[sn_inc]
            if prev:
                self.hw[eng][sn_inc] = max(self.hw[eng].get(sn_inc, 0), prev)
                real.append((sn_inc, prev))
        self.cnt[sn_inc] += n
        mycnt = self.cnt[sn_inc]
        sems = self.sems

        def f(e, build=build, real=tuple(real), sn_inc=sn_inc, n=n):
            for (sn, v0) in real:
                e.wait_ge(sems[sn], v0)
            build(e).then_inc(sems[sn_inc], n)
        self.th[eng].append(f)
        for k in w:
            d = self.lw.setdefault(k, {})
            d[sn_inc] = max(d.get(sn_inc, 0), mycnt)
        for k in r:
            d = self.rdrs.setdefault(k, {})
            d[sn_inc] = max(d.get(sn_inc, 0), mycnt)
        return mycnt

    def close_group(self, *keys):
        for k in keys:
            d = self.lw.setdefault(k, {})
            d["t"] = max(d.get("t", 0), self.cnt["t"])

    def flush_ep(self):
        if self.pend_ep is not None:
            ep, self.pend_ep = self.pend_ep, None
            ep()

    def soft_barrier(self):
        self.flush_ep()
        self.floor = {ESEM[e]: self.cnt[ESEM[e]] for e in ENGS}
        for sn in (*self.LDS, *self.STS):
            self.floor[sn] = self.cnt[sn]

    def barrier(self):
        self.flush_ep()
        self.do("sync", lambda e: e.nop(),
                xw=[(sn, self.cnt[sn]) for sn in (*self.LDS, *self.STS)])
        vals = {}
        for e in ENGS:
            vals[e] = self.do(e, lambda en: en.nop(),
                              xw=[(ESEM[e], self.cnt[ESEM[e]])], force=True)
        for e in ENGS:
            for o in ENGS:
                if o == e:
                    continue
                sn = ESEM[o]
                if self.hw[e].get(sn, 0) < vals[o]:
                    self.hw[e][sn] = vals[o]
                    self.th[e].append(
                        lambda en, sn=sn, v0=vals[o]: en.wait_ge(self.sems[sn], v0))
            for sn in (*self.LDS, *self.STS):
                self.hw[e][sn] = max(self.hw[e].get(sn, 0), self.cnt[sn])
        self.floor = {ESEM[o]: vals[o] for o in ENGS}
        for sn in (*self.LDS, *self.STS):
            self.floor[sn] = self.cnt[sn]

    def din(self, name, shape, dt_=F32):
        self.dr[name] = self.nc.dram_tensor(name, list(shape), dt_,
                                            kind="ExternalInput").ap()
        return self.dr[name]

    def sb(self, name, shape, dt_=F32):
        self.AL[name] = self.nc.alloc_sbuf_tensor(name, list(shape), dt_)
        return self.AL[name]

    def ov(self, name, shape, dt_, off):
        sz = int(np.prod(shape[1:])) * mybir.dt.size(dt_)
        assert off + sz <= self.ARENA, (name, off, sz, self.ARENA)
        if name in self.AL:
            return self.AL[name]
        self.AL[name] = self.nc.alloc_sbuf_tensor_at(
            name, list(shape), dt_, offset=self.ovl_base + off)
        return self.AL[name]

    def loadw(self, name, dst, dkey, cols=None, rowsl=None, dstcol0=0):
        ap = self.dr[name]
        rows = ap.shape[0] if rowsl is None else (rowsl.stop - rowsl.start)
        r0 = 0 if rowsl is None else rowsl.start
        cols = ap.shape[1] if cols is None else cols
        nch = (rows + 127) // 128
        v = None
        for ch in range(nch):
            v = self.do("sync", lambda e, ap=ap, ch=ch, r0=r0, cols=cols, dst=dst,
                        dstcol0=dstcol0:
                        e.dma_start(out=dst[:, dstcol0 + cols * ch:dstcol0 + cols * (ch + 1)],
                                    in_=ap[r0 + 128 * ch:r0 + 128 * (ch + 1), 0:cols]),
                        w=(dkey,), dma="ld")
        return v

    def build(self):
        nc = self.nc
        s = self
        self.declare_dram()
        self.X = [s.sb(f"X{c}", [128, T]) for c in range(NCH)]
        self.MU = s.sb("MU", [128, T])
        self.RBC = s.sb("RBC", [128, T])
        self.E32 = [s.sb(f"E32_{i}", [128, QW]) for i in range(3)]
        self.I128 = s.sb("I128", [128, 128])
        self.ALL1 = s.sb("ALL1", [128, 128])
        self.ONES2B = s.sb("ONES2B", [128, 2])
        self.ONESB2 = s.sb("ONESB2", [2, 128])
        self.ONE1 = s.sb("ONE1", [1, 128])
        self.SM = s.sb("SM", [128, 64])
        self.ROWS = s.sb("ROWS", [2, QW])
        self.SC = [s.sb(f"SC_{i}", [128, QW]) for i in range(4)]
        self.RBF = s.sb("RBF", [128, QW])
        self.TMPE = s.sb("TMPE", [128, QW])
        self.ARENA = 104 * 1024
        self.ovl_base = nc.sbuf_top - self.ARENA
        self.PS = [nc.alloc_psum_tensor(f"PS{i}", [128, QW], F32) for i in range(8)]
        self.MSK = s.ov("MSK", [128, 4 * QW], BF16, 90 * 1024)
        self.I128B = s.sb("I128B", [128, 128], BF16)

        self.emit_entry()
        for phase in ("l0a", "l0f", "l1a", "l1f"):
            if phase != "l1f":
                self.emit_ln()
            getattr(self, "emit_" + phase)()
            self.soft_barrier()
            if self.stop_after == phase:
                break
        return self.fin()

    def declare_dram(self):
        s = self
        s.din("x_in", [T, D])
        self.y_out = s.nc.dram_tensor("y_out", [T, D], F32,
                                      kind="ExternalOutput").ap()
        s.din("wdq", [512, HD]); s.din("wuqT", [512, HD])
        s.din("wdkv", [1024, HD]); s.din("wukvT", [1024, 2 * HD])
        s.din("wqrT", [HD, HD]); s.din("wkrT", [HD, HD]); s.din("woT", [HD, HD])
        s.din("qnw", [1, HD]); s.din("knw", [1, HD])
        s.din("w0T", [D, D])
        s.din("wdq1T", [D, 256]); s.din("wdq1", [256, D])
        s.din("wuq1", [D, 256]); s.din("wuq1T", [256, D])
        s.din("wdkv1T", [D, 256]); s.din("wuk1", [D, 256]); s.din("wuv1", [D, 256])
        s.din("wo1T", [D, D])
        s.din("w1p", [DFF, D]); s.din("w2", [DFF, D])
        s.din("b1", [32, 128]); s.din("th1", [32, 128])
        s.din("qwr", [32, 128]); s.din("qwi", [32, 128]); s.din("b2", [8, 128])
        s.din("ident", [128, 128]); s.din("mask", [128, 4 * QW], BF16)
        s.din("identb", [128, 128], BF16)
        s.din("all1", [128, 128]); s.din("one1", [1, 128])
        s.din("ones2b", [128, 2]); s.din("onesb2", [2, 128])

    def emit_entry(self):
        s = self
        TM = [s.ov(f"TM{i}", [128, D], F32, i * 4096) for i in range(16)]
        for i in range(16):
            s.do("sync", lambda e, i=i, TM=TM:
                 e.dma_start(out=TM[i][:], in_=s.dr["x_in"][128 * i:128 * (i + 1), :]),
                 w=(f"TM{i}",), dma="ld")
        for nm, dst in (("ident", s.I128), ("mask", s.MSK), ("all1", s.ALL1),
                        ("one1", s.ONE1), ("ones2b", s.ONES2B),
                        ("onesb2", s.ONESB2), ("identb", s.I128B)):
            s.do("sync", lambda e, nm=nm, dst=dst:
                 e.dma_start(out=dst[:, :], in_=s.dr[nm][:, :]),
                 w=(nm,), dma="ld")
        for c in range(NCH):
            for g in range(4):
                bk = f"PS{g % 2}"
                bank = s.PS[g % 2]
                for j in range(4):
                    tt = 4 * g + j
                    s.do("tensor",
                         lambda e, c=c, tt=tt, j=j, bank=bank, TM=TM:
                         e.transpose(bank[:, 128 * j:128 * (j + 1)],
                                     TM[tt][:, 128 * c:128 * (c + 1)], s.I128[:]),
                         r=(f"TM{tt}", "ident"), w=(bk,))
                s.do("vector",
                     lambda e, c=c, g=g, bank=bank:
                     e.tensor_copy(s.X[c][:, QW * g:QW * (g + 1)], bank[:]),
                     r=(bk,), w=(f"X{c}q{g}",))
        s.barrier()

    def emit_ln_win(self, qq):
        s = self
        for c in range(NCH):
            sq = s.SC[c % 2]
            s.do("scalar",
                 lambda e, c=c, qq=qq, sq=sq:
                 e.activation(sq[:], s.X[c][:, QW * qq:QW * (qq + 1)], AF.Square),
                 r=(f"X{c}q{qq}",), w=(f"SC{c % 2}",))
            s.do("tensor",
                 lambda e, c=c, qq=qq:
                 e.matmul(s.PS[4][:], s.ALL1[:], s.X[c][:, QW * qq:QW * (qq + 1)],
                          start=(c == 0), stop=(c == NCH - 1)),
                 r=(f"X{c}q{qq}", "all1"), w=("PS4",))
            s.do("tensor",
                 lambda e, sq=sq, c=c:
                 e.matmul(s.PS[5][:], s.ALL1[:], sq[:],
                          start=(c == 0), stop=(c == NCH - 1)),
                 r=(f"SC{c % 2}", "all1"), w=("PS5",))
        s.close_group("PS4", "PS5")
        mu_s = s.MU[:, QW * qq:QW * (qq + 1)]
        rb_s = s.RBC[:, QW * qq:QW * (qq + 1)]
        s.do("scalar", lambda e, mu_s=mu_s:
             e.activation(mu_s, s.PS[4][:], AF.Identity, scale=1.0 / D),
             r=("PS4",), w=(f"MU{qq}",))
        s.do("scalar", lambda e:
             e.activation(s.SC[2][:], s.PS[5][:], AF.Identity, scale=1.0 / D),
             r=("PS5",), w=("SC2",))
        s.do("scalar", lambda e, mu_s=mu_s:
             e.activation(s.SC[3][:], mu_s, AF.Square),
             r=(f"MU{qq}",), w=("SC3",))
        s.do("vector", lambda e:
             e.tensor_tensor(s.SC[2][:], s.SC[2][:], s.SC[3][:], OP.subtract),
             r=("SC2", "SC3"), w=("SC2",))
        s.do("vector", lambda e:
             e.tensor_single_scalar(s.SC[2][:], s.SC[2][:], LN_EPS, OP.add),
             r=("SC2",), w=("SC2",))
        s.do("scalar", lambda e:
             e.activation(s.SC[3][:], s.SC[2][:], AF.Ln), r=("SC2",), w=("SC3",))
        s.do("scalar", lambda e, rb_s=rb_s:
             e.activation(rb_s, s.SC[3][:], AF.Exp, scale=-0.5),
             r=("SC3",), w=(f"RB{qq}",))

    def emit_ln(self):
        for qq in range(NQ):
            self.emit_ln_win(qq)

    def xn_into(self, dst, dkey, c, qq, eng="vector", wx=()):
        s = self
        s.do(eng, lambda e, c=c, qq=qq, dst=dst:
             e.tensor_tensor(dst, s.X[c][:, QW * qq:QW * (qq + 1)],
                             s.MU[:, QW * qq:QW * (qq + 1)], OP.subtract),
             r=(f"X{c}q{qq}", f"MU{qq}"), w=(dkey, *wx))
        s.do(eng, lambda e, qq=qq, dst=dst:
             e.tensor_tensor(dst, dst, s.RBC[:, QW * qq:QW * (qq + 1)], OP.mult),
             r=(dkey, f"RB{qq}"), w=(dkey,))

    def attn_head(self, k_lhsT, kkey, q_rhs, qkey, v_lhsT, vkey, negc, nkey,
                  xout, xkey, half):
        s = self
        hs_ctx = slice(64, 128) if half else slice(0, 64)
        hs_z = slice(0, 64) if half else slice(64, 128)
        for qq in range(NQ):
            kcmax = 4 * qq + 4
            s.pvc = getattr(s, "pvc", 0) + 1
            pvb = (2, 3, 7)[s.pvc % 3]
            pvbank = s.PS[pvb]

            def scores(kc, qq=qq):
                sb = (0, 1, 4, 5)[kc % 4]
                diag = kc >= 4 * qq
                s.do("tensor", lambda e, kc=kc, sb=sb, qq=qq, diag=diag:
                     e.matmul(s.PS[sb][:], k_lhsT(kc), q_rhs(qq),
                              start=True, stop=not diag),
                     r=(kkey, qkey), w=(f"PS{sb}",))
                if diag:
                    j = kc - 4 * qq
                    s.do("tensor", lambda e, sb=sb, j=j:
                         e.matmul(s.PS[sb][:], s.I128B[:],
                                  s.MSK[:, QW * j:QW * (j + 1)],
                                  start=False, stop=True),
                         r=("identb", "mask"), w=(f"PS{sb}",))
                eb = kc % 3
                s.do("scalar", lambda e, sb=sb, eb=eb:
                     e.activation(s.E32[eb][:], s.PS[sb][:], AF.Exp, bias=negc),
                     r=(f"PS{sb}", nkey), w=(f"E{eb}",))

            def pv(kc, qq=qq, kcmax=kcmax, pvbank=pvbank, pvb=pvb):
                eb = kc % 3
                s.do("tensor", lambda e, kc=kc, eb=eb, pvbank=pvbank, kcmax=kcmax:
                     e.matmul(pvbank[:], v_lhsT(kc), s.E32[eb][:],
                              start=(kc == 0), stop=(kc == kcmax - 1)),
                     r=(vkey, f"E{eb}"), w=(f"PS{pvb}",))

            def epilogue(qq=qq, pvb=pvb, pvbank=pvbank,
                         hs_ctx=hs_ctx, hs_z=hs_z, xout=xout, xkey=xkey):
                s.do("tensor", lambda e:
                     e.matmul(s.PS[6][hs_ctx, :], s.I128[hs_z, hs_z],
                              s.RBF[hs_z, :], start=True, stop=True),
                     r=("RBFz", "ident"), w=("PS6",))
                s.do("vector", lambda e:
                     e.tensor_copy(s.RBF[hs_ctx, :], s.PS[6][hs_ctx, :]),
                     r=("PS6",), w=("RBFc",))
                s.do("vector", lambda e:
                     e.tensor_tensor(s.TMPE[hs_ctx, :], pvbank[hs_ctx, :],
                                     s.RBF[hs_ctx, :], OP.mult),
                     r=(f"PS{pvb}", "RBFc"), w=("TMPK",))
                s.do("vector", lambda e:
                     e.tensor_tensor(xout(qq), xout(qq), s.TMPE[hs_ctx, :],
                                     OP.add),
                     r=("TMPK", xkey(qq)), w=(xkey(qq),))

            for kc in range(kcmax):
                scores(kc)
                if kc >= 2:
                    pv(kc - 2)
                if kc == 3:
                    s.flush_ep()
            if kcmax >= 2:
                pv(kcmax - 2)
            pv(kcmax - 1)
            s.close_group(f"PS{pvb}")
            s.do("scalar", lambda e, pvbank=pvbank:
                 e.activation(s.TMPE[hs_z, :], pvbank[hs_z, :], AF.Ln),
                 r=(f"PS{pvb}",), w=("TMPK",))
            s.do("scalar", lambda e:
                 e.activation(s.RBF[hs_z, :], s.TMPE[hs_z, :], AF.Exp,
                              scale=-1.0),
                 r=("TMPK",), w=("RBFz",))
            s.pend_ep = epilogue

    def head_bound(self, qt, qkey, kt, kkey, col):
        s = self
        s.do("vector", lambda e, qt=qt:
             e.tensor_reduce(s.SC[0][:, 0:1], qt[:, 0:T], AX.X, OP.max,
                             apply_absolute_value=True),
             r=(qkey,), w=("SC0",))
        s.do("vector", lambda e, kt=kt:
             e.tensor_reduce(s.SC[0][:, 1:2], kt[:, 0:T], AX.X, OP.max,
                             apply_absolute_value=True),
             r=(kkey,), w=("SC0",))
        s.do("scalar", lambda e:
             e.activation(s.SC[0][:, 2:4], s.SC[0][:, 0:2], AF.Square),
             r=("SC0",), w=("SC0",))
        s.do("tensor", lambda e:
             e.matmul(s.PS[6][0:128, 0:2], s.ALL1[:], s.SC[0][:, 2:4],
                      start=True, stop=True),
             r=("SC0", "all1"), w=("PS6",))
        s.do("scalar", lambda e:
             e.activation(s.SC[0][:, 4:6], s.PS[6][0:128, 0:2], AF.Sqrt),
             r=("PS6",), w=("SC0",))
        s.do("vector", lambda e:
             e.tensor_tensor(s.SC[0][:, 6:7], s.SC[0][:, 4:5], s.SC[0][:, 5:6],
                             OP.mult),
             r=("SC0",), w=("SC0",))
        s.do("vector", lambda e, col=col:
             e.tensor_scalar(s.SM[:, col:col + 1], s.SC[0][:, 6:7],
                             43.0, -1.0, OP.min, OP.mult),
             r=("SC0",), w=(f"SM{col}",))

    def emit_l0a(self):
        s = self
        WT = {}
        off = 70 * 1024
        for name, rows, cols in (("wdq", 512, HD), ("wuqT", 512, HD),
                                 ("wdkv", 1024, HD), ("wukvT", 1024, 2 * HD)):
            nch = rows // 128
            WT[name] = s.ov("w_" + name, [128, nch * cols], F32, off)
            off += nch * cols * 4
            s.loadw(name, WT[name], "w_" + name, cols=cols)
        for name in ("wqrT", "wkrT", "woT"):
            WT[name] = s.ov("w_" + name, [128, HD], F32, off)
            off += HD * 4
            for hh in range(2):
                s.do("sync", lambda e, name=name, hh=hh, WT=WT:
                     e.dma_start(out=WT[name][64 * hh:64 * (hh + 1), :],
                                 in_=s.dr[name][:, :]),
                     w=("w_" + name,), dma="ld")
        for col, name in ((0, "qnw"), (1, "knw")):
            for hh in range(2):
                s.do("sync", lambda e, name=name, col=col, hh=hh:
                     e.dma_start(out=s.SM[64 * hh:64 * (hh + 1), col:col + 1],
                                 in_=s.dr[name][0:1, :]),
                     w=(f"SM{col}",), dma="ld")
        EQT = s.ov("EQT", [128, 128], F32, 84 * 1024)
        EKT = s.ov("EKT", [128, 128], F32, 84 * 1024 + 512)
        WOE2 = s.ov("WOE2", [128, 128], F32, 84 * 1024 + 1024)
        WQR2 = s.ov("WQR2", [128, HD], F32, 84 * 1024 + 1536)
        WKR2 = s.ov("WKR2", [128, HD], F32, 84 * 1024 + 1792)
        EVT = s.ov("EVT", [64, 64], F32, 84 * 1024 + 2048)
        STG = s.ov("STG", [64, 192], F32, 84 * 1024 + 2304)

        for ch in range(4):
            s.do("tensor", lambda e, ch=ch, WT=WT:
                 e.matmul(s.PS[6][0:64, 0:64], WT["wdq"][:, HD * ch:HD * (ch + 1)],
                          WT["wuqT"][:, HD * ch:HD * (ch + 1)],
                          start=(ch == 0), stop=(ch == 3)),
                 r=("w_wdq", "w_wuqT"), w=("PS6",))
        for ch in range(8):
            s.do("tensor", lambda e, ch=ch, WT=WT:
                 e.matmul(s.PS[6][0:64, 64:128], WT["wdkv"][:, HD * ch:HD * (ch + 1)],
                          WT["wukvT"][:, 128 * ch:128 * ch + 64],
                          start=(ch == 0), stop=(ch == 7)),
                 r=("w_wdkv", "w_wukvT"))
        for ch in range(8):
            s.do("tensor", lambda e, ch=ch, WT=WT:
                 e.matmul(s.PS[6][0:64, 128:192],
                          WT["wukvT"][:, 128 * ch + 64:128 * (ch + 1)],
                          WT["wdkv"][:, HD * ch:HD * (ch + 1)],
                          start=(ch == 0), stop=(ch == 7)),
                 r=("w_wdkv", "w_wukvT"))
        s.close_group("PS6")
        s.do("vector", lambda e, STG=STG:
             e.tensor_copy(STG[:, 0:192], s.PS[6][0:64, 0:192]),
             r=("PS6",), w=("STG",))
        s.do("vector", lambda e, STG=STG, EVT=EVT:
             e.tensor_copy(EVT[:], STG[:, 128:192]), r=("STG",), w=("EVT",))
        s.do("tensor", lambda e, EVT=EVT, WT=WT:
             e.matmul(s.PS[7][0:64, 0:64], EVT[:], WT["woT"][0:64, :],
                      start=True, stop=True),
             r=("EVT", "w_woT"), w=("PS7",))
        s.do("vector", lambda e, STG=STG:
             e.tensor_copy(STG[:, 128:192], s.PS[7][0:64, 0:64]),
             r=("PS7",), w=("STG",))
        for tn, key in ((EQT, "EQT"), (EKT, "EKT"), (WOE2, "WOE2")):
            s.do("vector", lambda e, tn=tn: e.memset(tn[:], 0.0), w=(key,))
        for tn, key, sl in ((EQT, "EQT", slice(0, 64)), (EKT, "EKT", slice(64, 128)),
                            (WOE2, "WOE2", slice(128, 192))):
            s.do("vector", lambda e, tn=tn, sl=sl, STG=STG:
                 e.tensor_copy(tn[0:64, 0:64], STG[:, sl]), r=("STG",), w=(key,))
            s.do("sync", lambda e, tn=tn, sl=sl, STG=STG:
                 e.dma_start(out=tn[64:128, 64:128], in_=STG[:, sl]),
                 r=("STG",), w=(key,), dma="ld")
        s.do("vector", lambda e:
             e.tensor_tensor(s.SM[:, 2:3], s.SM[:, 0:1], s.SM[:, 1:2], OP.mult),
             r=("SM0", "SM1"), w=("SM2",))
        s.do("vector", lambda e:
             e.tensor_single_scalar(s.SM[:, 2:3], s.SM[:, 2:3], HD ** -0.5, OP.mult),
             r=("SM2",), w=("SM2",))
        s.do("vector", lambda e, WQR2=WQR2, WT=WT:
             e.tensor_scalar_mul(WQR2[:], WT["wqrT"][:, :], s.SM[:, 0:1]),
             r=("w_wqrT", "SM0"), w=("WQR2",))
        s.do("vector", lambda e, WKR2=WKR2, WT=WT:
             e.tensor_scalar_mul(WKR2[:], WT["wkrT"][:, :], HD ** -0.5),
             r=("w_wkrT",), w=("WKR2",))

        NC_ = s.ov("NC_", [128, T], F32, 0)
        QA = s.ov("QA", [128, T], F32, 8 * 1024)
        QB = s.ov("QB", [128, T], F32, 16 * 1024)
        KA = s.ov("KA", [128, T], F32, 24 * 1024)
        KB = s.ov("KB", [128, T], F32, 32 * 1024)
        VFA = s.ov("VFA", [128, T], F32, 40 * 1024)
        VFB = s.ov("VFB", [128, T], F32, 48 * 1024)
        RSBP = [s.ov("RSB", [128, QW], F32, 56 * 1024),
                s.ov("RSB2", [128, QW], F32, 60 * 1024)]
        SQSP = [s.ov("SQS", [128, QW], F32, 58 * 1024),
                s.ov("SQS2", [128, QW], F32, 62 * 1024)]
        ROWP = [s.ROWS, s.ov("ROW2", [2, QW], F32, 64 * 1024)]

        units = ((EQT, "EQT", QA, "QA", QB, "QB", False),
                 (EKT, "EKT", KA, "KA", KB, "KB", True))

        def stA(i, NC_=NC_, SQSP=SQSP):
            (WTm, wkey, dA, kA_, dB, kB_, is_k) = units[i % 2]
            qq = i // 2
            bk = f"PS{4 + (i % 2)}"
            bank = s.PS[4 + (i % 2)]
            sq, sqk = SQSP[i % 2], f"SQSp{i % 2}"
            s.do("tensor", lambda e, WTm=WTm, bank=bank, qq=qq:
                 e.matmul(bank[:], WTm[:], NC_[:, QW * qq:QW * (qq + 1)],
                          start=True, stop=True),
                 r=(wkey, "NC"), w=(bk,))
            s.do("vector", lambda e, qq=qq, dA=dA, bank=bank:
                 e.tensor_copy(dA[0:64, QW * qq:QW * (qq + 1)], bank[0:64, :]),
                 r=(bk,), w=(kA_,))
            s.do("vector", lambda e, qq=qq, dB=dB, bank=bank:
                 e.tensor_copy(dB[64:128, QW * qq:QW * (qq + 1)],
                               bank[64:128, :]),
                 r=(bk,), w=(kB_,))
            s.do("scalar", lambda e, qq=qq, dA=dA, sq=sq:
                 e.activation(sq[0:64, :], dA[0:64, QW * qq:QW * (qq + 1)],
                              AF.Square),
                 r=(kA_,), w=(sqk,))
            s.do("scalar", lambda e, qq=qq, dB=dB, sq=sq:
                 e.activation(sq[64:128, :], dB[64:128, QW * qq:QW * (qq + 1)],
                              AF.Square),
                 r=(kB_,), w=(sqk,))

        def stB(i, SQSP=SQSP, ROWP=ROWP):
            pb = f"PS{6 + (i % 2)}"
            pbank = s.PS[6 + (i % 2)]
            sq, sqk = SQSP[i % 2], f"SQSp{i % 2}"
            rw, rwk = ROWP[i % 2], f"ROWSp{i % 2}"
            s.do("tensor", lambda e, pbank=pbank, sq=sq:
                 e.matmul(pbank[0:2, :], s.ONES2B[:], sq[:],
                          start=True, stop=True),
                 r=(sqk, "ones2b"), w=(pb,))
            s.do("vector", lambda e, pbank=pbank, rw=rw:
                 e.tensor_scalar(rw[:], pbank[0:2, :], 1.0 / HD, RMS_EPS,
                                 OP.mult, OP.add),
                 r=(pb,), w=(rwk,))
            s.do("scalar", lambda e, sq=sq, rw=rw:
                 e.activation(sq[0:2, :], rw[:], AF.Ln), r=(rwk,), w=(sqk,))
            s.do("scalar", lambda e, sq=sq, rw=rw:
                 e.activation(rw[:], sq[0:2, :], AF.Exp, scale=-0.5),
                 r=(sqk,), w=(rwk,))

        def stC(i, ROWP=ROWP, RSBP=RSBP):
            (WTm, wkey, dA, kA_, dB, kB_, is_k) = units[i % 2]
            qq = i // 2
            pb = f"PS{6 + (i % 2)}"
            pbank = s.PS[6 + (i % 2)]
            rw, rwk = ROWP[i % 2], f"ROWSp{i % 2}"
            rb, rbk = RSBP[i % 2], f"RSBp{i % 2}"
            s.do("tensor", lambda e, pbank=pbank, rw=rw:
                 e.matmul(pbank[:], s.ONESB2[:], rw[:], start=True, stop=True),
                 r=(rwk, "onesb2"), w=(pb,))
            s.do("vector", lambda e, pbank=pbank, rb=rb:
                 e.tensor_copy(rb[:], pbank[:]), r=(pb,), w=(rbk,))
            for half, (dd, dk) in enumerate(((dA, kA_), (dB, kB_))):
                hcn = slice(0, 64) if half == 0 else slice(64, 128)
                if is_k:
                    s.do("vector", lambda e, qq=qq, dd=dd, hcn=hcn, rb=rb:
                         e.scalar_tensor_tensor(
                             dd[hcn, QW * qq:QW * (qq + 1)],
                             dd[hcn, QW * qq:QW * (qq + 1)],
                             s.SM[hcn, 2:3], rb[hcn, :], OP.mult, OP.mult),
                         r=(dk, "SM2", rbk), w=(dk,))
                else:
                    s.do("vector", lambda e, qq=qq, dd=dd, hcn=hcn, rb=rb:
                         e.tensor_tensor(
                             dd[hcn, QW * qq:QW * (qq + 1)],
                             dd[hcn, QW * qq:QW * (qq + 1)],
                             rb[hcn, :], OP.mult),
                         r=(dk, rbk), w=(dk,))

        def stD(i, NC_=NC_, WQR2=WQR2, WKR2=WKR2):
            (WTm, wkey, dA, kA_, dB, kB_, is_k) = units[i % 2]
            qq = i // 2
            for half, (dd, dk) in enumerate(((dA, kA_), (dB, kB_))):
                hcn = slice(0, 64) if half == 0 else slice(64, 128)
                hr = slice(64, 128) if half == 0 else slice(0, 64)
                bk = f"PS{half}"
                bank = s.PS[half]
                if is_k:
                    s.do("tensor", lambda e, qq=qq, hcn=hcn, hr=hr, bank=bank:
                         e.matmul(bank[hr, :], WKR2[hcn, :],
                                  NC_[hcn, QW * qq:QW * (qq + 1)],
                                  start=True, stop=True),
                         r=("WKR2", "NC"), w=(bk,))
                else:
                    s.do("tensor", lambda e, qq=qq, dd=dd, hcn=hcn, hr=hr,
                         bank=bank:
                         e.matmul(bank[hr, :], WQR2[hcn, :],
                                  dd[hcn, QW * qq:QW * (qq + 1)],
                                  start=True, stop=True),
                         r=("WQR2", dk), w=(bk,))
                s.do("vector", lambda e, qq=qq, dd=dd, hr=hr, bank=bank:
                     e.tensor_copy(dd[hr, QW * qq:QW * (qq + 1)], bank[hr, :]),
                     r=(bk,), w=(dk,))

        def veff(kc, NC_=NC_, WOE2=WOE2, VFA=VFA, VFB=VFB):
            bk = f"PS{2 + (kc % 2)}"
            bank = s.PS[2 + (kc % 2)]
            s.do("tensor", lambda e, kc=kc, bank=bank:
                 e.matmul(bank[:, 0:128], NC_[:, 128 * kc:128 * (kc + 1)],
                          WOE2[:], start=True, stop=True),
                 r=("NC", "WOE2"), w=(bk,))
            s.do("vector", lambda e, kc=kc, bank=bank:
                 e.tensor_copy(VFA[:, 128 * kc:128 * kc + 64], bank[:, 0:64]),
                 r=(bk,), w=("VFA",))
            s.do("vector", lambda e, kc=kc, bank=bank:
                 e.tensor_copy(VFB[:, 128 * kc + 64:128 * (kc + 1)],
                               bank[:, 64:128]),
                 r=(bk,), w=("VFB",))

        for c in range(NCH):
            s.flush_ep()
            if c == 0:
                s.do("vector", lambda e, VFA=VFA: e.memset(VFA[:], 1.0),
                     w=("VFA",))
                s.do("vector", lambda e, VFB=VFB: e.memset(VFB[:], 1.0),
                     w=("VFB",))
            s.xn_into(NC_[:, 0:QW], "NC", c, 0, eng="gpsimd")
            s.xn_into(NC_[:, QW:2 * QW], "NC", c, 1, eng="gpsimd")
            for i in range(11):
                if i in (2, 4):
                    w = i // 2 + 1
                    s.xn_into(NC_[:, QW * w:QW * (w + 1)], "NC", c, w,
                              eng="gpsimd")
                if i < 8:
                    stA(i)
                    veff(2 * i)
                    veff(2 * i + 1)
                if 1 <= i <= 8:
                    stB(i - 1)
                if 2 <= i <= 9:
                    stC(i - 2)
                if i >= 3:
                    stD(i - 3)
            s.head_bound(QA, "QA", KA, "KA", 3 + 2 * c)
            s.head_bound(QB, "QB", KB, "KB", 3 + 2 * c + 1)
            for half, (qt, qk_, kt, kk_, vt, vk_) in enumerate(
                    ((QA, "QA", KA, "KA", VFA, "VFA"),
                     (QB, "QB", KB, "KB", VFB, "VFB"))):
                h = 2 * c + half
                negc = s.SM[:, 3 + h:4 + h]
                xs = s.X[c]
                hs0 = 64 * half
                s.attn_head(
                    k_lhsT=lambda kc, kt=kt: kt[:, 128 * kc:128 * (kc + 1)],
                    kkey=kk_,
                    q_rhs=lambda qq, qt=qt: qt[:, QW * qq:QW * (qq + 1)],
                    qkey=qk_,
                    v_lhsT=lambda kc, vt=vt: vt[:, 128 * kc:128 * (kc + 1)],
                    vkey=vk_,
                    negc=negc, nkey=f"SM{3 + h}",
                    xout=lambda qq, xs=xs, hs0=hs0:
                        xs[hs0:hs0 + 64, QW * qq:QW * (qq + 1)],
                    xkey=lambda qq, c=c: f"X{c}q{qq}",
                    half=half)

    def emit_l0f(self):
        s = self
        WQ = s.ov("WQB", [128, 8 * 1024], BF16, 0)
        AQW = s.ov("AQW", [128, 8 * QW], BF16, 16 * 1024)
        WPB = s.ov("WPB", [128, T], F32, 24 * 1024)
        WTR = [s.ov(f"WTR{i}", [128, 1024], F32, 48 * 1024 + i * 4096)
               for i in range(2)]
        AMX = s.ov("AMX", [128, 64], F32, 56 * 1024)
        SB32 = s.ov("SB32", [32, 128], F32, 57 * 1024)
        RAB = s.ov("RAB", [128, QW], F32, 58 * 1024)
        ASB = s.ov("ASB", [128, QW], F32, 60 * 1024)
        ROWR = s.ov("ROWR", [1, 4096], F32, 64 * 1024)
        WPA = s.ov("WPA", [128, T], F32, 82 * 1024)
        U_ = s.ov("U_", [128, 2 * QW], F32, 24 * 1024)
        R_ = s.ov("R_", [128, 2 * QW], F32, 28 * 1024)
        B_ = s.ov("B_", [128, 2 * QW], F32, 94 * 1024)
        XN_ = s.ov("XN_", [128, 2 * QW], F32, 98 * 1024)
        s.KEF = [s.ov(f"KEF{i}", [128, 1024], F32, 32 * 1024 + i * 4096)
                 for i in range(2)]
        s.VEF = [s.ov(f"VEF{i}", [128, 1024], F32, 40 * 1024 + i * 4096)
                 for i in range(2)]
        AT = [s.ov(f"AT{i}", [128, 256], F32, 80 * 1024 + i * 1024)
              for i in range(2)]
        s.do("vector", lambda e: e.memset(s.SM[:, 30:31], 128.5), w=("SM30",))

        KEF, VEF = s.KEF, s.VEF
        s.loadw("wuq1", WPA, "WPA", cols=256)
        s.loadw("wuk1", WPB, "WPB", cols=256)
        for m in range(2):
            for ch in range(8):
                s.do("tensor", lambda e, m=m, ch=ch, WPA=WPA, WPB=WPB:
                     e.matmul(s.PS[2][0:128, 0:256],
                              WPA[:, 256 * ch + 128 * m:256 * ch + 128 * (m + 1)],
                              WPB[:, 256 * ch:256 * (ch + 1)],
                              start=(ch == 0), stop=(ch == 7)),
                     r=("WPA", "WPB"), w=("PS2",))
            s.close_group("PS2")
            s.do("vector", lambda e, m=m, AT=AT:
                 e.tensor_copy(AT[m][:], s.PS[2][0:128, 0:256]),
                 r=("PS2",), w=(f"AT{m}",))
        s.loadw("wdq1", WPA, "WPA", cols=1024)
        for m in range(2):
            for nn in range(2):
                for ch in range(2):
                    s.do("tensor", lambda e, m=m, nn=nn, ch=ch, AT=AT, WPA=WPA:
                         e.matmul(s.PS[2 + nn][:],
                                  AT[ch][:, 128 * m:128 * (m + 1)],
                                  WPA[:, 1024 * ch + 512 * nn:1024 * ch + 512 * (nn + 1)],
                                  start=(ch == 0), stop=(ch == 1)),
                         r=(f"AT{ch}", "WPA"), w=(f"PS{2 + nn}",))
                s.close_group(f"PS{2 + nn}")
            for nn in range(2):
                s.do("vector", lambda e, m=m, nn=nn, KEF=KEF:
                     e.tensor_copy(KEF[m][:, 512 * nn:512 * (nn + 1)],
                                   s.PS[2 + nn][:]),
                     r=(f"PS{2 + nn}",), w=(f"KEF{m}",))
        s.loadw("wuv1", WPB, "WPB", cols=256)
        for nn in range(2):
            for g4 in range(2):
                for ci in range(4):
                    ch = 4 * g4 + ci
                    s.do("sync", lambda e, ch=ch, ci=ci, nn=nn, WPA=WPA:
                         e.dma_start(out=WPA[:, 512 * ci:512 * (ci + 1)],
                                     in_=s.dr["wo1T"][128 * ch:128 * (ch + 1),
                                                      512 * nn:512 * (nn + 1)]),
                         w=("WPA",), dma="ld")
                for m in range(2):
                    for ci in range(4):
                        ch = 4 * g4 + ci
                        s.do("tensor", lambda e, m=m, ci=ci, ch=ch, WPA=WPA,
                             WPB=WPB:
                             e.matmul(s.PS[2 + m][:],
                                      WPB[:, 256 * ch + 128 * m:256 * ch + 128 * (m + 1)],
                                      WPA[:, 512 * ci:512 * (ci + 1)],
                                      start=(ch == 0), stop=(ch == 7)),
                             r=("WPB", "WPA"),
                             w=(f"PS{2 + m}",))
            s.close_group("PS2", "PS3")
            for m in range(2):
                s.do("vector", lambda e, m=m, nn=nn, VEF=VEF:
                     e.tensor_copy(VEF[m][:, 512 * nn:512 * (nn + 1)],
                                   s.PS[2 + m][:]),
                     r=(f"PS{2 + m}",), w=(f"VEF{m}",))
        for ch in range(NCH):
            wt = WTR[ch % 2]
            s.loadw("w0T", wt, f"WTR{ch % 2}", rowsl=slice(128 * ch, 128 * (ch + 1)))
            s.do("vector", lambda e, ch=ch, wt=wt:
                 e.tensor_reduce(s.SC[0][:, ch:ch + 1], wt[:], AX.X, OP.add,
                                 apply_absolute_value=True),
                 r=(f"WTR{ch % 2}",), w=("SC0",))
        s.do("tensor", lambda e:
             e.matmul(s.PS[6][0:128, 0:8], s.ALL1[:], s.SC[0][:, 0:8],
                      start=True, stop=True),
             r=("SC0", "all1"), w=("PS6",))
        s.do("vector", lambda e:
             e.tensor_reduce(s.SC[1][:, 0:1], s.PS[6][0:128, 0:8], AX.X, OP.add),
             r=("PS6",), w=("SC1",))
        s.do("vector", lambda e:
             e.tensor_scalar(s.SC[1][:, 1:2], s.SC[1][:, 0:1],
                             1.0 / (1024.0 * 1024.0), 1e-8, OP.mult, OP.add),
             r=("SC1",), w=("SC1",))
        s.do("vector", lambda e:
             e.reciprocal(s.SM[:, 20:21], s.SC[1][:, 1:2]),
             r=("SC1",), w=("SM20",))
        s.do("vector", lambda e:
             e.tensor_scalar_mul(s.SM[:, 21:22], s.SC[1][:, 1:2], 1.0 / 127.0),
             r=("SC1",), w=("SM21",))
        for ch in range(NCH):
            wt = WTR[ch % 2]
            s.loadw("w0T", wt, f"WTR{ch % 2}", rowsl=slice(128 * ch, 128 * (ch + 1)))
            for hf in range(2):
                sl0 = 512 * hf
                s.do("vector", lambda e, wt=wt, sl0=sl0:
                     e.tensor_scalar(s.SC[0][:], wt[:, sl0:sl0 + 512],
                                     s.SM[:, 20:21], 0.5, OP.mult, OP.is_ge),
                     r=(f"WTR{ch % 2}", "SM20"), w=("SC0",))
                s.do("vector", lambda e, wt=wt, sl0=sl0:
                     e.tensor_scalar(s.SC[1][:], wt[:, sl0:sl0 + 512],
                                     s.SM[:, 20:21], -0.5, OP.mult, OP.is_le),
                     r=(f"WTR{ch % 2}", "SM20"), w=("SC1",))
                s.do("vector", lambda e, ch=ch, sl0=sl0, WQ=WQ:
                     e.tensor_tensor(WQ[:, 1024 * ch + sl0:1024 * ch + sl0 + 512],
                                     s.SC[0][:], s.SC[1][:], OP.subtract),
                     r=("SC0", "SC1"), w=("WQ",))
        for tt in range(16):
            for grp in range(2):
                bk = f"PS{6 + grp}"
                bank = s.PS[6 + grp]
                dst = XN_ if grp == 0 else U_
                eng = "vector" if grp == 0 else "gpsimd"
                for ci in range(4):
                    c = 4 * grp + ci
                    xk = f"XN{grp}{ci}"
                    xwx = ("WPB",) if grp == 1 and tt == 0 else ()
                    s.do(eng, lambda e, c=c, tt=tt, dst=dst, ci=ci:
                         e.tensor_tensor(dst[:, 128 * ci:128 * (ci + 1)],
                                         s.X[c][:, 128 * tt:128 * (tt + 1)],
                                         s.MU[:, 128 * tt:128 * (tt + 1)],
                                         OP.subtract),
                         r=(f"X{c}q{tt // 4}", f"MU{tt // 4}"), w=(xk,) + xwx)
                    s.do(eng, lambda e, tt=tt, dst=dst, ci=ci:
                         e.tensor_tensor(dst[:, 128 * ci:128 * (ci + 1)],
                                         dst[:, 128 * ci:128 * (ci + 1)],
                                         s.RBC[:, 128 * tt:128 * (tt + 1)],
                                         OP.mult),
                         r=(xk, f"RB{tt // 4}"), w=(xk,))
                    s.do("tensor", lambda e, ci=ci, bank=bank, dst=dst:
                         e.transpose(bank[:, 128 * ci:128 * (ci + 1)],
                                     dst[:, 128 * ci:128 * (ci + 1)], s.I128[:]),
                         r=(xk, "ident"), w=(bk,))
                s.close_group(bk)
            s.do("vector", lambda e, tt=tt, AMX=AMX:
                 e.tensor_reduce(AMX[:, tt:tt + 1], s.PS[6][:], AX.X, OP.max,
                                 apply_absolute_value=True),
                 r=("PS6",), w=("AMXa",))
            s.do("vector", lambda e, tt=tt, AMX=AMX:
                 e.tensor_reduce(AMX[:, 32 + tt:33 + tt], s.PS[7][:], AX.X, OP.max,
                                 apply_absolute_value=True),
                 r=("PS7",), w=("AMXb",))
            s.do("vector", lambda e, tt=tt, AMX=AMX:
                 e.tensor_tensor(AMX[:, tt:tt + 1], AMX[:, tt:tt + 1],
                                 AMX[:, 32 + tt:33 + tt], OP.max),
                 r=("AMXa", "AMXb"), w=("AMXa",))
        s.do("vector", lambda e, AMX=AMX:
             e.tensor_single_scalar(AMX[:, 0:16], AMX[:, 0:16], 1e-8, OP.add),
             r=("AMXa",), w=("AMXa",))
        s.do("vector", lambda e, AMX=AMX:
             e.tensor_scalar_mul(AMX[:, 16:32], AMX[:, 0:16], s.SM[:, 21:22]),
             r=("AMXa", "SM21"), w=("AMXa",))
        s.do("vector", lambda e, AMX=AMX:
             e.reciprocal(AMX[:, 0:16], AMX[:, 0:16]), r=("AMXa",), w=("AMXa",))
        s.do("vector", lambda e, AMX=AMX:
             e.tensor_single_scalar(AMX[:, 0:16], AMX[:, 0:16], 127.0, OP.mult),
             r=("AMXa",), w=("AMXa",))
        s.do("tensor", lambda e, AMX=AMX:
             e.transpose(s.PS[6][0:32, 0:128], AMX[:, 0:32], s.I128[:]),
             r=("AMXa", "ident"), w=("PS6",))
        s.do("vector", lambda e, SB32=SB32:
             e.tensor_copy(SB32[:], s.PS[6][0:32, 0:128]), r=("PS6",), w=("SB32",))
        s.do("sync", lambda e, SB32=SB32, ROWR=ROWR:
             e.dma_start(out=ROWR[0:1, :].rearrange("p (r j) -> p r j", r=32),
                         in_=SB32[:, :]),
             r=("SB32",), w=("ROWR",), dma="ld")
        RABP = [RAB, s.ov("RAB2", [128, QW], F32, 102 * 1024)]

        def rab_bcast(qq):
            rab, rk = RABP[qq % 2], f"RABp{qq % 2}"
            s.do("tensor", lambda e, qq=qq, ROWR=ROWR:
                 e.matmul(s.PS[6][:], s.ONE1[0:1, :],
                          ROWR[0:1, QW * qq:QW * (qq + 1)], start=True, stop=True),
                 r=("ROWR", "one1"), w=("PS6",))
            s.do("vector", lambda e, rab=rab:
                 e.tensor_copy(rab[:], s.PS[6][:]), r=("PS6",), w=(rk,))

        rab_bcast(0)
        for qq in range(NQ):
            rab, rk = RABP[qq % 2], f"RABp{qq % 2}"
            s.do("tensor", lambda e, qq=qq, ROWR=ROWR:
                 e.matmul(s.PS[6][:], s.ONE1[0:1, :],
                          ROWR[0:1, 2048 + QW * qq:2048 + QW * (qq + 1)],
                          start=True, stop=True),
                 r=("ROWR", "one1"), w=("PS6",))
            s.do("vector", lambda e, ASB=ASB:
                 e.tensor_copy(ASB[:], s.PS[6][:]), r=("PS6",), w=("ASB",))
            if qq + 1 < NQ:
                rab_bcast(qq + 1)
            for cp in range(0, NCH, 2):
                wx = ("XN00", "XN01", "XN02", "XN03") if qq == 0 and cp == 0 \
                    else ()
                s.xn_into(XN_[:, 0:QW], "XN", cp, qq, eng="gpsimd", wx=wx)
                s.xn_into(XN_[:, QW:2 * QW], "XN", cp + 1, qq, eng="gpsimd")
                for j in range(2):
                    s.do("gpsimd", lambda e, j=j, XN_=XN_, rab=rab:
                         e.tensor_tensor(XN_[:, QW * j:QW * (j + 1)],
                                         XN_[:, QW * j:QW * (j + 1)],
                                         rab[:], OP.mult),
                         r=("XN", rk), w=("XN",))
                uwx = ("XN10", "XN11", "XN12", "XN13", "WPB") \
                    if qq == 0 and cp == 0 else ()
                s.do("scalar", lambda e, U_=U_, XN_=XN_:
                     e.activation(U_[:], XN_[:], AF.Identity,
                                  bias=s.SM[:, 30:31], scale=1.0),
                     r=("XN", "SM30"), w=("U",) + uwx)
                rwx = ("WPB",) if qq == 0 and cp == 0 else ()
                s.do("scalar", lambda e, R_=R_, U_=U_:
                     e.activation(R_[:], U_[:], AF.Copy),
                     r=("U",), w=("R",) + rwx)
                for k in range(7, -1, -1):
                    val = float(1 << k)
                    s.do("vector", lambda e, val=val, B_=B_, R_=R_:
                         e.tensor_scalar(B_[:], R_[:], val, val, OP.is_ge, OP.mult),
                         r=("R",), w=("B",))
                    s.do("vector", lambda e, R_=R_, B_=B_:
                         e.tensor_tensor(R_[:], R_[:], B_[:], OP.subtract),
                         r=("R", "B"), w=("R",))
                s.do("vector", lambda e, cp=cp, U_=U_, R_=R_, AQW=AQW:
                     e.scalar_tensor_tensor(
                         AQW[:, QW * cp:QW * (cp + 2)],
                         U_[:], -128.0, R_[:], OP.add, OP.subtract),
                     r=("U", "R"), w=(f"AQc{cp}", f"AQc{cp + 1}"))
                for mch in range(NCH):
                    for kc in (cp, cp + 1):
                        s.do("tensor", lambda e, mch=mch, kc=kc,
                             WQ=WQ, AQW=AQW:
                             e.matmul(s.PS[mch][:],
                                      WQ[:, 1024 * kc + 128 * mch:1024 * kc + 128 * (mch + 1)],
                                      AQW[:, QW * kc:QW * (kc + 1)],
                                      start=(kc == 0), stop=(kc == NCH - 1)),
                             r=("WQ", f"AQc{kc}"), w=(f"PS{mch}",))
            for mch in range(NCH):
                s.close_group(f"PS{mch}")
            for mch in range(NCH):
                s.do("vector", lambda e, mch=mch, ASB=ASB:
                     e.tensor_tensor(s.SC[2 + mch % 2][:], s.PS[mch][:], ASB[:],
                                     OP.mult),
                     r=(f"PS{mch}", "ASB"), w=(f"SC{2 + mch % 2}",))
                s.do("gpsimd", lambda e, mch=mch, qq=qq:
                     e.tensor_tensor(s.X[mch][:, QW * qq:QW * (qq + 1)],
                                     s.X[mch][:, QW * qq:QW * (qq + 1)],
                                     s.SC[2 + mch % 2][:], OP.add),
                     r=(f"SC{2 + mch % 2}", f"X{mch}q{qq}"), w=(f"X{mch}q{qq}",))

    def emit_l1a(self):
        s = self
        CKV = [s.ov(f"CKV{i}", [128, T], F32, i * 8192) for i in range(2)]
        UU = [s.ov(f"UU{i}", [128, T], F32, 16 * 1024 + i * 8192) for i in range(2)]
        KEF, VEF = s.KEF, s.VEF
        WA = [s.ov(f"L1W{i}", [128, T], F32, 48 * 1024 + i * 8192) for i in range(4)]
        WUQT = s.ov("WUQT", [128, T], F32, 94 * 1024)

        s.loadw("wdq1T", WA[0], "WA0", cols=256)
        s.loadw("wdkv1T", WA[1], "WA1", cols=256)
        for qq in range(NQ):
            for c in range(NCH):
                s.xn_into(s.SC[0][:], "SC0", c, qq)
                for m in range(2):
                    s.do("tensor", lambda e, c=c, m=m:
                         e.matmul(s.PS[2 + m][:],
                                  WA[0][:, 256 * c + 128 * m:256 * c + 128 * (m + 1)],
                                  s.SC[0][:], start=(c == 0), stop=(c == NCH - 1)),
                         r=("WA0", "SC0"), w=(f"PS{2 + m}",))
                    s.do("tensor", lambda e, c=c, m=m:
                         e.matmul(s.PS[4 + m][:],
                                  WA[1][:, 256 * c + 128 * m:256 * c + 128 * (m + 1)],
                                  s.SC[0][:], start=(c == 0), stop=(c == NCH - 1)),
                         r=("WA1", "SC0"), w=(f"PS{4 + m}",))
            s.close_group("PS2", "PS3", "PS4", "PS5")
            for m in range(2):
                s.do("vector", lambda e, m=m, qq=qq:
                     e.tensor_copy(UU[m][:, QW * qq:QW * (qq + 1)], s.PS[2 + m][:]),
                     r=(f"PS{2 + m}",), w=(f"UU{m}",))
                s.do("vector", lambda e, m=m, qq=qq:
                     e.tensor_copy(CKV[m][:, QW * qq:QW * (qq + 1)], s.PS[4 + m][:]),
                     r=(f"PS{4 + m}",), w=(f"CKV{m}",))
        s.loadw("wuq1T", WUQT, "WUQT", cols=1024)
        for m in range(2):
            s.do("vector", lambda e, m=m, WUQT=WUQT:
                 e.tensor_single_scalar(WUQT[:, 1024 * m:1024 * (m + 1)],
                                        WUQT[:, 1024 * m:1024 * (m + 1)],
                                        0.125, OP.mult),
                 r=("WUQT",), w=("WUQT",))
        QP = s.ov("QP", [128, T], F32, 48 * 1024)
        KHD = s.ov("KHD", [128, T], F32, 56 * 1024)
        VFA = s.ov("VFA1", [128, T], F32, 64 * 1024)
        VFB = s.ov("VFB1", [128, T], F32, 72 * 1024)
        s.do("vector", lambda e, VFA=VFA: e.memset(VFA[:], 1.0), w=("WA2",))
        s.do("vector", lambda e, VFB=VFB: e.memset(VFB[:], 1.0), w=("WA3",))
        for c in range(NCH):
            s.flush_ep()
            if c == 2:
                s.emit_l1f_prologue()
            for qq in range(NQ):
                bk = f"PS{2 + (qq % 4)}"
                bank = s.PS[2 + (qq % 4)]
                for m in range(2):
                    s.do("tensor", lambda e, c=c, qq=qq, m=m, bank=bank, WUQT=WUQT,
                         UU=UU:
                         e.matmul(bank[:],
                                  WUQT[:, 1024 * m + 128 * c:1024 * m + 128 * (c + 1)],
                                  UU[m][:, QW * qq:QW * (qq + 1)],
                                  start=(m == 0), stop=(m == 1)),
                         r=("WUQT", f"UU{m}"), w=(bk, "WA0"))
                s.close_group(bk)
                s.do("vector", lambda e, qq=qq, bank=bank, QP=QP:
                     e.tensor_copy(QP[:, QW * qq:QW * (qq + 1)], bank[:]),
                     r=(bk,), w=("WA0",))
            for nn in range(NQ):
                bk = f"PS{2 + (nn % 4)}"
                bank = s.PS[2 + (nn % 4)]
                for m in range(2):
                    s.do("tensor", lambda e, c=c, nn=nn, m=m, bank=bank, KEF=KEF,
                         CKV=CKV:
                         e.matmul(bank[:], KEF[m][:, 128 * c:128 * (c + 1)],
                                  CKV[m][:, QW * nn:QW * (nn + 1)],
                                  start=(m == 0), stop=(m == 1)),
                         r=(f"KEF{m}", f"CKV{m}"), w=(bk, "WA1"))
                s.close_group(bk)
                s.do("vector", lambda e, nn=nn, bank=bank, KHD=KHD:
                     e.tensor_copy(KHD[:, QW * nn:QW * (nn + 1)], bank[:]),
                     r=(bk,), w=("WA1",))
            for kc in range(NKC):
                bk = f"PS{2 + (kc % 4)}"
                bank = s.PS[2 + (kc % 4)]
                for m in range(2):
                    s.do("tensor", lambda e, kc=kc, m=m, bank=bank, CKV=CKV,
                         VEF=VEF, c=c:
                         e.matmul(bank[:, 0:128],
                                  CKV[m][:, 128 * kc:128 * (kc + 1)],
                                  VEF[m][:, 128 * c:128 * c + 128],
                                  start=(m == 0), stop=(m == 1)),
                         r=(f"CKV{m}", f"VEF{m}"), w=(bk,))
                s.close_group(bk)
                s.do("vector", lambda e, kc=kc, bank=bank, VFA=VFA:
                     e.tensor_copy(VFA[:, 128 * kc:128 * kc + 64],
                                   bank[:, 0:64]),
                     r=(bk,), w=("WA2",))
                s.do("vector", lambda e, kc=kc, bank=bank, VFB=VFB:
                     e.tensor_copy(VFB[:, 128 * kc + 64:128 * kc + 128],
                                   bank[:, 64:128]),
                     r=(bk,), w=("WA3",))
            for half, (VF, vk_) in enumerate(((VFA, "WA2"), (VFB, "WA3"))):
                h = 2 * c + half
                hs0 = 64 * half
                s.head_bound(QP, "WA0", KHD, "WA1", 20 + h)
                negc = s.SM[:, 20 + h:21 + h]
                xs = s.X[c]
                s.attn_head(
                    k_lhsT=lambda kc, KHD=KHD, hs0=hs0:
                        KHD[hs0:hs0 + 64, 128 * kc:128 * (kc + 1)],
                    kkey="WA1",
                    q_rhs=lambda qq, QP=QP, hs0=hs0:
                        QP[hs0:hs0 + 64, QW * qq:QW * (qq + 1)],
                    qkey="WA0",
                    v_lhsT=lambda kc, VF=VF: VF[:, 128 * kc:128 * (kc + 1)],
                    vkey=vk_,
                    negc=negc, nkey=f"SM{20 + h}",
                    xout=lambda qq, xs=xs, hs0=hs0:
                        xs[hs0:hs0 + 64, QW * qq:QW * (qq + 1)],
                    xkey=lambda qq, c=c: f"X{c}q{qq}",
                    half=half)
        s.flush_ep()
        for qq in range(NQ):
            s.emit_ln_win(qq)
        W1F = [s.ov(f"W1F{i}", [128, 8 * FCW], F32, 80 * 1024 + i * 4096)
               for i in range(2)]
        W1H0 = s.ov("W1H0", [128, 8 * FCW], BF16, 88 * 1024)
        W1L0 = s.ov("W1L0", [128, 8 * FCW], BF16, 90 * 1024)
        for fc in range(2):
            s.do("sync", lambda e, fc=fc, W1F=W1F:
                 e.dma_start(out=W1F[fc][:],
                             in_=s.dr["w1p"][128 * fc:128 * (fc + 1), :]),
                 w=(f"W1F{fc}",), dma="ld")
        s.do("vector", lambda e, W1F=W1F, W1H0=W1H0:
             e.tensor_copy(W1H0[:], W1F[0][:]), r=("W1F0",), w=("W1H0",))
        s.do("vector", lambda e, W1F=W1F, W1H0=W1H0, W1L0=W1L0:
             e.tensor_tensor(W1L0[:], W1F[0][:], W1H0[:], OP.subtract),
             r=("W1F0", "W1H0"), w=("W1L0", "mask"))

    def emit_l1f_prologue(self):
        s = self
        VEC = s.VEC = s.sb("VECL", [128, 192], F32)
        TR32 = s.sb("TR32", [32, 128], F32)

        def col_vec(name, dst_c0, ncols, key):
            nch = ncols // 128
            s.do("sync", lambda e, name=name, nch=nch:
                 e.dma_start(out=TR32[0:nch, :], in_=s.dr[name][0:nch, :]),
                 w=("TR32",), dma="ld")
            s.do("tensor", lambda e, nch=nch:
                 e.transpose(s.PS[7][0:128, 0:nch], TR32[0:nch, :],
                             s.I128[0:nch, 0:nch]),
                 r=("TR32", "ident"), w=("PS7",))
            s.do("vector", lambda e, dst_c0=dst_c0, nch=nch:
                 e.tensor_copy(VEC[:, dst_c0:dst_c0 + nch], s.PS[7][0:128, 0:nch]),
                 r=("PS7",), w=(key,))

        col_vec("b1", 64, DFF, "B1C")
        col_vec("b2", 96, D, "B2C")
        col_vec("qwr", 104, DFF, "QWR")
        col_vec("th1", 0, DFF, "THR")
        col_vec("qwi", 32, DFF, "QWI")
        SCR = s.SC[1]

        def rred(c0, key):
            s.do("vector", lambda e, c0=c0:
                 e.tensor_scalar(SCR[:, 0:32], VEC[:, c0:c0 + 32],
                                 1.0 / (2 * math.pi), 8.5, OP.mult, OP.add),
                 r=(key,), w=("RRS",))
            s.do("vector", lambda e:
                 e.tensor_copy(SCR[:, 32:64], SCR[:, 0:32]),
                 r=("RRS",), w=("RRS",))
            for k in (8.0, 4.0, 2.0, 1.0):
                s.do("vector", lambda e, k=k:
                     e.tensor_scalar(SCR[:, 64:96], SCR[:, 32:64], k, k,
                                     OP.is_ge, OP.mult),
                     r=("RRS",), w=("RRS",))
                s.do("vector", lambda e:
                     e.tensor_tensor(SCR[:, 32:64], SCR[:, 32:64],
                                     SCR[:, 64:96], OP.subtract),
                     r=("RRS",), w=("RRS",))
            s.do("vector", lambda e:
                 e.scalar_tensor_tensor(SCR[:, 64:96], SCR[:, 0:32], -8.0,
                                        SCR[:, 32:64], OP.add, OP.subtract),
                 r=("RRS",), w=("RRS",))
            s.do("vector", lambda e, c0=c0:
                 e.scalar_tensor_tensor(VEC[:, c0:c0 + 32], SCR[:, 64:96],
                                        -TPI_HI, VEC[:, c0:c0 + 32],
                                        OP.mult, OP.add),
                 r=("RRS", key), w=(key,))
            s.do("vector", lambda e, c0=c0:
                 e.scalar_tensor_tensor(VEC[:, c0:c0 + 32], SCR[:, 64:96],
                                        -TPI_LO, VEC[:, c0:c0 + 32],
                                        OP.mult, OP.add),
                 r=("RRS", key), w=(key,))

        s.do("vector", lambda e:
             e.tensor_copy(SCR[:, 96:128], VEC[:, 0:32]), r=("THR",), w=("TSIN",))
        s.do("vector", lambda e:
             e.tensor_copy(VEC[:, 136:168], SCR[:, 96:128]), r=("TSIN",), w=("TS2",))
        rred(136, "TS2")
        s.do("scalar", lambda e:
             e.activation(s.SC[2][:, 0:32], VEC[:, 136:168], AF.Sin),
             r=("TS2",), w=("SC2",))
        s.do("vector", lambda e:
             e.tensor_scalar(VEC[:, 136:168], SCR[:, 96:128], 1.0, PI_F / 2.0,
                             OP.mult, OP.add),
             r=("TSIN",), w=("TS2",))
        rred(136, "TS2")
        s.do("scalar", lambda e:
             e.activation(s.SC[3][:, 0:32], VEC[:, 136:168], AF.Sin),
             r=("TS2",), w=("SC3",))
        s.do("vector", lambda e:
             e.tensor_tensor(s.SC[2][:, 32:64], s.SC[2][:, 0:32], VEC[:, 32:64],
                             OP.mult),
             r=("SC2", "QWI"), w=("SC2",))
        s.do("vector", lambda e:
             e.tensor_tensor(s.SC[3][:, 32:64], s.SC[3][:, 0:32], VEC[:, 104:136],
                             OP.mult),
             r=("SC3", "QWR"), w=("SC3",))
        s.do("vector", lambda e:
             e.tensor_tensor(s.SC[2][:, 32:64], s.SC[2][:, 32:64],
                             s.SC[3][:, 32:64], OP.add),
             r=("SC2", "SC3"), w=("SC2",))
        s.do("vector", lambda e:
             e.scalar_tensor_tensor(VEC[:, 32:64], s.SC[2][:, 32:64], 0.1,
                                    VEC[:, 32:64], OP.mult, OP.mult),
             r=("SC2", "QWI"), w=("PP",))
        s.do("vector", lambda e:
             e.scalar_tensor_tensor(VEC[:, 0:32], VEC[:, 64:96], 0.1,
                                    VEC[:, 0:32], OP.mult, OP.add),
             r=("B1C", "THR"), w=("THR",))
        rred(0, "THR")

    def emit_l1f(self):
        s = self
        XNH = s.ov("XNH", [128, 8 * QW], BF16, 0)
        XNL = s.ov("XNL", [128, 8 * QW], BF16, 8 * 1024)
        ACH = s.ov("ACH", [128, 32 * QW], BF16, 16 * 1024)
        ACL = s.ov("ACL", [128, 32 * QW], BF16, 48 * 1024)
        W1F = [s.ov(f"W1F{i}", [128, 8 * FCW], F32, 80 * 1024 + i * 4096)
               for i in range(2)]
        W1H = [s.ov(f"W1H{i}", [128, 8 * FCW], BF16, 88 * 1024 + i * 4096)
               for i in range(2)]
        W1L = [s.ov(f"W1L{i}", [128, 8 * FCW], BF16, 90 * 1024 + i * 4096)
               for i in range(2)]
        W2F = [s.ov(f"W2F{i}", [128, 1024], F32, 96 * 1024 + i * 4096)
               for i in range(2)]
        W2H, W2L = W1H, W1L
        VEC = s.VEC

        def xn_chunk(qq, c):
            s.xn_into(s.SC[0][:], "SC0", c, qq)
            s.do("vector", lambda e, c=c, XNH=XNH:
                 e.tensor_copy(XNH[:, QW * c:QW * (c + 1)], s.SC[0][:]),
                 r=("SC0",), w=("XNH",))
            s.do("vector", lambda e, c=c, XNH=XNH, XNL=XNL:
                 e.tensor_tensor(XNL[:, QW * c:QW * (c + 1)], s.SC[0][:],
                                 XNH[:, QW * c:QW * (c + 1)], OP.subtract),
                 r=("SC0", "XNH"), w=("XNL",))

        def w1_load(fc):
            w1f, w1k = W1F[fc % 2], f"W1F{fc % 2}"
            s.do("sync", lambda e, fc=fc, w1f=w1f:
                 e.dma_start(out=w1f[:],
                             in_=s.dr["w1p"][128 * fc:128 * (fc + 1), :]),
                 w=(w1k,), dma="ld")

        def w1_split(fc):
            w1f, w1k = W1F[fc % 2], f"W1F{fc % 2}"
            w1h, w1l = W1H[fc % 2], W1L[fc % 2]
            hk, lk2 = f"W1H{fc % 2}", f"W1L{fc % 2}"
            s.do("scalar", lambda e, w1f=w1f, w1h=w1h:
                 e.activation(w1h[:], w1f[:], AF.Copy), r=(w1k,), w=(hk,))
            s.do("vector", lambda e, w1f=w1f, w1h=w1h, w1l=w1l:
                 e.tensor_tensor(w1l[:], w1f[:], w1h[:], OP.subtract),
                 r=(w1k, hk), w=(lk2,))

        def w2_load(kc):
            w2f, w2k = W2F[kc % 2], f"W2F{kc % 2}"
            s.do("sync", lambda e, kc=kc, w2f=w2f:
                 e.dma_start(out=w2f[:],
                             in_=s.dr["w2"][128 * kc:128 * (kc + 1), :]),
                 w=(w2k,), dma="ld")

        def w2_split(kc):
            w2f, w2k = W2F[kc % 2], f"W2F{kc % 2}"
            w2h, w2l = W2H[kc % 2], W2L[kc % 2]
            hk, lk2 = f"W1H{kc % 2}", f"W1L{kc % 2}"
            s.do("scalar", lambda e, w2f=w2f, w2h=w2h:
                 e.activation(w2h[:, 0:1024], w2f[:], AF.Copy),
                 r=(w2k,), w=(hk,))
            s.do("vector", lambda e, w2f=w2f, w2h=w2h, w2l=w2l:
                 e.tensor_tensor(w2l[:, 0:1024], w2f[:], w2h[:, 0:1024],
                                 OP.subtract),
                 r=(w2k, hk), w=(lk2,))

        def fc_matmuls(fc):
            w1h, w1l = W1H[fc % 2], W1L[fc % 2]
            hk, lk2 = f"W1H{fc % 2}", f"W1L{fc % 2}"
            bk = f"PS{fc % 4}"
            bank = s.PS[fc % 4]
            for kc in range(8):
                for pi, (lh, lk, rh, rk) in enumerate(
                        ((w1h, hk, XNH, "XNH"),
                         (w1h, hk, XNL, "XNL"),
                         (w1l, lk2, XNH, "XNH"))):
                    s.do("tensor", lambda e, kc=kc, lh=lh, rh=rh, bank=bank,
                         pi=pi:
                         e.matmul(bank[:], lh[:, FCW * kc:FCW * (kc + 1)],
                                  rh[:, QW * kc:QW * (kc + 1)],
                                  start=(kc == 0 and pi == 0),
                                  stop=(kc == 7 and pi == 2)),
                         r=(lk, rk), w=(bk,))
            s.close_group(bk)

        def quantum(fc, qq):
            bk = f"PS{fc % 4}"
            bank = s.PS[fc % 4]
            colv = fc
            s.do("scalar", lambda e, bank=bank, colv=colv:
                 e.activation(s.SC[0][:], bank[:], AF.Identity,
                              bias=VEC[:, colv:colv + 1], scale=0.1),
                 r=(bk, "THR"), w=("SC0",))
            s.do("scalar", lambda e, bank=bank, colv=colv:
                 e.activation(s.SC[1][:], bank[:], AF.Identity,
                              bias=VEC[:, 64 + colv:65 + colv], scale=1.0),
                 r=(bk, "B1C"), w=("SC1",))
            s.do("vector", lambda e:
                 e.tensor_scalar(s.SC[2][:], s.SC[0][:], PI_F, TPI_HI,
                                 OP.is_ge, OP.mult),
                 r=("SC0",), w=("SC2",))
            s.do("vector", lambda e:
                 e.tensor_scalar(s.SC[3][:], s.SC[0][:], -PI_F, TPI_HI,
                                 OP.is_le, OP.mult),
                 r=("SC0",), w=("SC3",))
            s.do("vector", lambda e:
                 e.tensor_tensor(s.SC[0][:], s.SC[0][:], s.SC[2][:],
                                 OP.subtract),
                 r=("SC0", "SC2"), w=("SC0",))
            s.do("gpsimd", lambda e:
                 e.tensor_tensor(s.SC[0][:], s.SC[0][:], s.SC[3][:], OP.add),
                 r=("SC0", "SC3"), w=("SC0",))
            s.do("scalar", lambda e:
                 e.activation(s.SC[2][:], s.SC[0][:], AF.Sin),
                 r=("SC0",), w=("SC2",))
            s.do("vector", lambda e, colv=colv:
                 e.scalar_tensor_tensor(s.SC[0][:], s.SC[2][:],
                                        VEC[:, 32 + colv:33 + colv],
                                        s.SC[1][:], OP.mult, OP.add),
                 r=("SC2", "PP", "SC1"), w=("SC0",))
            s.do("scalar", lambda e, fc=fc, ACH=ACH:
                 e.activation(ACH[:, QW * fc:QW * (fc + 1)], s.SC[0][:],
                              AF.Relu),
                 r=("SC0",), w=("ACH",))
            s.do("scalar", lambda e:
                 e.activation(s.SC[3][:], s.SC[0][:], AF.Relu),
                 r=("SC0",), w=("SC3",))
            s.do("gpsimd", lambda e, fc=fc, ACH=ACH, ACL=ACL:
                 e.tensor_tensor(ACL[:, QW * fc:QW * (fc + 1)], s.SC[3][:],
                                 ACH[:, QW * fc:QW * (fc + 1)], OP.subtract),
                 r=("SC3", "ACH"), w=("ACL",))

        def w2_matmuls(kc):
            w2h, w2l = W2H[kc % 2], W2L[kc % 2]
            hk, lk2 = f"W1H{kc % 2}", f"W1L{kc % 2}"
            for mch in range(NCH):
                for pi, (lh, lk, rh, rk) in enumerate(
                        ((w2h, hk, ACH, "ACH"),
                         (w2h, hk, ACL, "ACL"),
                         (w2l, lk2, ACH, "ACH"))):
                    s.do("tensor", lambda e, kc=kc, mch=mch, lh=lh, rh=rh,
                         pi=pi:
                         e.matmul(s.PS[mch][:],
                                  lh[:, 128 * mch:128 * (mch + 1)],
                                  rh[:, QW * kc:QW * (kc + 1)],
                                  start=(kc == 0 and pi == 0),
                                  stop=(kc == 31 and pi == 2)),
                         r=(lk, rk), w=(f"PS{mch}",))

        for qq in range(NQ):
            if qq == 0:
                for c in range(NCH):
                    xn_chunk(0, c)
            for fc in range(NFC):
                if fc + 1 < NFC:
                    w1_split(fc + 1)
                if fc + 2 < NFC:
                    w1_load(fc + 2)
                fc_matmuls(fc)
                if fc == NFC - 3:
                    w2_load(0)
                if fc == NFC - 2:
                    w2_split(0)
                    w2_load(1)
                quantum(fc, qq)
            for kc in range(32):
                if kc + 1 < 32:
                    w2_split(kc + 1)
                if kc + 2 < 32:
                    w2_load(kc + 2)
                w2_matmuls(kc)
                if kc < NCH and qq + 1 < NQ:
                    xn_chunk(qq + 1, kc)
                if qq + 1 < NQ:
                    if kc == 29:
                        w1_load(0)
                    if kc == 30:
                        w1_split(0)
                        w1_load(1)
            for mch in range(NCH):
                s.close_group(f"PS{mch}")
            for mch in range(NCH):
                s.do("vector", lambda e, mch=mch, qq=qq:
                     e.scalar_tensor_tensor(s.X[mch][:, QW * qq:QW * (qq + 1)],
                                            s.PS[mch][:], 0.0,
                                            s.X[mch][:, QW * qq:QW * (qq + 1)],
                                            OP.add, OP.add),
                     r=(f"PS{mch}", f"X{mch}q{qq}"), w=(f"X{mch}q{qq}",))
        for mch in range(NCH):
            for qq in range(NQ):
                s.do("vector", lambda e, mch=mch, qq=qq:
                     e.tensor_scalar_add(s.X[mch][:, QW * qq:QW * (qq + 1)],
                                         s.X[mch][:, QW * qq:QW * (qq + 1)],
                                         VEC[:, 96 + mch:97 + mch]),
                     r=(f"X{mch}q{qq}", "B2C"), w=(f"X{mch}q{qq}",))

    def fin(self):
        s = self
        self.emit_exit()
        nc = self.nc
        with nc.Block() as block:
            @block.tensor
            def _(e):
                for f in s.th["tensor"]:
                    f(e)

            @block.vector
            def _(e):
                for f in s.th["vector"]:
                    f(e)

            @block.scalar
            def _(e):
                for f in s.th["scalar"]:
                    f(e)

            @block.gpsimd
            def _(e):
                for f in s.th["gpsimd"]:
                    f(e)

            @block.sync
            def _(e):
                for f in s.th["sync"]:
                    f(e)
        return nc

    def emit_exit(self):
        s = self
        OT = [s.nc.alloc_sbuf_tensor_at(f"XO{i}", [128, 1024], F32,
                                        offset=self.ovl_base + 40 * 1024 + i * 4096)
              for i in range(2)]
        for tt in range(16):
            ot = OT[tt % 2]
            ok = f"OT{tt % 2}"
            for c in range(NCH):
                bk = f"PS{c}"
                bank = s.PS[c]
                s.do("tensor", lambda e, tt=tt, c=c, bank=bank:
                     e.transpose(bank[:, 0:128],
                                 s.X[c][:, 128 * tt:128 * (tt + 1)], s.I128[:]),
                     r=(f"X{c}q{tt // 4}", "ident"), w=(bk,))
                s.do("vector", lambda e, c=c, ot=ot, bank=bank:
                     e.tensor_copy(ot[:, 128 * c:128 * (c + 1)], bank[:, 0:128]),
                     r=(bk,), w=(ok,))
            s.do("sync", lambda e, tt=tt, ot=ot:
                 e.dma_start(out=s.y_out[128 * tt:128 * (tt + 1), :], in_=ot[:]),
                 r=(ok,), dma="st")
        s.do("sync", lambda e: e.nop(),
             xw=[(sn, s.cnt[sn]) for sn in s.STS])


def _host_inputs(inputs, core):
    f32 = np.float32
    x = np.ascontiguousarray(np.asarray(inputs["L0_attn_x"])[core]).astype(
        f32, copy=False)
    m = {"x_in": x}
    t_ = lambda a: np.ascontiguousarray(np.asarray(a, dtype=f32).T)
    c_ = lambda a: np.ascontiguousarray(np.asarray(a, dtype=f32))
    m["wdq"] = c_(inputs["L0_attn_W_DQ"])
    m["wuqT"] = t_(inputs["L0_attn_W_UQ"])
    m["wdkv"] = c_(inputs["L0_attn_W_DKV"])
    m["wukvT"] = t_(inputs["L0_attn_W_UKV"])
    m["wqrT"] = t_(inputs["L0_attn_W_QR"])
    m["wkrT"] = t_(inputs["L0_attn_W_KR"])
    m["woT"] = t_(inputs["L0_attn_W_O"])
    m["qnw"] = c_(inputs["L0_attn_q_norm_w"]).reshape(1, HD)
    m["knw"] = c_(inputs["L0_attn_k_norm_w"]).reshape(1, HD)
    m["w0T"] = t_(inputs["L0_ffn_weight"])
    m["wdq1T"] = t_(inputs["L1_attn_W_dq"])
    m["wdq1"] = c_(inputs["L1_attn_W_dq"])
    m["wuq1"] = c_(inputs["L1_attn_W_uq"])
    m["wuq1T"] = t_(inputs["L1_attn_W_uq"])
    m["wdkv1T"] = t_(inputs["L1_attn_W_dkv"])
    m["wuk1"] = c_(inputs["L1_attn_W_uk"])
    m["wuv1"] = c_(inputs["L1_attn_W_uv"])
    m["wo1T"] = t_(inputs["L1_attn_W_o"])
    w1 = c_(inputs["L1_ffn_W1"])
    m["w1p"] = np.ascontiguousarray(
        w1.reshape(8, 128, 32, 128).transpose(2, 1, 0, 3).reshape(DFF, D))
    m["w2"] = c_(inputs["L1_ffn_W2"])
    m["b1"] = c_(inputs["L1_ffn_b1"]).reshape(32, 128)
    m["th1"] = c_(inputs["L1_ffn_quantum_phases"]).reshape(32, 128)
    m["qwr"] = c_(inputs["L1_ffn_quantum_weights_real"]).reshape(32, 128)
    m["qwi"] = c_(inputs["L1_ffn_quantum_weights_imag"]).reshape(32, 128)
    m["b2"] = c_(inputs["L1_ffn_b2"]).reshape(8, 128)
    import ml_dtypes
    m["ident"] = np.eye(128, dtype=f32)
    m["identb"] = np.eye(128, dtype=ml_dtypes.bfloat16)
    msk = np.zeros((128, 4 * QW), dtype=f32)
    i = np.arange(128)[:, None]
    jj = np.arange(QW)[None, :]
    for j in range(4):
        msk[:, QW * j:QW * (j + 1)] = np.where((i + 128 * j) <= jj, 0.0,
                                               -30000.0).astype(f32)
    m["mask"] = msk.astype(ml_dtypes.bfloat16)
    m["all1"] = np.ones((128, 128), dtype=f32)
    m["one1"] = np.ones((1, 128), dtype=f32)
    o2b = np.zeros((128, 2), dtype=f32)
    o2b[0:64, 0] = 1.0
    o2b[64:128, 1] = 1.0
    m["ones2b"] = o2b
    ob2 = np.zeros((2, 128), dtype=f32)
    ob2[0, 0:64] = 1.0
    ob2[1, 64:128] = 1.0
    m["onesb2"] = ob2
    return m


_NC_CACHE = {}


def _get_nc(stop_after):
    if stop_after not in _NC_CACHE:
        _NC_CACHE[stop_after] = KB(stop_after).build()
    return _NC_CACHE[stop_after]


def kernel(**inputs):
    nc = _get_nc("l1f")
    in_maps = [_host_inputs(inputs, c) for c in range(8)]
    res = run_bass_kernel_spmd(nc, in_maps, core_ids=list(range(8)))
    return np.stack([r["y_out"] for r in res.results]).astype(np.float32)
